# version 02
# baseline (speedup 1.0000x reference)
import contextlib
import numpy as np
import concourse.bass as bass
import concourse.mybir as mybir
from concourse.bass_utils import run_bass_kernel_spmd

F32 = mybir.dt.float32
BF16 = mybir.dt.bfloat16
AF = mybir.ActivationFunctionType
ALU = mybir.AluOpType
AX = mybir.AxisListType

B = 2
S = 4096
D = 2048
NH = 16
TOK = 1024
E = 8
IM = 1408
ISH = 5632
DCH = D // 128
NQT = TOK // 128
ISQ = INV_SQRT_HD = 1.0 / float(np.sqrt(128.0))
LN_EPS = 1e-5

_OVR = {}


class Ctr:
    def __init__(self, nc, name):
        self.h = nc.alloc_semaphore(name=name)
        self.n = 0

    def inc(self, ins, k=1):
        ins.then_inc(self.h, k)
        self.n += k
        return self.n


class Env:
    pass


def build(dbg=(), phases=("p1", "p2", "p3", "p4", "p5")):
    nc = bass.Bass("TRN2", target_bir_lowering=False)
    env = Env()
    env.nc = nc
    t = env.t = {}
    for name, shape in (
        ("xT", [D, S]), ("xTo", [D, TOK]), ("qoff", [128, 1]),
        ("wqT", [D, D]), ("wkT", [D, D]), ("wvT", [D, D]), ("wpT", [D, D]),
        ("rwT", [D, E]), ("guT", [E, D, 2 * IM]), ("dpT", [E, IM, D]),
        ("sgT", [D, ISH]), ("suT", [D, ISH]), ("sdT", [ISH, D]), ("sgwT", [128, DCH]),
    ):
        t[name] = nc.dram_tensor(name, shape, F32, kind="ExternalInput")
    t["y"] = nc.dram_tensor("y", [D, TOK], F32, kind="ExternalOutput")
    t["ksc"] = nc.dram_tensor("ksc", [D, S], F32, kind="Internal")
    t["vsc"] = nc.dram_tensor("vsc", [S, D], F32, kind="Internal")
    t["qsc"] = nc.dram_tensor("qsc", [D, TOK], F32, kind="Internal")
    t["osc"] = nc.dram_tensor("osc", [TOK, D], F32, kind="Internal")
    t["csc"] = nc.dram_tensor("csc", [E, TOK], F32, kind="Internal")

    env.dbg = {}
    for entry in dbg:
        name, shape = entry[0], entry[1]
        dt_ = entry[2] if len(entry) > 2 else F32
        env.dbg[name] = nc.dram_tensor("dbg_" + name, shape, dt_, kind="ExternalOutput")

    env.ident = nc.alloc_sbuf_tensor("ident", [128, 128], F32)
    env.onescol = nc.alloc_sbuf_tensor("onescol", [128, 1], F32)
    env.onesrow = nc.alloc_sbuf_tensor("onesrow", [1, 128], F32)
    env.p02 = contextlib.ExitStack()
    env.kpos = env.p02.enter_context(nc.sbuf_tensor("kpos", [128, S], F32))
    env.qpos = env.p02.enter_context(nc.sbuf_tensor("qpos", [128, NQT], F32))
    env.qoff_sb = env.p02.enter_context(nc.sbuf_tensor("qoff_sb", [128, 1], F32))

    env.s = {k: Ctr(nc, k) for k in
             ("lw", "lx", "pe", "pe2", "ev", "ev2", "sc", "sp", "st", "a1", "a2", "a3")}

    with nc.named_scope("p0"):
        emit_p0(env)
    if "p1" in phases:
        with nc.named_scope("p1"):
            emit_p1(env)
    if "p2" in phases:
        with nc.named_scope("p2"):
            emit_p2(env)
    env.p02.close()
    if "p3" in phases or "p4" in phases or "p5" in phases:
        env.h1h = nc.alloc_sbuf_tensor("h1h", [128, DCH * TOK], BF16)
        env.h1l = nc.alloc_sbuf_tensor("h1l", [128, DCH * TOK], BF16)
    if "p3" in phases:
        with nc.sbuf_tensor("r1buf", [128, DCH * TOK], F32) as r1buf:
            env.r1buf = r1buf
            with nc.named_scope("p3"):
                emit_p3(env)
    if "p4" in phases:
        env.sig = nc.alloc_sbuf_tensor("sig", [1, TOK], F32)
        with nc.named_scope("p4"):
            emit_p4(env)
    if "p5" in phases:
        with nc.named_scope("p5"):
            emit_p5(env)

    with nc.Block() as blk:
        @blk.sync
        def _(sync):
            sync.wait_ge(env.s["st"].h, env.s["st"].n)
            sync.wait_ge(env.s["sp"].h, env.s["sp"].n)
    return nc


def emit_p0(env):
    nc, t = env.nc, env.t
    lx = env.s["lx"]
    a1 = env.s["a1"]
    with nc.Block() as blk:
        @blk.sync
        def _(sync):
            lx.inc(sync.dma_start(env.qoff_sb[:], t["qoff"].ap()), 16)

        @blk.gpsimd
        def _(g):
            g.memset(env.onescol[:], 1.0)
            g.memset(env.onesrow[:], 1.0)
            g.iota(env.kpos[:], [[1, S]], base=0, channel_multiplier=0,
                   allow_small_or_imprecise_dtypes=True)
            a1.inc(g.iota(env.qpos[:], [[128, NQT]], base=0, channel_multiplier=1,
                          allow_small_or_imprecise_dtypes=True), 1)
            g.memset(env.ident[:], 1.0)
            g.affine_select(env.ident[:], env.ident[:], [[1, 128]], ALU.is_equal, 0.0,
                            base=0, channel_multiplier=-1)

        @blk.vector
        def _(v):
            v.wait_ge(lx.h, lx.n)
            v.wait_ge(a1.h, a1.n)
            v.tensor_scalar_add(env.qpos[:], env.qpos[:], env.qoff_sb[:, 0:1])


def nc_ap(handle):
    return handle[:]


def emit_p1(env):
    nc, t, s = env.nc, env.t, env.s
    NTT = S // 512
    jobs = []
    for tt in range(NTT):
        jobs.append(("k", tt, "xT", t["wkT"], t["ksc"]))
        jobs.append(("v", tt, "xT", t["wvT"], t["vsc"]))
    for tq in range(TOK // 512):
        jobs.append(("q", tq, "xTo", t["wqT"], t["qsc"]))

    lw, lx, pe, pe2, ev, ev2, sp = (s[k] for k in ("lw", "lx", "pe", "pe2", "ev", "ev2", "sp"))
    pe_b, pe2_b, ev_b, ev2_b, lw_b, lx_b, sp_b = pe.n, pe2.n, ev.n, ev2.n, lw.n, lx.n, sp.n

    xloads = []
    for j in jobs:
        key = (j[2], j[1])
        if key not in xloads:
            xloads.append(key)

    pe_after = lambda j, f: pe_b + 16 * j + f + 1
    ev_after = lambda j, f: ev_b + 16 * j + f + 1
    lw_after = lambda j, f: lw_b + 16 * (16 * j + f + 1)
    lx_after = lambda i: lx_b + 16 * (i + 1)
    pe2_after = lambda vj, c: pe2_b + 64 * vj + c + 1
    ev2_after = lambda vj, c: ev2_b + 64 * vj + c + 1
    spill_counts = []
    c0 = sp_b
    for job in jobs:
        c0 += 16 * (4 if job[0] == "v" else 1)
        spill_counts.append(c0)

    with (
        nc.sbuf_tensor("p1_x0", [128, DCH * 512], F32) as x0,
        nc.sbuf_tensor("p1_x1", [128, DCH * 512], F32) as x1,
        nc.sbuf_tensor("p1_w0", [128, DCH * 128], F32) as w0,
        nc.sbuf_tensor("p1_w1", [128, DCH * 128], F32) as w1,
        nc.sbuf_tensor("p1_kst", [128, DCH * 512], F32) as kst,
        nc.sbuf_tensor("p1_vst", [128, DCH * 512], F32) as vst,
        nc.sbuf_tensor("p1_vtst", [128, 4 * D], F32) as vtst,
        nc.psum_tensor("p1_ps", [128, 1024], F32) as ps,
        nc.psum_tensor("p1_pt", [128, 1024], F32) as pt,
        nc.Block() as blk,
    ):
        xbufs, wbufs = (x0, x1), (w0, w1)

        @blk.sync
        def _(sync):
            li = 0
            for j, job in enumerate(jobs):
                kind, ts, xsrc, wT, dst = job
                idx = xloads.index((xsrc, ts))
                if idx == li:
                    if idx >= 2:
                        last_j = max(jj for jj, jb in enumerate(jobs)
                                     if (jb[2], jb[1]) == xloads[idx - 2])
                        sync.wait_ge(pe.h, pe_after(last_j, 15))
                    src = t[xsrc].ap()[:, ts * 512:(ts + 1) * 512] \
                        .rearrange("(c p) n -> p c n", p=128)
                    lx.inc(sync.dma_start(
                        xbufs[idx % 2][:].rearrange("p (c n) -> p c n", c=DCH), src), 16)
                    li += 1
                for f in range(16):
                    gi = 16 * j + f
                    if gi >= 2:
                        pj, pf = divmod(gi - 2, 16)
                        sync.wait_ge(pe.h, pe_after(pj, pf))
                    src = wT.ap()[:, f * 128:(f + 1) * 128] \
                        .rearrange("(c p) n -> p c n", p=128)
                    lw.inc(sync.dma_start(
                        wbufs[gi % 2][:].rearrange("p (c n) -> p c n", c=DCH), src), 16)

        @blk.tensor
        def _(tensor):
            vj = 0
            for j, job in enumerate(jobs):
                kind, ts, xsrc, wT, dst = job
                xi = xloads.index((xsrc, ts))
                xb = xbufs[xi % 2]
                for f in range(16):
                    gi = 16 * j + f
                    tensor.wait_ge(lw.h, lw_after(j, f))
                    if f == 0:
                        tensor.wait_ge(lx.h, lx_after(xi))
                    if gi >= 2:
                        pj, pf = divmod(gi - 2, 16)
                        tensor.wait_ge(ev.h, ev_after(pj, pf))
                    wb = wbufs[gi % 2]
                    pss = ps[:, (gi % 2) * 512:(gi % 2) * 512 + 512]
                    for d in range(DCH):
                        ins = tensor.matmul(pss, wb[:, d * 128:(d + 1) * 128],
                                            xb[:, d * 512:(d + 1) * 512],
                                            start=(d == 0), stop=(d == DCH - 1))
                    pe.inc(ins, 1)
                if kind == "v":
                    for c in range(64):
                        sblk, fc = divmod(c, 16)
                        tensor.wait_ge(ev.h, ev_after(j, fc))
                        if c >= 2:
                            tensor.wait_ge(ev2.h, ev2_after(vj, c - 2))
                        ins = tensor.transpose(
                            pt[:, (c % 2) * 512:(c % 2) * 512 + 128],
                            vst[:, fc * 512 + sblk * 128: fc * 512 + sblk * 128 + 128],
                            env.ident[:])
                        pe2.inc(ins, 1)
                    vj += 1

        @blk.vector
        def _(vector):
            vj = 0
            for j, job in enumerate(jobs):
                kind = job[0]
                stg = vst if kind == "v" else kst
                prev = None
                for jj in range(j - 1, -1, -1):
                    if (jobs[jj][0] == "v") == (kind == "v"):
                        prev = jj
                        break
                for f in range(16):
                    gi = 16 * j + f
                    vector.wait_ge(pe.h, pe_after(j, f))
                    if f == 0 and prev is not None:
                        vector.wait_ge(sp.h, spill_counts[prev])
                    pss = ps[:, (gi % 2) * 512:(gi % 2) * 512 + 512]
                    ev.inc(vector.tensor_copy(stg[:, f * 512:(f + 1) * 512], pss), 1)
                if kind == "v":
                    for c in range(64):
                        sblk, fc = divmod(c, 16)
                        vector.wait_ge(pe2.h, pe2_after(vj, c))
                        ev2.inc(vector.tensor_copy(
                            vtst[:, sblk * D + fc * 128: sblk * D + fc * 128 + 128],
                            pt[:, (c % 2) * 512:(c % 2) * 512 + 128]), 1)
                    vj += 1

        @blk.gpsimd
        def _(g):
            vj = 0
            for j, job in enumerate(jobs):
                kind, ts, xsrc, wT, dst = job
                if kind in ("k", "q"):
                    g.wait_ge(ev.h, ev_after(j, 15))
                    dstap = dst.ap()[:, ts * 512:(ts + 1) * 512] \
                        .rearrange("(c p) n -> p c n", p=128)
                    sp.inc(g.dma_start(dstap, kst[:].rearrange("p (c n) -> p c n", c=DCH)), 16)
                else:
                    for sblk in range(4):
                        g.wait_ge(ev2.h, ev2_after(vj, 16 * (sblk + 1) - 1))
                        row = ts * 512 + sblk * 128
                        sp.inc(g.dma_start(dst.ap()[row:row + 128, :],
                                           vtst[:, sblk * D:(sblk + 1) * D]), 16)
                    vj += 1

    dbg_dump(env, [("ksc", t["ksc"], None), ("vsc", t["vsc"], None), ("qsc", t["qsc"], None)],
             wait=("sp",))


def dbg_dump(env, items, wait=()):
    nc, s = env.nc, env.s
    items = [(n, src, ap) for (n, src, ap) in items if n in env.dbg]
    if not items:
        return
    with nc.Block() as blk:
        @blk.sync
        def _(sync):
            for w in wait:
                sync.wait_ge(s[w].h, s[w].n)
            for name, src, ap in items:
                srcap = src.ap() if hasattr(src, "ap") else (ap if ap is not None else src[:])
                s["st"].inc(sync.dma_start(env.dbg[name].ap(), srcap), 16)


def emit_p2(env):
    nc, t, s = env.nc, env.t, env.s
    lx, pe, ev, sc, sp = (s[k] for k in ("lx", "pe", "ev", "sc", "sp"))
    lx_b, pe_b, ev_b, sc_b, sp_b = lx.n, pe.n, ev.n, sc.n, sp.n

    PE_PP = 34
    EV_PP = 36

    def pe_qk(p):
        return pe_b + PE_PP * p + 1

    def pe_tr(p, c):
        return pe_b + PE_PP * p + 2 + c

    def pe_av(p):
        return pe_b + PE_PP * p + PE_PP

    def ev_mx(p):
        return ev_b + EV_PP * p + 1

    def ev_stt(p):
        return ev_b + EV_PP * p + 2

    def ev_rc(p):
        return ev_b + EV_PP * p + 3

    def ev_et(p, c):
        return ev_b + EV_PP * p + 4 + c

    def ev_o(p):
        return ev_b + EV_PP * p + EV_PP

    def sc_exp(p):
        return sc_b + 8 * (p + 1)

    with (
        nc.sbuf_tensor("p2_k0", [128, S], F32) as k0,
        nc.sbuf_tensor("p2_k1", [128, S], F32) as k1,
        nc.sbuf_tensor("p2_v0", [128, S], F32) as v0,
        nc.sbuf_tensor("p2_v1", [128, S], F32) as v1,
        nc.sbuf_tensor("p2_q0", [128, TOK], F32) as q0,
        nc.sbuf_tensor("p2_q1", [128, TOK], F32) as q1,
        nc.sbuf_tensor("p2_e", [128, S], F32) as e_sb,
        nc.sbuf_tensor("p2_em", [128, S], F32) as e_m,
        nc.sbuf_tensor("p2_et", [128, 256], F32) as et_sb,
        nc.sbuf_tensor("p2_o", [128, 256], F32) as o_st,
        nc.sbuf_tensor("p2_bm", [128, 64], F32) as bm,
        nc.sbuf_tensor("p2_ac", [128, 64], F32) as acc,
        nc.sbuf_tensor("p2_nm", [128, 8], F32) as nm,
        nc.sbuf_tensor("p2_bi", [128, 8], F32) as bias,
        nc.sbuf_tensor("p2_ll", [128, 8], F32) as ll,
        nc.sbuf_tensor("p2_rr", [128, 8], F32) as rr,
        nc.psum_tensor("p2_ps", [128, 4096], F32) as pss,
        nc.Block() as blk,
    ):
        kb_, vb_, qb_ = (k0, k1), (v0, v1), (q0, q1)

        @blk.sync
        def _(sync):
            sync.wait_ge(sp.h, sp_b)
            for h in range(NH):
                if h >= 2:
                    sync.wait_ge(pe.h, pe_av((h - 1) * NQT - 1))
                lx.inc(sync.dma_start(kb_[h % 2][:],
                                      t["ksc"].ap()[h * 128:(h + 1) * 128, :]), 16)
                src = t["vsc"].ap()[:, h * 128:(h + 1) * 128] \
                    .rearrange("(c p) n -> p c n", p=128)
                lx.inc(sync.dma_start(
                    vb_[h % 2][:].rearrange("p (c n) -> p c n", c=S // 128), src), 16)
                lx.inc(sync.dma_start(qb_[h % 2][:],
                                      t["qsc"].ap()[h * 128:(h + 1) * 128, :]), 16)

        @blk.tensor
        def _(tensor):
            for h in range(NH):
                kb, vb, qb = kb_[h % 2], vb_[h % 2], qb_[h % 2]
                for qt in range(NQT):
                    p = h * NQT + qt
                    if qt == 0:
                        tensor.wait_ge(lx.h, lx_b + 48 * (h + 1))
                    if p > 0:
                        tensor.wait_ge(ev.h, ev_o(p - 1))
                        tensor.wait_ge(sc.h, sc_exp(p - 1))
                    for kbi in range(8):
                        ins = tensor.matmul(pss[:, kbi * 512:(kbi + 1) * 512],
                                            qb[:, qt * 128:(qt + 1) * 128],
                                            kb[:, kbi * 512:(kbi + 1) * 512],
                                            start=True, stop=True)
                    pe.inc(ins, 1)
                    for c in range(32):
                        if c == 0:
                            tensor.wait_ge(ev.h, ev_stt(p))
                        if c >= 4:
                            tensor.wait_ge(ev.h, ev_et(p, c - 4))
                        ins = tensor.transpose(
                            pss[:, (c % 4) * 512:(c % 4) * 512 + 128],
                            e_m[:, c * 128:(c + 1) * 128], env.ident[:])
                        pe.inc(ins, 1)
                        tensor.wait_ge(ev.h, ev_et(p, c))
                        ins = tensor.matmul(pss[:, 4 * 512: 4 * 512 + 128],
                                            et_sb[:, (c % 2) * 128:(c % 2) * 128 + 128],
                                            vb[:, c * 128:(c + 1) * 128],
                                            start=(c == 0), stop=(c == 31))
                        if c == 31:
                            pe.inc(ins, 1)

        @blk.vector
        def _(vector):
            NOPS = 6
            for h in range(NH):
                for qt in range(NQT):
                    p = h * NQT + qt
                    vector.wait_ge(pe.h, pe_qk(p))
                    for i in range(8):
                        vector.reduce_max(bm[:, 8 * i:8 * i + 1],
                                          pss[:, i * 512:(i + 1) * 512], axis=AX.X)
                    for _ in range(NOPS):
                        vector.engine_nop()
                    vector.reduce_max(nm[:, 0:1], bm[:, 0:57:8], axis=AX.X, negate=True)
                    for _ in range(NOPS):
                        vector.engine_nop()
                    ev.inc(vector.tensor_scalar_mul(bias[:, 0:1], nm[:, 0:1], ISQ), 1)
                    vector.wait_ge(sc.h, sc_exp(p))
                    for i in range(8):
                        ins = vector.scalar_tensor_tensor(
                            e_m[:, i * 512:(i + 1) * 512],
                            env.kpos[:, i * 512:(i + 1) * 512],
                            env.qpos[:, qt:qt + 1], e_sb[:, i * 512:(i + 1) * 512],
                            ALU.is_le, ALU.mult, accum_out=acc[:, 8 * i:8 * i + 1])
                    ev.inc(ins, 1)
                    for _ in range(NOPS):
                        vector.engine_nop()
                    vector.reduce_sum(ll[:, 0:1], acc[:, 0:57:8], axis=AX.X)
                    for _ in range(NOPS):
                        vector.engine_nop()
                    ev.inc(vector.reciprocal(rr[:, 0:1], ll[:, 0:1]), 1)
                    for c in range(32):
                        vector.wait_ge(pe.h, pe_tr(p, c))
                        ev.inc(vector.tensor_copy(
                            et_sb[:, (c % 2) * 128:(c % 2) * 128 + 128],
                            pss[:, (c % 4) * 512:(c % 4) * 512 + 128]), 1)
                    vector.wait_ge(pe.h, pe_av(p))
                    if p >= 2:
                        vector.wait_ge(sp.h, sp_b + 16 * (p - 1))
                    ev.inc(vector.tensor_scalar_mul(
                        o_st[:, (p % 2) * 128:(p % 2) * 128 + 128],
                        pss[:, 4 * 512: 4 * 512 + 128], rr[:, 0:1]), 1)

        @blk.scalar
        def _(scalar):
            for h in range(NH):
                for qt in range(NQT):
                    p = h * NQT + qt
                    scalar.wait_ge(ev.h, ev_mx(p))
                    for i in range(8):
                        sc.inc(scalar.activation(e_sb[:, i * 512:(i + 1) * 512],
                                                 pss[:, i * 512:(i + 1) * 512], AF.Exp,
                                                 bias=bias[:, 0:1], scale=ISQ), 1)

        @blk.gpsimd
        def _(g):
            for h in range(NH):
                for qt in range(NQT):
                    p = h * NQT + qt
                    g.wait_ge(ev.h, ev_o(p))
                    sp.inc(g.dma_start(
                        t["osc"].ap()[qt * 128:(qt + 1) * 128, h * 128:(h + 1) * 128],
                        o_st[:, (p % 2) * 128:(p % 2) * 128 + 128]), 16)

    dbg_dump(env, [("osc", t["osc"], None)], wait=("sp",))


def emit_ln(env, buf, stride, off, width, tag="", out_pair=None):
    nc, s = env.nc, env.s
    pe, ev, sc = s["pe"], s["ev"], s["sc"]
    NHW = width // 512
    assert width % 512 == 0 and NHW in (1, 2)
    pe_b, ev_b, sc_b = pe.n, ev.n, sc.n
    NOPS = 6

    with (
        nc.sbuf_tensor(f"ln_sq{tag}", [128, 2 * width], F32) as sq,
        nc.sbuf_tensor(f"ln_srow{tag}", [1, 2 * width], F32) as srow,
        nc.sbuf_tensor(f"ln_inv{tag}", [1, 2 * width], F32) as invr,
        nc.psum_tensor(f"ln_ps{tag}", [128, 2 * width], F32) as pln,
        nc.psum_tensor(f"ln_bc{tag}", [128, 2 * width], F32) as pbc,
        nc.Block() as blk,
    ):
        def sl(d):
            return buf[:, d * stride + off: d * stride + off + width]

        @blk.scalar
        def _(scalar):
            for d in range(DCH):
                if d >= 2:
                    scalar.wait_ge(pe.h, pe_b + d - 1)
                sc.inc(scalar.activation(sq[:, (d % 2) * width:(d % 2) * width + width],
                                         sl(d), AF.Square), 1)
            scalar.wait_ge(ev.h, ev_b + 2)
            sc.inc(scalar.activation(invr[0:1, width:2 * width], srow[0:1, width:2 * width],
                                     AF.Sqrt, bias=0.0, scale=1.0), 1)

        @blk.tensor
        def _(tensor):
            for d in range(DCH):
                tensor.wait_ge(sc.h, sc_b + d + 1)
                for hw in range(NHW):
                    tensor.matmul(pln[0:1, hw * 512:(hw + 1) * 512], env.onescol[:],
                                  sl(d)[:, hw * 512:(hw + 1) * 512],
                                  start=(d == 0), stop=(d == DCH - 1))
                    ins = tensor.matmul(
                        pln[0:1, width + hw * 512: width + (hw + 1) * 512],
                        env.onescol[:],
                        sq[:, (d % 2) * width + hw * 512:(d % 2) * width + (hw + 1) * 512],
                        start=(d == 0), stop=(d == DCH - 1))
                pe.inc(ins, 1)
            tensor.wait_ge(ev.h, ev_b + 4)
            for hw in range(NHW):
                tensor.matmul(pbc[:, hw * 512:(hw + 1) * 512], env.onesrow[:],
                              invr[0:1, hw * 512:(hw + 1) * 512], start=True, stop=True)
                ins = tensor.matmul(pbc[:, width + hw * 512: width + (hw + 1) * 512],
                                    env.onesrow[:],
                                    srow[0:1, width + hw * 512: width + (hw + 1) * 512],
                                    start=True, stop=True)
            pe.inc(ins, 1)

        @blk.vector
        def _(vector):
            vector.wait_ge(pe.h, pe_b + DCH)
            ev.inc(vector.tensor_scalar_mul(srow[0:1, 0:2 * width],
                                            pln[0:1, 0:2 * width], 1.0 / D), 1)
            for _ in range(NOPS):
                vector.engine_nop()
            vector.tensor_mul(invr[0:1, 0:width], srow[0:1, 0:width], srow[0:1, 0:width])
            for _ in range(NOPS):
                vector.engine_nop()
            vector.tensor_sub(invr[0:1, width:2 * width],
                              srow[0:1, width:2 * width], invr[0:1, 0:width])
            for _ in range(NOPS):
                vector.engine_nop()
            ev.inc(vector.tensor_scalar_add(srow[0:1, width:2 * width],
                                            invr[0:1, width:2 * width], LN_EPS), 1)
            vector.wait_ge(sc.h, sc_b + DCH + 1)
            vector.reciprocal(invr[0:1, 0:width], invr[0:1, width:2 * width])
            for _ in range(NOPS):
                vector.engine_nop()
            ev.inc(vector.engine_nop(), 1)
            ev.inc(vector.tensor_mul(srow[0:1, width:2 * width], srow[0:1, 0:width],
                                     invr[0:1, 0:width]), 1)
            vector.wait_ge(pe.h, pe_b + DCH + 1)
            for d in range(DCH):
                vector.tensor_mul(sq[:, 0:width], sl(d), pbc[:, 0:width])
                if out_pair is None:
                    ins = vector.tensor_sub(sl(d), sq[:, 0:width], pbc[:, width:2 * width])
                else:
                    hi, lo, ostride = out_pair
                    osl_h = hi[:, d * ostride + off: d * ostride + off + width]
                    osl_l = lo[:, d * ostride + off: d * ostride + off + width]
                    vector.tensor_sub(sq[:, width:2 * width], sq[:, 0:width],
                                      pbc[:, width:2 * width])
                    vector.tensor_copy(osl_h, sq[:, width:2 * width])
                    ins = vector.tensor_sub(osl_l, sq[:, width:2 * width], osl_h)
            ev.inc(ins, 1)


def emit_wsplit(env, vector, wf, wh, wl, ncols):
    vector.tensor_copy(wh[:, 0:ncols], wf[:, 0:ncols])
    return vector.tensor_sub(wl[:, 0:ncols], wf[:, 0:ncols], wh[:, 0:ncols])


def mm3(tensor, out, lh, ll, rh, rl, start, stop):
    tensor.matmul(out, lh, rh, start=start, stop=False)
    tensor.matmul(out, lh, rl, start=False, stop=False)
    return tensor.matmul(out, ll, rh, start=False, stop=stop)


def emit_p3(env):
    nc, t, s = env.nc, env.t, env.s
    lw, lx, pe, pe2, ev, ev2, sp, sc, a2 = (s[k] for k in
        ("lw", "lx", "pe", "pe2", "ev", "ev2", "sp", "sc", "a2"))
    lw_b, lx_b, pe_b, pe2_b, ev_b, ev2_b, sc_b, a2_b = \
        lw.n, lx.n, pe.n, pe2.n, ev.n, ev2.n, sc.n, a2.n

    pe_pf = lambda tt, f: pe_b + 16 * tt + f + 1
    ev_r1 = lambda tt, f: ev_b + 16 * tt + f + 1
    pe2_tr = lambda tt, c: pe2_b + 64 * tt + c + 1
    ev2_tr = lambda tt, c: ev2_b + 128 * tt + 2 * (c + 1)
    lw_after = lambda gi: lw_b + 16 * (gi + 1)
    lx_after = lambda i: lx_b + 16 * (i + 1)

    r1 = env.r1buf

    with (
        nc.sbuf_tensor("p3_ot", [128, 2 * D], F32) as ot,
        nc.sbuf_tensor("p3_afh", [128, DCH * 512], BF16) as afh,
        nc.sbuf_tensor("p3_afl", [128, DCH * 512], BF16) as afl,
        nc.sbuf_tensor("p3_wf", [128, DCH * 128], F32) as wf,
        nc.sbuf_tensor("p3_wh0", [128, DCH * 128], BF16) as wh0,
        nc.sbuf_tensor("p3_wh1", [128, DCH * 128], BF16) as wh1,
        nc.sbuf_tensor("p3_wl0", [128, DCH * 128], BF16) as wl0,
        nc.sbuf_tensor("p3_wl1", [128, DCH * 128], BF16) as wl1,
        nc.sbuf_tensor("p3_xr", [128, 1024], F32) as xr,
        nc.psum_tensor("p3_ps", [128, 1024], F32) as ps,
        nc.psum_tensor("p3_pt", [128, 2048], F32) as pt,
        nc.Block() as blk,
    ):
        whb, wlb = (wh0, wh1), (wl0, wl1)

        @blk.sync
        def _(sync):
            for tt in range(2):
                sync.wait_ge(sp.h, sp.n)
                if tt == 1:
                    sync.wait_ge(pe2.h, pe2_tr(0, 63))
                for cch in range(4):
                    if cch >= 2:
                        sync.wait_ge(pe2.h, pe2_tr(tt, 16 * (cch - 1) - 1))
                    lx.inc(sync.dma_start(
                        ot[:, (cch % 2) * D:(cch % 2) * D + D],
                        t["osc"].ap()[tt * 512 + cch * 128: tt * 512 + (cch + 1) * 128, :]), 16)
                for f in range(16):
                    gi = 16 * tt + f
                    if gi >= 1:
                        sync.wait_ge(a2.h, a2_b + gi)
                    src = t["wpT"].ap()[:, f * 128:(f + 1) * 128] \
                        .rearrange("(c p) n -> p c n", p=128)
                    lw.inc(sync.dma_start(
                        wf[:].rearrange("p (c n) -> p c n", c=DCH), src), 16)
                    if gi >= 2:
                        pj, pf = divmod(gi - 2, 16)
                        sync.wait_ge(ev.h, ev_r1(pj, pf))
                    src2 = t["xTo"].ap()[f * 128:(f + 1) * 128, tt * 512:(tt + 1) * 512]
                    lx.inc(sync.dma_start(xr[:, (gi % 2) * 512:(gi % 2) * 512 + 512], src2), 16)

        @blk.scalar
        def _(scalar):
            for tt in range(2):
                for f in range(16):
                    gi = 16 * tt + f
                    scalar.wait_ge(lw.h, lw_after(gi))
                    if gi >= 2:
                        pj, pf = divmod(gi - 2, 16)
                        scalar.wait_ge(pe.h, pe_pf(pj, pf))
                    sc.inc(scalar.copy(whb[gi % 2][:], wf[:]), 1)

        @blk.tensor
        def _(tensor):
            for tt in range(2):
                for c in range(64):
                    sblk, fc = divmod(c, 16)
                    tensor.wait_ge(lx.h, lx_after(tt * 20 + sblk))
                    if c >= 4:
                        tensor.wait_ge(ev2.h, ev2_tr(tt, c - 4))
                    ins = tensor.transpose(pt[:, (c % 4) * 512:(c % 4) * 512 + 128],
                                           ot[:, (sblk % 2) * D + fc * 128: (sblk % 2) * D + fc * 128 + 128],
                                           env.ident[:])
                    pe2.inc(ins, 1)
                for f in range(16):
                    gi = 16 * tt + f
                    if f == 0:
                        tensor.wait_ge(ev2.h, ev2_tr(tt, 63))
                    tensor.wait_ge(a2.h, a2_b + gi + 1)
                    if gi >= 2:
                        pj, pf = divmod(gi - 2, 16)
                        tensor.wait_ge(ev.h, ev_r1(pj, pf))
                    par = gi % 2
                    for d in range(DCH):
                        ins = mm3(tensor, ps[:, par * 512: par * 512 + 512],
                                  whb[par][:, d * 128:(d + 1) * 128],
                                  wlb[par][:, d * 128:(d + 1) * 128],
                                  afh[:, d * 512:(d + 1) * 512],
                                  afl[:, d * 512:(d + 1) * 512],
                                  start=(d == 0), stop=(d == DCH - 1))
                    pe.inc(ins, 1)

        @blk.vector
        def _(vector):
            for tt in range(2):
                for c in range(64):
                    sblk, fc = divmod(c, 16)
                    vector.wait_ge(pe2.h, pe2_tr(tt, c))
                    if tt == 1 and c == 0:
                        vector.wait_ge(pe.h, pe_pf(0, 15))
                    dsth = afh[:, fc * 512 + sblk * 128: fc * 512 + sblk * 128 + 128]
                    dstl = afl[:, fc * 512 + sblk * 128: fc * 512 + sblk * 128 + 128]
                    pslot = pt[:, (c % 4) * 512:(c % 4) * 512 + 128]
                    ev2.inc(vector.tensor_copy(dsth, pslot), 1)
                    ev2.inc(vector.tensor_sub(dstl, pslot, dsth), 1)
                for f in range(16):
                    gi = 16 * tt + f
                    vector.wait_ge(sc.h, sc_b + gi + 1)
                    if gi >= 2:
                        pj, pf = divmod(gi - 2, 16)
                        vector.wait_ge(pe.h, pe_pf(pj, pf))
                    a2.inc(vector.tensor_sub(wlb[gi % 2][:], wf[:], whb[gi % 2][:]), 1)
                    vector.wait_ge(pe.h, pe_pf(tt, f))
                    vector.wait_ge(lx.h, lx_after(tt * 20 + 4 + f))
                    ev.inc(vector.tensor_add(
                        r1[:, f * TOK + tt * 512: f * TOK + tt * 512 + 512],
                        ps[:, (gi % 2) * 512:(gi % 2) * 512 + 512],
                        xr[:, (gi % 2) * 512:(gi % 2) * 512 + 512]), 1)

    emit_ln(env, r1, TOK, 0, 512, "a", out_pair=(env.h1h, env.h1l, TOK))
    emit_ln(env, r1, TOK, 512, 512, "b", out_pair=(env.h1h, env.h1l, TOK))
    if "h1" in env.dbg:
        with nc.Block() as blk:
            @blk.sync
            def _(sync):
                for c2 in range(DCH):
                    s["st"].inc(sync.dma_start(
                        env.dbg["h1"].ap()[c2 * 128:(c2 + 1) * 128, 0:TOK],
                        env.h1h[:, c2 * TOK:(c2 + 1) * TOK]), 16)
                    s["st"].inc(sync.dma_start(
                        env.dbg["h1"].ap()[c2 * 128:(c2 + 1) * 128, TOK:2 * TOK],
                        env.h1l[:, c2 * TOK:(c2 + 1) * TOK]), 16)


def emit_p4(env):
    nc, t, s = env.nc, env.t, env.s
    h1h, h1l = env.h1h, env.h1l
    lw, pe, ev, sc, sp = (s[k] for k in ("lw", "pe", "ev", "sc", "sp"))
    NOPS = 6

    with contextlib.ExitStack() as stack:
        en = stack.enter_context
        rwf = en(nc.sbuf_tensor("p4_rwf", [128, DCH * E], F32))
        rwh = en(nc.sbuf_tensor("p4_rwh", [128, DCH * E], BF16))
        rwl = en(nc.sbuf_tensor("p4_rwl", [128, DCH * E], BF16))
        zwf = en(nc.sbuf_tensor("p4_zwf", [128, DCH], F32))
        zwh = en(nc.sbuf_tensor("p4_zwh", [128, DCH], BF16))
        zwl = en(nc.sbuf_tensor("p4_zwl", [128, DCH], BF16))
        lg = en(nc.sbuf_tensor("p4_lg", [E, TOK], F32))
        lt = en(nc.sbuf_tensor("p4_lt", [128, 8], F32))
        er = en(nc.sbuf_tensor("p4_er", [128, 8], F32))
        pr = en(nc.sbuf_tensor("p4_pr", [128, 8], F32))
        eq1 = en(nc.sbuf_tensor("p4_e1", [128, 8], F32))
        t1 = en(nc.sbuf_tensor("p4_t1", [128, 8], F32))
        pr2 = en(nc.sbuf_tensor("p4_p2", [128, 8], F32))
        eq2 = en(nc.sbuf_tensor("p4_e2", [128, 8], F32))
        t2 = en(nc.sbuf_tensor("p4_t2", [128, 8], F32))
        cm = en(nc.sbuf_tensor("p4_cm", [128, 8], F32))
        cmb = en(nc.sbuf_tensor("p4_cmb", [E, TOK], F32))
        s_nm = en(nc.sbuf_tensor("p4_s0", [128, 8], F32))
        s_sum = en(nc.sbuf_tensor("p4_s1", [128, 8], F32))
        s_rs = en(nc.sbuf_tensor("p4_s2", [128, 8], F32))
        s_m1 = en(nc.sbuf_tensor("p4_s3", [128, 8], F32))
        s_m2 = en(nc.sbuf_tensor("p4_s4", [128, 8], F32))
        s_w = en(nc.sbuf_tensor("p4_s5", [128, 8], F32))
        s_wr = en(nc.sbuf_tensor("p4_s6", [128, 8], F32))
        pl = en(nc.psum_tensor("p4_pl", [128, 1024], F32))
        pz = en(nc.psum_tensor("p4_pz", [128, 1024], F32))
        pt = en(nc.psum_tensor("p4_pt", [128, 1024], F32))
        blk = en(nc.Block())

        lw_b, pe_b, ev_b, sc_b, sp_b = lw.n, pe.n, ev.n, sc.n, sp.n
        pe_t = lambda i, k: pe_b + 2 + 2 * i + k
        ev_t = lambda i, k: ev_b + 2 + 3 * i + k
        sc_t = lambda i: sc_b + 1 + i + 1

        @blk.sync
        def _(sync):
            src2 = t["rwT"].ap()[:, :].rearrange("(c p) n -> p c n", p=128)
            lw.inc(sync.dma_start(rwf[:].rearrange("p (c n) -> p c n", c=DCH), src2), 16)
            lw.inc(sync.dma_start(zwf[:], t["sgwT"].ap()), 16)

        @blk.vector
        def _(vector):
            vector.wait_ge(lw.h, lw_b + 32)
            vector.tensor_copy(rwh[:], rwf[:])
            vector.tensor_copy(zwh[:], zwf[:])
            for _ in range(NOPS):
                vector.engine_nop()
            vector.tensor_sub(rwl[:], rwf[:], rwh[:])
            ev.inc(vector.tensor_sub(zwl[:], zwf[:], zwh[:]), 1)
            vector.wait_ge(pe.h, pe_b + 2)
            vector.tensor_copy(lg[0:E, 0:512], pl[0:E, 0:512])
            ev.inc(vector.tensor_copy(lg[0:E, 512:1024], pl[0:E, 512:1024]), 1)
            for i in range(NQT):
                par = i % 2
                vector.wait_ge(pe.h, pe_t(i, 1))
                vector.tensor_copy(lt[:, 0:E], pt[:, par * 512: par * 512 + E])
                for _ in range(NOPS):
                    vector.engine_nop()
                ev.inc(vector.reduce_max(s_nm[:, 0:1], lt[:, 0:E], axis=AX.X, negate=True), 1)
                vector.wait_ge(sc.h, sc_t(i))
                vector.reciprocal(s_rs[:, 0:1], s_sum[:, 0:1])
                for _ in range(NOPS):
                    vector.engine_nop()
                vector.tensor_scalar_mul(pr[:, 0:E], er[:, 0:E], s_rs[:, 0:1])
                for _ in range(NOPS):
                    vector.engine_nop()
                vector.reduce_max(s_m1[:, 0:1], pr[:, 0:E], axis=AX.X)
                for _ in range(NOPS):
                    vector.engine_nop()
                vector.tensor_scalar(eq1[:, 0:E], pr[:, 0:E], s_m1[:, 0:1], None, ALU.is_equal)
                for _ in range(NOPS):
                    vector.engine_nop()
                vector.tensor_mul(t1[:, 0:E], pr[:, 0:E], eq1[:, 0:E])
                for _ in range(NOPS):
                    vector.engine_nop()
                vector.tensor_sub(pr2[:, 0:E], pr[:, 0:E], t1[:, 0:E])
                for _ in range(NOPS):
                    vector.engine_nop()
                vector.reduce_max(s_m2[:, 0:1], pr2[:, 0:E], axis=AX.X)
                for _ in range(NOPS):
                    vector.engine_nop()
                vector.tensor_scalar(eq2[:, 0:E], pr2[:, 0:E], s_m2[:, 0:1], None, ALU.is_equal)
                for _ in range(NOPS):
                    vector.engine_nop()
                vector.tensor_add(t1[:, 0:E], eq1[:, 0:E], eq2[:, 0:E])
                vector.tensor_add(s_w[:, 0:1], s_m1[:, 0:1], s_m2[:, 0:1])
                for _ in range(NOPS):
                    vector.engine_nop()
                vector.reciprocal(s_wr[:, 0:1], s_w[:, 0:1])
                vector.tensor_mul(t2[:, 0:E], pr[:, 0:E], t1[:, 0:E])
                for _ in range(NOPS):
                    vector.engine_nop()
                ev.inc(vector.tensor_scalar_mul(cm[:, 0:E], t2[:, 0:E], s_wr[:, 0:1]), 1)
                vector.wait_ge(pe.h, pe_t(i, 2))
                ev.inc(vector.tensor_copy(cmb[0:E, i * 128:(i + 1) * 128],
                                          pt[0:E, par * 512 + 128: par * 512 + 256]), 1)

        @blk.tensor
        def _(tensor):
            tensor.wait_ge(ev.h, ev_b + 1)
            for hw in range(2):
                for d in range(DCH):
                    rh = h1h[:, d * TOK + hw * 512: d * TOK + hw * 512 + 512]
                    rl = h1l[:, d * TOK + hw * 512: d * TOK + hw * 512 + 512]
                    mm3(tensor, pl[0:E, hw * 512:(hw + 1) * 512],
                        rwh[:, d * E:(d + 1) * E], rwl[:, d * E:(d + 1) * E],
                        rh, rl, start=(d == 0), stop=(d == DCH - 1))
                    ins = mm3(tensor, pz[0:1, hw * 512:(hw + 1) * 512],
                              zwh[:, d:d + 1], zwl[:, d:d + 1],
                              rh, rl, start=(d == 0), stop=(d == DCH - 1))
                pe.inc(ins, 1)
            for i in range(NQT):
                par = i % 2
                tensor.wait_ge(ev.h, ev_b + 2)
                if i >= 2:
                    tensor.wait_ge(ev.h, ev_t(i - 2, 3))
                ins = tensor.transpose(pt[:, par * 512: par * 512 + E],
                                       lg[0:E, i * 128:(i + 1) * 128],
                                       env.ident[0:E, 0:E])
                pe.inc(ins, 1)
                tensor.wait_ge(ev.h, ev_t(i, 2))
                ins = tensor.transpose(pt[0:E, par * 512 + 128: par * 512 + 256],
                                       cm[:, 0:E], env.ident[:])
                pe.inc(ins, 1)

        @blk.scalar
        def _(scalar):
            scalar.wait_ge(pe.h, pe_b + 2)
            scalar.wait_ge(ev.h, ev_b + 2)
            scalar.activation(env.sig[0:1, 0:512], pz[0:1, 0:512], AF.Sigmoid)
            sc.inc(scalar.activation(env.sig[0:1, 512:1024], pz[0:1, 512:1024], AF.Sigmoid), 1)
            for i in range(NQT):
                scalar.wait_ge(ev.h, ev_t(i, 1))
                sc.inc(scalar.activation(er[:, 0:E], lt[:, 0:E], AF.Exp,
                                         bias=s_nm[:, 0:1], scale=1.0,
                                         accum_out=s_sum[:, 0:1]), 1)

        @blk.gpsimd
        def _(g):
            g.wait_ge(ev.h, ev_t(NQT - 1, 3))
            sp.inc(g.dma_start(t["csc"].ap(), cmb[0:E, :]), 16)


def emit_p5(env):
    nc, t, s = env.nc, env.t, env.s
    lx, pe, ev, sp = (s[k] for k in ("lx", "pe", "ev", "sp"))

    moe = nc.alloc_sbuf_tensor("p5_moe", [128, DCH * 512], F32)
    sig_bc = nc.alloc_sbuf_tensor("p5_sigbc", [128, TOK], F32)

    with (
        nc.psum_tensor("p5_pb", [128, 1024], F32) as pb,
        nc.Block() as blk,
    ):
        pe_b, ev_b = pe.n, ev.n

        @blk.sync
        def _(sync):
            sync.wait_ge(sp.h, sp.n)

        @blk.tensor
        def _(tensor):
            for hw in range(2):
                ins = tensor.matmul(pb[:, hw * 512:(hw + 1) * 512], env.onesrow[:],
                                    env.sig[0:1, hw * 512:(hw + 1) * 512],
                                    start=True, stop=True)
            pe.inc(ins, 1)

        @blk.vector
        def _(vector):
            vector.wait_ge(pe.h, pe_b + 1)
            for hw in range(2):
                ev.inc(vector.tensor_copy(sig_bc[:, hw * 512:(hw + 1) * 512],
                                          pb[:, hw * 512:(hw + 1) * 512]), 1)

    for H in range(2):
        _p5_shared(env, moe, sig_bc, H)
        _p5_experts(env, moe, H)
        _p5_tail(env, moe, H)


def _p5_shared(env, moe, sig_bc, H):
    nc, t, s = env.nc, env.t, env.s
    h1h, h1l = env.h1h, env.h1l
    lw, pe, ev, sc = (s[k] for k in ("lw", "pe", "ev", "sc"))
    NSH = ISH // 128

    for q in range(2):
        off = H * 512 + q * 256
        with contextlib.ExitStack() as ostack:
            acth = ostack.enter_context(nc.sbuf_tensor(f"p5s_ah{H}{q}", [128, NSH * 256], BF16))
            actl = ostack.enter_context(nc.sbuf_tensor(f"p5s_al{H}{q}", [128, NSH * 256], BF16))
            stack = ostack.enter_context(contextlib.ExitStack())
            en = stack.enter_context
            gf = en(nc.sbuf_tensor(f"p5s_gf{H}{q}", [128, DCH * 128], F32))
            uf = en(nc.sbuf_tensor(f"p5s_uf{H}{q}", [128, DCH * 128], F32))
            gh0 = en(nc.sbuf_tensor(f"p5s_gh0{H}{q}", [128, DCH * 128], BF16))
            gh1 = en(nc.sbuf_tensor(f"p5s_gh1{H}{q}", [128, DCH * 128], BF16))
            gl0 = en(nc.sbuf_tensor(f"p5s_gl0{H}{q}", [128, DCH * 128], BF16))
            gl1 = en(nc.sbuf_tensor(f"p5s_gl1{H}{q}", [128, DCH * 128], BF16))
            uh0 = en(nc.sbuf_tensor(f"p5s_uh0{H}{q}", [128, DCH * 128], BF16))
            uh1 = en(nc.sbuf_tensor(f"p5s_uh1{H}{q}", [128, DCH * 128], BF16))
            ul0 = en(nc.sbuf_tensor(f"p5s_ul0{H}{q}", [128, DCH * 128], BF16))
            ul1 = en(nc.sbuf_tensor(f"p5s_ul1{H}{q}", [128, DCH * 128], BF16))
            tmp = en(nc.sbuf_tensor(f"p5s_tmp{H}{q}", [128, 512], F32))
            tm2 = en(nc.sbuf_tensor(f"p5s_tm2{H}{q}", [128, 512], F32))
            pg = en(nc.psum_tensor(f"p5s_pg{H}{q}", [128, 1024], F32))
            pu = en(nc.psum_tensor(f"p5s_pu{H}{q}", [128, 1024], F32))
            blk = en(nc.Block())
            ghb, glb, uhb, ulb = (gh0, gh1), (gl0, gl1), (uh0, uh1), (ul0, ul1)
            lw_b, pe_b, ev_b, sc_b = lw.n, pe.n, ev.n, sc.n
            ev_sp = lambda i: ev_b + 2 * i + 1
            ev_ac = lambda i: ev_b + 2 * i + 2

            @blk.sync
            def _(sync):
                for i in range(NSH):
                    if i >= 1:
                        sync.wait_ge(ev.h, ev_sp(i - 1))
                    src = t["sgT"].ap()[:, i * 128:(i + 1) * 128] \
                        .rearrange("(c p) n -> p c n", p=128)
                    lw.inc(sync.dma_start(
                        gf[:].rearrange("p (c n) -> p c n", c=DCH), src), 16)
                    src = t["suT"].ap()[:, i * 128:(i + 1) * 128] \
                        .rearrange("(c p) n -> p c n", p=128)
                    lw.inc(sync.dma_start(
                        uf[:].rearrange("p (c n) -> p c n", c=DCH), src), 16)

            @blk.vector
            def _(vector):
                for i in range(NSH):
                    par = i % 2
                    vector.wait_ge(lw.h, lw_b + 32 * (i + 1))
                    if i >= 2:
                        vector.wait_ge(pe.h, pe_b + i - 1)
                    vector.tensor_copy(ghb[par][:], gf[:])
                    vector.tensor_copy(uhb[par][:], uf[:])
                    vector.tensor_sub(glb[par][:], gf[:], ghb[par][:])
                    ev.inc(vector.tensor_sub(ulb[par][:], uf[:], uhb[par][:]), 1)
                    vector.wait_ge(sc.h, sc_b + i + 1)
                    vector.tensor_mul(tm2[:, par * 256: par * 256 + 256],
                                      tmp[:, par * 256: par * 256 + 256],
                                      sig_bc[:, off:off + 256])
                    vector.tensor_mul(tmp[:, par * 256: par * 256 + 256],
                                      tm2[:, par * 256: par * 256 + 256],
                                      pu[:, par * 512: par * 512 + 256])
                    vector.tensor_copy(acth[:, i * 256:(i + 1) * 256],
                                       tmp[:, par * 256: par * 256 + 256])
                    ev.inc(vector.tensor_sub(actl[:, i * 256:(i + 1) * 256],
                                             tmp[:, par * 256: par * 256 + 256],
                                             acth[:, i * 256:(i + 1) * 256]), 1)

            @blk.tensor
            def _(tensor):
                for i in range(NSH):
                    tensor.wait_ge(ev.h, ev_sp(i))
                    if i >= 2:
                        tensor.wait_ge(sc.h, sc_b + i - 1)
                        tensor.wait_ge(ev.h, ev_ac(i - 1))
                    par = i % 2
                    for d in range(DCH):
                        rh = h1h[:, d * TOK + off: d * TOK + off + 256]
                        rl = h1l[:, d * TOK + off: d * TOK + off + 256]
                        mm3(tensor, pg[:, par * 512: par * 512 + 256],
                            ghb[par][:, d * 128:(d + 1) * 128],
                            glb[par][:, d * 128:(d + 1) * 128],
                            rh, rl, start=(d == 0), stop=(d == DCH - 1))
                        ins = mm3(tensor, pu[:, par * 512: par * 512 + 256],
                                  uhb[par][:, d * 128:(d + 1) * 128],
                                  ulb[par][:, d * 128:(d + 1) * 128],
                                  rh, rl, start=(d == 0), stop=(d == DCH - 1))
                    pe.inc(ins, 1)

            @blk.scalar
            def _(scalar):
                for i in range(NSH):
                    par = i % 2
                    scalar.wait_ge(pe.h, pe_b + i + 1)
                    if i >= 2:
                        scalar.wait_ge(ev.h, ev_ac(i - 2))
                    sc.inc(scalar.activation(tmp[:, par * 256: par * 256 + 256],
                                             pg[:, par * 512: par * 512 + 256],
                                             AF.Silu), 1)

            stack.close()
            _p5_shared_down(env, moe, acth, actl, H, q)


def _p5_shared_down(env, moe, acth, actl, H, q):
    nc, t, s = env.nc, env.t, env.s
    lw, pe, ev = (s[k] for k in ("lw", "pe", "ev"))
    NSH = ISH // 128
    NHH = NSH // 2
    with (
        nc.sbuf_tensor(f"p5d_sf{H}{q}", [128, NHH * 128], F32) as sf,
        nc.sbuf_tensor(f"p5d_h0{H}{q}", [128, NHH * 128], BF16) as sh0,
        nc.sbuf_tensor(f"p5d_h1{H}{q}", [128, NHH * 128], BF16) as sh1,
        nc.sbuf_tensor(f"p5d_l0{H}{q}", [128, NHH * 128], BF16) as sl0,
        nc.sbuf_tensor(f"p5d_l1{H}{q}", [128, NHH * 128], BF16) as sl1,
        nc.psum_tensor(f"p5d_pd{H}{q}", [128, 1024], F32) as pd,
        nc.Block() as blk,
    ):
        shb, slb = (sh0, sh1), (sl0, sl1)
        lw_b, pe_b, ev_b = lw.n, pe.n, ev.n
        ev_sp = lambda g: ev_b + g + 1 + (g // 2 if False else (g + 1) // 2 * 0)
        def ev_split(g):
            f, hh = divmod(g, 2)
            return ev_b + g + 1 + f
        def ev_split(g):
            f, hh = divmod(g, 2)
            return ev_b + f * 3 + hh + 1
        def ev_moe(f):
            return ev_b + f * 3 + 3
        pe_g = lambda g: pe_b + g + 1

        @blk.sync
        def _(sync):
            for g in range(2 * DCH):
                f, hh = divmod(g, 2)
                if g >= 1:
                    sync.wait_ge(ev.h, ev_split(g - 1))
                src = t["sdT"].ap()[hh * (NHH * 128):(hh + 1) * (NHH * 128),
                                    f * 128:(f + 1) * 128] \
                    .rearrange("(c p) n -> p c n", p=128)
                lw.inc(sync.dma_start(
                    sf[:].rearrange("p (c n) -> p c n", c=NHH), src), 16)

        @blk.vector
        def _(vector):
            for g in range(2 * DCH):
                f, hh = divmod(g, 2)
                par = g % 2
                vector.wait_ge(lw.h, lw_b + 16 * (g + 1))
                if g >= 2:
                    vector.wait_ge(pe.h, pe_g(g - 2))
                vector.tensor_copy(shb[par][:], sf[:])
                ev.inc(vector.tensor_sub(slb[par][:], sf[:], shb[par][:]), 1)
                if hh == 1:
                    vector.wait_ge(pe.h, pe_g(g))
                    ev.inc(vector.tensor_copy(
                        moe[:, f * 512 + q * 256: f * 512 + q * 256 + 256],
                        pd[:, (f % 2) * 512:(f % 2) * 512 + 256]), 1)

        @blk.tensor
        def _(tensor):
            for g in range(2 * DCH):
                f, hh = divmod(g, 2)
                par = g % 2
                tensor.wait_ge(ev.h, ev_split(g))
                if hh == 0 and f >= 2:
                    tensor.wait_ge(ev.h, ev_moe(f - 2))
                for i in range(NHH):
                    ins = mm3(tensor, pd[:, (f % 2) * 512:(f % 2) * 512 + 256],
                              shb[par][:, i * 128:(i + 1) * 128],
                              slb[par][:, i * 128:(i + 1) * 128],
                              acth[:, (hh * NHH + i) * 256:(hh * NHH + i + 1) * 256],
                              actl[:, (hh * NHH + i) * 256:(hh * NHH + i + 1) * 256],
                              start=(g % 2 == 0 and i == 0), stop=(g % 2 == 1 and i == NHH - 1))
                pe.inc(ins, 1)


def _p5_experts(env, moe, H):
    nc, t, s = env.nc, env.t, env.s
    h1h, h1l = env.h1h, env.h1l
    lw, lx, pe, ev, sc = (s[k] for k in ("lw", "lx", "pe", "ev", "sc"))
    NIM = IM // 128
    off = H * 512

    with contextlib.ExitStack() as stack:
        en = stack.enter_context
        heh = en(nc.sbuf_tensor(f"p5e_hh{H}", [128, NIM * 512], BF16))
        hel = en(nc.sbuf_tensor(f"p5e_hl{H}", [128, NIM * 512], BF16))
        gf = en(nc.sbuf_tensor(f"p5e_gf{H}", [128, DCH * 128], F32))
        uf = en(nc.sbuf_tensor(f"p5e_uf{H}", [128, DCH * 128], F32))
        gh0 = en(nc.sbuf_tensor(f"p5e_gh0{H}", [128, DCH * 128], BF16))
        gh1 = en(nc.sbuf_tensor(f"p5e_gh1{H}", [128, DCH * 128], BF16))
        gl0 = en(nc.sbuf_tensor(f"p5e_gl0{H}", [128, DCH * 128], BF16))
        gl1 = en(nc.sbuf_tensor(f"p5e_gl1{H}", [128, DCH * 128], BF16))
        uh0 = en(nc.sbuf_tensor(f"p5e_uh0{H}", [128, DCH * 128], BF16))
        uh1 = en(nc.sbuf_tensor(f"p5e_uh1{H}", [128, DCH * 128], BF16))
        ul0 = en(nc.sbuf_tensor(f"p5e_ul0{H}", [128, DCH * 128], BF16))
        ul1 = en(nc.sbuf_tensor(f"p5e_ul1{H}", [128, DCH * 128], BF16))
        df = en(nc.sbuf_tensor(f"p5e_df{H}", [128, NIM * 128], F32))
        dh0 = en(nc.sbuf_tensor(f"p5e_dh0{H}", [128, NIM * 128], BF16))
        dh1 = en(nc.sbuf_tensor(f"p5e_dh1{H}", [128, NIM * 128], BF16))
        dl0 = en(nc.sbuf_tensor(f"p5e_dl0{H}", [128, NIM * 128], BF16))
        dl1 = en(nc.sbuf_tensor(f"p5e_dl1{H}", [128, NIM * 128], BF16))
        bc = en(nc.sbuf_tensor(f"p5e_bc{H}", [128, 512], F32))
        cmbr = en(nc.sbuf_tensor(f"p5e_cr{H}", [1, 2 * 512], F32))
        tmp = en(nc.sbuf_tensor(f"p5e_tmp{H}", [128, 1024], F32))
        tm2 = en(nc.sbuf_tensor(f"p5e_tm2{H}", [128, 1024], F32))
        pg = en(nc.psum_tensor(f"p5e_pg{H}", [128, 1024], F32))
        pu = en(nc.psum_tensor(f"p5e_pu{H}", [128, 1024], F32))
        pd = en(nc.psum_tensor(f"p5e_pd{H}", [128, 1024], F32))
        pb = en(nc.psum_tensor(f"p5e_pb{H}", [128, 512], F32))
        blk = en(nc.Block())

        ghb, glb, uhb, ulb = (gh0, gh1), (gl0, gl1), (uh0, uh1), (ul0, ul1)
        dhb, dlb = (dh0, dh1), (dl0, dl1)
        lw_b, pe_b, ev_b, sc_b, lx_bE = lw.n, pe.n, ev.n, sc.n, lx.n
        PE_E = 1 + NIM + DCH
        EV_E = 1 + 2 * NIM + 2 * DCH
        SC_E = NIM
        pe_bc = lambda e: pe_b + PE_E * e + 1
        pe_gu = lambda e, i: pe_b + PE_E * e + 1 + i + 1
        pe_dn = lambda e, f: pe_b + PE_E * e + 1 + NIM + f + 1
        ev_bc = lambda e: ev_b + EV_E * e + 1
        ev_gs = lambda e, i: ev_b + EV_E * e + 1 + 2 * i + 1
        ev_he = lambda e, i: ev_b + EV_E * e + 1 + 2 * i + 2
        ev_ds = lambda e, f: ev_b + EV_E * e + 1 + 2 * NIM + 2 * f + 1
        ev_mo = lambda e, f: ev_b + EV_E * e + 1 + 2 * NIM + 2 * f + 2
        sc_si = lambda e, i: sc_b + SC_E * e + i + 1

        @blk.sync
        def _(sync):
            for e in range(E):
                if e >= 2:
                    sync.wait_ge(pe.h, pe_bc(e - 1))
                lx.inc(sync.dma_start(
                    cmbr[0:1, (e % 2) * 512:(e % 2) * 512 + 512],
                    t["csc"].ap()[e:e + 1, off:off + 512]), 16)
                for i in range(NIM):
                    gi = e * NIM + i
                    if gi >= 1:
                        pj = gi - 1
                        sync.wait_ge(ev.h, ev_gs(pj // NIM, pj % NIM))
                    src = t["guT"].ap()[e, :, i * 128:(i + 1) * 128] \
                        .rearrange("(c p) n -> p c n", p=128)
                    lw.inc(sync.dma_start(
                        gf[:].rearrange("p (c n) -> p c n", c=DCH), src), 16)
                    src = t["guT"].ap()[e, :, IM + i * 128: IM + (i + 1) * 128] \
                        .rearrange("(c p) n -> p c n", p=128)
                    lw.inc(sync.dma_start(
                        uf[:].rearrange("p (c n) -> p c n", c=DCH), src), 16)
                for f in range(DCH):
                    gi = e * DCH + f
                    if gi >= 1:
                        pj = gi - 1
                        sync.wait_ge(ev.h, ev_ds(pj // DCH, pj % DCH))
                    src = t["dpT"].ap()[e, :, f * 128:(f + 1) * 128] \
                        .rearrange("(c p) n -> p c n", p=128)
                    lw.inc(sync.dma_start(
                        df[:].rearrange("p (c n) -> p c n", c=NIM), src), 16)

        @blk.tensor
        def _(tensor):
            for e in range(E):
                if e >= 1:
                    tensor.wait_ge(ev.h, ev_he(e - 1, NIM - 1))
                tensor.wait_ge(lx.h, lx_bE + 16 * (e + 1))
                ins = tensor.matmul(pb[:, 0:512], env.onesrow[:],
                                    cmbr[0:1, (e % 2) * 512:(e % 2) * 512 + 512],
                                    start=True, stop=True)
                pe.inc(ins, 1)
                for i in range(NIM):
                    gi = e * NIM + i
                    tensor.wait_ge(ev.h, ev_gs(e, i))
                    if gi >= 2:
                        pj = gi - 2
                        tensor.wait_ge(sc.h, sc_si(pj // NIM, pj % NIM))
                        tensor.wait_ge(ev.h, ev_he(pj // NIM, pj % NIM))
                    par = i % 2
                    for d in range(DCH):
                        rh = h1h[:, d * TOK + off: d * TOK + off + 512]
                        rl = h1l[:, d * TOK + off: d * TOK + off + 512]
                        mm3(tensor, pg[:, par * 512: par * 512 + 512],
                            ghb[par][:, d * 128:(d + 1) * 128],
                            glb[par][:, d * 128:(d + 1) * 128],
                            rh, rl, start=(d == 0), stop=(d == DCH - 1))
                        ins = mm3(tensor, pu[:, par * 512: par * 512 + 512],
                                  uhb[par][:, d * 128:(d + 1) * 128],
                                  ulb[par][:, d * 128:(d + 1) * 128],
                                  rh, rl, start=(d == 0), stop=(d == DCH - 1))
                    pe.inc(ins, 1)
                for f in range(DCH):
                    gi = e * DCH + f
                    tensor.wait_ge(ev.h, ev_ds(e, f))
                    if gi >= 2:
                        pj = gi - 2
                        tensor.wait_ge(ev.h, ev_mo(pj // DCH, pj % DCH))
                    par = f % 2
                    for i in range(NIM):
                        ins = mm3(tensor, pd[:, par * 512: par * 512 + 512],
                                  dhb[par][:, i * 128:(i + 1) * 128],
                                  dlb[par][:, i * 128:(i + 1) * 128],
                                  heh[:, i * 512:(i + 1) * 512],
                                  hel[:, i * 512:(i + 1) * 512],
                                  start=(i == 0), stop=(i == NIM - 1))
                    pe.inc(ins, 1)

        @blk.scalar
        def _(scalar):
            for e in range(E):
                for i in range(NIM):
                    par = i % 2
                    gi = e * NIM + i
                    scalar.wait_ge(pe.h, pe_gu(e, i))
                    if gi >= 2:
                        pj = gi - 2
                        scalar.wait_ge(ev.h, ev_he(pj // NIM, pj % NIM))
                    sc.inc(scalar.activation(tmp[:, par * 512: par * 512 + 512],
                                             pg[:, par * 512: par * 512 + 512],
                                             AF.Silu), 1)

        @blk.vector
        def _(vector):
            for e in range(E):
                vector.wait_ge(pe.h, pe_bc(e))
                ev.inc(vector.tensor_copy(bc[:], pb[:, 0:512]), 1)
                for i in range(NIM):
                    par = i % 2
                    vector.wait_ge(lw.h, lw_b + (32 * NIM + 16 * DCH) * e + 32 * (i + 1))
                    if e * NIM + i >= 2:
                        pj = e * NIM + i - 2
                        vector.wait_ge(pe.h, pe_gu(pj // NIM, pj % NIM))
                    vector.tensor_copy(ghb[par][:], gf[:])
                    vector.tensor_copy(uhb[par][:], uf[:])
                    vector.tensor_sub(glb[par][:], gf[:], ghb[par][:])
                    ev.inc(vector.tensor_sub(ulb[par][:], uf[:], uhb[par][:]), 1)
                    vector.wait_ge(sc.h, sc_si(e, i))
                    vector.tensor_mul(tm2[:, par * 512: par * 512 + 512],
                                      tmp[:, par * 512: par * 512 + 512],
                                      bc[:])
                    vector.tensor_mul(tmp[:, par * 512: par * 512 + 512],
                                      tm2[:, par * 512: par * 512 + 512],
                                      pu[:, par * 512: par * 512 + 512])
                    vector.tensor_copy(heh[:, i * 512:(i + 1) * 512],
                                       tmp[:, par * 512: par * 512 + 512])
                    ev.inc(vector.tensor_sub(hel[:, i * 512:(i + 1) * 512],
                                             tmp[:, par * 512: par * 512 + 512],
                                             heh[:, i * 512:(i + 1) * 512]), 1)
                for f in range(DCH):
                    par = f % 2
                    vector.wait_ge(lw.h, lw_b + (32 * NIM + 16 * DCH) * e + 32 * NIM + 16 * (f + 1))
                    if e * DCH + f >= 2:
                        pj = e * DCH + f - 2
                        vector.wait_ge(pe.h, pe_dn(pj // DCH, pj % DCH))
                    vector.tensor_copy(dhb[par][:], df[:])
                    ev.inc(vector.tensor_sub(dlb[par][:], df[:], dhb[par][:]), 1)
                    vector.wait_ge(pe.h, pe_dn(e, f))
                    ev.inc(vector.tensor_add(moe[:, f * 512:(f + 1) * 512],
                                             moe[:, f * 512:(f + 1) * 512],
                                             pd[:, par * 512: par * 512 + 512]), 1)


def _p5_tail(env, moe, H):
    nc, t, s = env.nc, env.t, env.s
    h1h, h1l = env.h1h, env.h1l
    pe, ev, st = s["pe"], s["ev"], s["st"]

    with nc.Block() as blk:
        @blk.vector
        def _(vector):
            for f in range(DCH):
                vector.tensor_add(moe[:, f * 512:(f + 1) * 512],
                                  moe[:, f * 512:(f + 1) * 512],
                                  h1h[:, f * TOK + H * 512: f * TOK + H * 512 + 512])
                ins = vector.tensor_add(moe[:, f * 512:(f + 1) * 512],
                                        moe[:, f * 512:(f + 1) * 512],
                                        h1l[:, f * TOK + H * 512: f * TOK + H * 512 + 512])
            ev.inc(ins, 1)

    emit_ln(env, moe, 512, 0, 512, f"t{H}")

    with nc.Block() as blk:
        @blk.sync
        def _(sync):
            dst = t["y"].ap()[:, H * 512:(H + 1) * 512].rearrange("(c p) n -> p c n", p=128)
            st.inc(sync.dma_start(dst, moe[:].rearrange("p (c n) -> p c n", c=DCH)), 16)


def kernel(**inputs):
    x = np.asarray(inputs["L0_attn_x"], dtype=np.float32)
    wq = np.asarray(inputs["L0_attn_Wq"], dtype=np.float32)
    wk = np.asarray(inputs["L0_attn_Wk"], dtype=np.float32)
    wv = np.asarray(inputs["L0_attn_Wv"], dtype=np.float32)
    wp = np.asarray(inputs["L0_attn_Wp"], dtype=np.float32)
    rw = np.asarray(inputs["L0_ffn_router_weight"], dtype=np.float32)
    gu = np.asarray(inputs["L0_ffn_gate_up_proj"], dtype=np.float32)
    dp = np.asarray(inputs["L0_ffn_down_proj"], dtype=np.float32)
    sg = np.asarray(inputs["L0_ffn_sh_gate_w"], dtype=np.float32)
    su = np.asarray(inputs["L0_ffn_sh_up_w"], dtype=np.float32)
    sd = np.asarray(inputs["L0_ffn_sh_down_w"], dtype=np.float32)
    sgw = np.asarray(inputs["L0_ffn_shared_gate_w"], dtype=np.float32)

    xT = [np.ascontiguousarray(x[b].T) for b in range(B)]
    common = {
        "wqT": np.ascontiguousarray(wq.T), "wkT": np.ascontiguousarray(wk.T),
        "wvT": np.ascontiguousarray(wv.T), "wpT": np.ascontiguousarray(wp.T),
        "rwT": np.ascontiguousarray(rw.T),
        "guT": np.ascontiguousarray(gu.transpose(0, 2, 1)),
        "dpT": np.ascontiguousarray(dp.transpose(0, 2, 1)),
        "sgT": np.ascontiguousarray(sg.T), "suT": np.ascontiguousarray(su.T),
        "sdT": np.ascontiguousarray(sd.T),
        "sgwT": np.ascontiguousarray(sgw.reshape(DCH, 128).T),
    }
    in_maps = []
    for c in range(8):
        b, k = c // 4, c % 4
        m = dict(common)
        m["xT"] = xT[b]
        m["xTo"] = np.ascontiguousarray(xT[b][:, k * TOK:(k + 1) * TOK])
        m["qoff"] = np.full((128, 1), k * TOK, dtype=np.float32)
        in_maps.append(m)

    nc = build(dbg=_OVR.get("dbg", ()), phases=_OVR.get("phases", ("p1", "p2", "p3", "p4", "p5")))
    res = run_bass_kernel_spmd(nc, in_maps, core_ids=list(range(8)), **_OVR.get("runkw", {}))
    if "post" in _OVR:
        return _OVR["post"](res)
    out = np.empty((B, S, D), dtype=np.float32)
    for c in range(8):
        b, k = c // 4, c % 4
        out[b, k * TOK:(k + 1) * TOK, :] = res.results[c]["y"].T
    return out


# version 04
# speedup vs baseline: 1.1682x; 1.1682x over previous
import contextlib
import numpy as np
import concourse.bass as bass
import concourse.mybir as mybir
from concourse.bass_utils import run_bass_kernel_spmd

F32 = mybir.dt.float32
BF16 = mybir.dt.bfloat16
AF = mybir.ActivationFunctionType
ALU = mybir.AluOpType
AX = mybir.AxisListType

B = 2
S = 4096
D = 2048
NH = 16
TOK = 1024
E = 8
IM = 1408
ISH = 5632
DCH = D // 128
NQT = TOK // 128
ISQ = INV_SQRT_HD = 1.0 / float(np.sqrt(128.0))
LN_EPS = 1e-5

_OVR = {}


class Ctr:
    def __init__(self, nc, name):
        self.h = nc.alloc_semaphore(name=name)
        self.n = 0

    def inc(self, ins, k=1):
        ins.then_inc(self.h, k)
        self.n += k
        return self.n


class Env:
    pass


def build(dbg=(), phases=("p1", "p2", "p3", "p4", "p5")):
    nc = bass.Bass("TRN2", target_bir_lowering=False)
    env = Env()
    env.nc = nc
    t = env.t = {}
    for name, shape in (
        ("xT", [D, S]), ("xTo", [D, TOK]), ("qoff", [128, 1]),
        ("wqT", [D, D]), ("wkT", [D, D]), ("wvT", [D, D]), ("wpT", [D, D]),
        ("rwT", [D, E]), ("guT", [E, D, 2 * IM]), ("dpT", [E, IM, D]),
        ("sgT", [D, ISH]), ("suT", [D, ISH]), ("sdT", [ISH, D]), ("sgwT", [128, DCH]),
    ):
        t[name] = nc.dram_tensor(name, shape, F32, kind="ExternalInput")
    t["y"] = nc.dram_tensor("y", [D, TOK], F32, kind="ExternalOutput")
    t["ksc"] = nc.dram_tensor("ksc", [D, S], F32, kind="Internal")
    t["vsc"] = nc.dram_tensor("vsc", [S, D], F32, kind="Internal")
    t["qsc"] = nc.dram_tensor("qsc", [D, TOK], F32, kind="Internal")
    t["osc"] = nc.dram_tensor("osc", [TOK, D], F32, kind="Internal")
    t["csc"] = nc.dram_tensor("csc", [E, TOK], F32, kind="Internal")

    env.dbg = {}
    for entry in dbg:
        name, shape = entry[0], entry[1]
        dt_ = entry[2] if len(entry) > 2 else F32
        env.dbg[name] = nc.dram_tensor("dbg_" + name, shape, dt_, kind="ExternalOutput")

    env.ident = nc.alloc_sbuf_tensor("ident", [128, 128], F32)
    env.onescol = nc.alloc_sbuf_tensor("onescol", [128, 1], F32)
    env.onesrow = nc.alloc_sbuf_tensor("onesrow", [1, 128], F32)
    env.p02 = contextlib.ExitStack()
    env.kpos = env.p02.enter_context(nc.sbuf_tensor("kpos", [128, S], F32))
    env.qpos = env.p02.enter_context(nc.sbuf_tensor("qpos", [128, NQT], F32))
    env.qoff_sb = env.p02.enter_context(nc.sbuf_tensor("qoff_sb", [128, 1], F32))
    env.bias30 = env.p02.enter_context(nc.sbuf_tensor("bias30", [128, 1], F32))

    env.s = {k: Ctr(nc, k) for k in
             ("lw", "lx", "pe", "pe2", "ev", "ev2", "sc", "sp", "st", "a1", "a2", "a3")}

    with nc.named_scope("p0"):
        emit_p0(env)
    if "p1" in phases:
        with nc.named_scope("p1"):
            emit_p1(env)
    if "p2" in phases:
        with nc.named_scope("p2"):
            emit_p2(env)
    env.p02.close()
    if "p3" in phases or "p4" in phases or "p5" in phases:
        env.h1h = nc.alloc_sbuf_tensor("h1h", [128, DCH * TOK], BF16)
        env.h1l = nc.alloc_sbuf_tensor("h1l", [128, DCH * TOK], BF16)
    if "p3" in phases:
        with nc.sbuf_tensor("r1buf", [128, DCH * TOK], F32) as r1buf:
            env.r1buf = r1buf
            with nc.named_scope("p3"):
                emit_p3(env)
    if "p4" in phases:
        env.sig = nc.alloc_sbuf_tensor("sig", [1, TOK], F32)
        with nc.named_scope("p4"):
            emit_p4(env)
    if "p5" in phases:
        with nc.named_scope("p5"):
            emit_p5(env)

    with nc.Block() as blk:
        @blk.sync
        def _(sync):
            sync.wait_ge(env.s["st"].h, env.s["st"].n)
            sync.wait_ge(env.s["sp"].h, env.s["sp"].n)
    return nc


def emit_p0(env):
    nc, t = env.nc, env.t
    lx = env.s["lx"]
    a1 = env.s["a1"]
    with nc.Block() as blk:
        @blk.sync
        def _(sync):
            lx.inc(sync.dma_start(env.qoff_sb[:], t["qoff"].ap()), 16)

        @blk.gpsimd
        def _(g):
            g.memset(env.onescol[:], 1.0)
            g.memset(env.bias30[:], -30.0)
            g.memset(env.onesrow[:], 1.0)
            g.iota(env.kpos[:], [[1, S]], base=0, channel_multiplier=0,
                   allow_small_or_imprecise_dtypes=True)
            a1.inc(g.iota(env.qpos[:], [[128, NQT]], base=0, channel_multiplier=1,
                          allow_small_or_imprecise_dtypes=True), 1)
            g.memset(env.ident[:], 1.0)
            g.affine_select(env.ident[:], env.ident[:], [[1, 128]], ALU.is_equal, 0.0,
                            base=0, channel_multiplier=-1)

        @blk.vector
        def _(v):
            v.wait_ge(lx.h, lx.n)
            v.wait_ge(a1.h, a1.n)
            v.tensor_scalar_add(env.qpos[:], env.qpos[:], env.qoff_sb[:, 0:1])


def nc_ap(handle):
    return handle[:]


def emit_p1(env):
    nc, t, s = env.nc, env.t, env.s
    NTT = S // 512
    jobs = []
    for tt in range(NTT):
        jobs.append(("k", tt, "xT", t["wkT"], t["ksc"]))
        jobs.append(("v", tt, "xT", t["wvT"], t["vsc"]))
    for tq in range(TOK // 512):
        jobs.append(("q", tq, "xTo", t["wqT"], t["qsc"]))

    lw, lx, pe, pe2, ev, ev2, sp = (s[k] for k in ("lw", "lx", "pe", "pe2", "ev", "ev2", "sp"))
    pe_b, pe2_b, ev_b, ev2_b, lw_b, lx_b, sp_b = pe.n, pe2.n, ev.n, ev2.n, lw.n, lx.n, sp.n

    xloads = []
    for j in jobs:
        key = (j[2], j[1])
        if key not in xloads:
            xloads.append(key)

    pe_after = lambda j, f: pe_b + 16 * j + f + 1
    ev_after = lambda j, f: ev_b + 16 * j + f + 1
    lw_after = lambda j, f: lw_b + 16 * (16 * j + f + 1)
    lx_after = lambda i: lx_b + 16 * (i + 1)
    pe2_after = lambda vj, c: pe2_b + 64 * vj + c + 1
    ev2_after = lambda vj, c: ev2_b + 64 * vj + c + 1
    spill_counts = []
    c0 = sp_b
    for job in jobs:
        c0 += 16 * (4 if job[0] == "v" else 1)
        spill_counts.append(c0)

    with (
        nc.sbuf_tensor("p1_x0", [128, DCH * 512], F32) as x0,
        nc.sbuf_tensor("p1_x1", [128, DCH * 512], F32) as x1,
        nc.sbuf_tensor("p1_w0", [128, DCH * 128], F32) as w0,
        nc.sbuf_tensor("p1_w1", [128, DCH * 128], F32) as w1,
        nc.sbuf_tensor("p1_kst", [128, DCH * 512], F32) as kst,
        nc.sbuf_tensor("p1_vst", [128, DCH * 512], F32) as vst,
        nc.sbuf_tensor("p1_vtst", [128, 4 * D], F32) as vtst,
        nc.psum_tensor("p1_ps", [128, 1024], F32) as ps,
        nc.psum_tensor("p1_pt", [128, 1024], F32) as pt,
        nc.Block() as blk,
    ):
        xbufs, wbufs = (x0, x1), (w0, w1)

        @blk.sync
        def _(sync):
            li = 0
            for j, job in enumerate(jobs):
                kind, ts, xsrc, wT, dst = job
                idx = xloads.index((xsrc, ts))
                if idx == li:
                    if idx >= 2:
                        last_j = max(jj for jj, jb in enumerate(jobs)
                                     if (jb[2], jb[1]) == xloads[idx - 2])
                        sync.wait_ge(pe.h, pe_after(last_j, 15))
                    src = t[xsrc].ap()[:, ts * 512:(ts + 1) * 512] \
                        .rearrange("(c p) n -> p c n", p=128)
                    lx.inc(sync.dma_start(
                        xbufs[idx % 2][:].rearrange("p (c n) -> p c n", c=DCH), src), 16)
                    li += 1
                for f in range(16):
                    gi = 16 * j + f
                    if gi >= 2:
                        pj, pf = divmod(gi - 2, 16)
                        sync.wait_ge(pe.h, pe_after(pj, pf))
                    src = wT.ap()[:, f * 128:(f + 1) * 128] \
                        .rearrange("(c p) n -> p c n", p=128)
                    lw.inc(sync.dma_start(
                        wbufs[gi % 2][:].rearrange("p (c n) -> p c n", c=DCH), src), 16)

        @blk.tensor
        def _(tensor):
            vj = 0
            for j, job in enumerate(jobs):
                kind, ts, xsrc, wT, dst = job
                xi = xloads.index((xsrc, ts))
                xb = xbufs[xi % 2]
                for f in range(16):
                    gi = 16 * j + f
                    tensor.wait_ge(lw.h, lw_after(j, f))
                    if f == 0:
                        tensor.wait_ge(lx.h, lx_after(xi))
                    if gi >= 2:
                        pj, pf = divmod(gi - 2, 16)
                        tensor.wait_ge(ev.h, ev_after(pj, pf))
                    wb = wbufs[gi % 2]
                    pss = ps[:, (gi % 2) * 512:(gi % 2) * 512 + 512]
                    for d in range(DCH):
                        ins = tensor.matmul(pss, wb[:, d * 128:(d + 1) * 128],
                                            xb[:, d * 512:(d + 1) * 512],
                                            start=(d == 0), stop=(d == DCH - 1))
                    pe.inc(ins, 1)
                if kind == "v":
                    for c in range(64):
                        sblk, fc = divmod(c, 16)
                        tensor.wait_ge(ev.h, ev_after(j, fc))
                        if c >= 2:
                            tensor.wait_ge(ev2.h, ev2_after(vj, c - 2))
                        ins = tensor.transpose(
                            pt[:, (c % 2) * 512:(c % 2) * 512 + 128],
                            vst[:, fc * 512 + sblk * 128: fc * 512 + sblk * 128 + 128],
                            env.ident[:])
                        pe2.inc(ins, 1)
                    vj += 1

        @blk.vector
        def _(vector):
            vj = 0
            for j, job in enumerate(jobs):
                kind = job[0]
                stg = vst if kind == "v" else kst
                prev = None
                for jj in range(j - 1, -1, -1):
                    if (jobs[jj][0] == "v") == (kind == "v"):
                        prev = jj
                        break
                for f in range(16):
                    gi = 16 * j + f
                    vector.wait_ge(pe.h, pe_after(j, f))
                    if f == 0 and prev is not None:
                        vector.wait_ge(sp.h, spill_counts[prev])
                    pss = ps[:, (gi % 2) * 512:(gi % 2) * 512 + 512]
                    ev.inc(vector.tensor_copy(stg[:, f * 512:(f + 1) * 512], pss), 1)
                if kind == "v":
                    for c in range(64):
                        sblk, fc = divmod(c, 16)
                        vector.wait_ge(pe2.h, pe2_after(vj, c))
                        ev2.inc(vector.tensor_copy(
                            vtst[:, sblk * D + fc * 128: sblk * D + fc * 128 + 128],
                            pt[:, (c % 2) * 512:(c % 2) * 512 + 128]), 1)
                    vj += 1

        @blk.gpsimd
        def _(g):
            vj = 0
            for j, job in enumerate(jobs):
                kind, ts, xsrc, wT, dst = job
                if kind in ("k", "q"):
                    g.wait_ge(ev.h, ev_after(j, 15))
                    dstap = dst.ap()[:, ts * 512:(ts + 1) * 512] \
                        .rearrange("(c p) n -> p c n", p=128)
                    sp.inc(g.dma_start(dstap, kst[:].rearrange("p (c n) -> p c n", c=DCH)), 16)
                else:
                    for sblk in range(4):
                        g.wait_ge(ev2.h, ev2_after(vj, 16 * (sblk + 1) - 1))
                        row = ts * 512 + sblk * 128
                        sp.inc(g.dma_start(dst.ap()[row:row + 128, :],
                                           vtst[:, sblk * D:(sblk + 1) * D]), 16)
                    vj += 1

    dbg_dump(env, [("ksc", t["ksc"], None), ("vsc", t["vsc"], None), ("qsc", t["qsc"], None)],
             wait=("sp",))


def dbg_dump(env, items, wait=()):
    nc, s = env.nc, env.s
    items = [(n, src, ap) for (n, src, ap) in items if n in env.dbg]
    if not items:
        return
    with nc.Block() as blk:
        @blk.sync
        def _(sync):
            for w in wait:
                sync.wait_ge(s[w].h, s[w].n)
            for name, src, ap in items:
                srcap = src.ap() if hasattr(src, "ap") else (ap if ap is not None else src[:])
                s["st"].inc(sync.dma_start(env.dbg[name].ap(), srcap), 16)


def emit_p2(env):
    nc, t, s = env.nc, env.t, env.s
    lx, pe, ev, sc, sp = (s[k] for k in ("lx", "pe", "ev", "sc", "sp"))
    lx_b, pe_b, ev_b, sc_b, sp_b = lx.n, pe.n, ev.n, sc.n, sp.n

    PE_PP = 41
    EV_PP = 42

    def pe_qk(p, i):
        return pe_b + PE_PP * p + i + 1

    def pe_tr(p, c):
        return pe_b + PE_PP * p + 8 + c + 1

    def pe_av(p):
        return pe_b + PE_PP * p + PE_PP

    def ev_stt(p, i):
        return ev_b + EV_PP * p + i + 1

    def ev_rc(p):
        return ev_b + EV_PP * p + 9

    def ev_et(p, c):
        return ev_b + EV_PP * p + 10 + c

    def ev_o(p):
        return ev_b + EV_PP * p + EV_PP

    def sc_exp(p, i):
        return sc_b + 8 * p + i + 1

    with (
        nc.sbuf_tensor("p2_k0", [128, S], F32) as k0,
        nc.sbuf_tensor("p2_k1", [128, S], F32) as k1,
        nc.sbuf_tensor("p2_v0", [128, S], F32) as v0,
        nc.sbuf_tensor("p2_v1", [128, S], F32) as v1,
        nc.sbuf_tensor("p2_q0", [128, TOK], F32) as q0,
        nc.sbuf_tensor("p2_q1", [128, TOK], F32) as q1,
        nc.sbuf_tensor("p2_e", [128, S], F32) as e_sb,
        nc.sbuf_tensor("p2_em", [128, S], F32) as e_m,
        nc.sbuf_tensor("p2_et", [128, 256], F32) as et_sb,
        nc.sbuf_tensor("p2_o", [128, 256], F32) as o_st,
        nc.sbuf_tensor("p2_bm", [128, 64], F32) as bm,
        nc.sbuf_tensor("p2_ac", [128, 64], F32) as acc,
        nc.sbuf_tensor("p2_nm", [128, 8], F32) as nm,
        nc.sbuf_tensor("p2_bi", [128, 8], F32) as bias,
        nc.sbuf_tensor("p2_ll", [128, 8], F32) as ll,
        nc.sbuf_tensor("p2_rr", [128, 8], F32) as rr,
        nc.psum_tensor("p2_ps", [128, 4096], F32) as pss,
        nc.Block() as blk,
    ):
        kb_, vb_, qb_ = (k0, k1), (v0, v1), (q0, q1)

        @blk.sync
        def _(sync):
            sync.wait_ge(sp.h, sp_b)
            for h in range(NH):
                if h >= 2:
                    sync.wait_ge(pe.h, pe_av((h - 1) * NQT - 1))
                lx.inc(sync.dma_start(kb_[h % 2][:],
                                      t["ksc"].ap()[h * 128:(h + 1) * 128, :]), 16)
                src = t["vsc"].ap()[:, h * 128:(h + 1) * 128] \
                    .rearrange("(c p) n -> p c n", p=128)
                lx.inc(sync.dma_start(
                    vb_[h % 2][:].rearrange("p (c n) -> p c n", c=S // 128), src), 16)
                lx.inc(sync.dma_start(qb_[h % 2][:],
                                      t["qsc"].ap()[h * 128:(h + 1) * 128, :]), 16)

        @blk.tensor
        def _(tensor):
            for h in range(NH):
                kb, vb, qb = kb_[h % 2], vb_[h % 2], qb_[h % 2]
                for qt in range(NQT):
                    p = h * NQT + qt
                    if qt == 0:
                        tensor.wait_ge(lx.h, lx_b + 48 * (h + 1))
                    if p > 0:
                        tensor.wait_ge(ev.h, ev_o(p - 1))
                        tensor.wait_ge(sc.h, sc_exp(p - 1, 7))
                    for kbi in range(8):
                        pe.inc(tensor.matmul(pss[:, kbi * 512:(kbi + 1) * 512],
                                             qb[:, qt * 128:(qt + 1) * 128],
                                             kb[:, kbi * 512:(kbi + 1) * 512],
                                             start=True, stop=True), 1)
                    for c in range(32):
                        if c < 4:
                            tensor.wait_ge(sc.h, sc_exp(p, c))
                        tensor.wait_ge(ev.h, ev_stt(p, c // 4))
                        if c >= 4:
                            tensor.wait_ge(ev.h, ev_et(p, c - 4))
                        ins = tensor.transpose(
                            pss[:, (c % 4) * 512:(c % 4) * 512 + 128],
                            e_m[:, c * 128:(c + 1) * 128], env.ident[:])
                        pe.inc(ins, 1)
                        tensor.wait_ge(ev.h, ev_et(p, c))
                        if c == 0:
                            tensor.wait_ge(sc.h, sc_exp(p, 4))
                        ins = tensor.matmul(pss[:, 4 * 512: 4 * 512 + 128],
                                            et_sb[:, (c % 2) * 128:(c % 2) * 128 + 128],
                                            vb[:, c * 128:(c + 1) * 128],
                                            start=(c == 0), stop=(c == 31))
                        if c == 31:
                            pe.inc(ins, 1)

        @blk.vector
        def _(vector):
            NOPS = 6
            for h in range(NH):
                for qt in range(NQT):
                    p = h * NQT + qt
                    for i in range(8):
                        vector.wait_ge(sc.h, sc_exp(p, i))
                        ev.inc(vector.scalar_tensor_tensor(
                            e_m[:, i * 512:(i + 1) * 512],
                            env.kpos[:, i * 512:(i + 1) * 512],
                            env.qpos[:, qt:qt + 1], e_sb[:, i * 512:(i + 1) * 512],
                            ALU.is_le, ALU.mult, accum_out=acc[:, 8 * i:8 * i + 1]), 1)
                    for _ in range(NOPS):
                        vector.engine_nop()
                    vector.reduce_sum(ll[:, 0:1], acc[:, 0:57:8], axis=AX.X)
                    for _ in range(NOPS):
                        vector.engine_nop()
                    ev.inc(vector.reciprocal(rr[:, 0:1], ll[:, 0:1]), 1)
                    for c in range(32):
                        vector.wait_ge(pe.h, pe_tr(p, c))
                        ev.inc(vector.tensor_copy(
                            et_sb[:, (c % 2) * 128:(c % 2) * 128 + 128],
                            pss[:, (c % 4) * 512:(c % 4) * 512 + 128]), 1)
                    vector.wait_ge(pe.h, pe_av(p))
                    if p >= 2:
                        vector.wait_ge(sp.h, sp_b + 16 * (p - 1))
                    ev.inc(vector.tensor_scalar_mul(
                        o_st[:, (p % 2) * 128:(p % 2) * 128 + 128],
                        pss[:, 4 * 512: 4 * 512 + 128], rr[:, 0:1]), 1)

        @blk.scalar
        def _(scalar):
            for h in range(NH):
                for qt in range(NQT):
                    p = h * NQT + qt
                    for i in range(8):
                        scalar.wait_ge(pe.h, pe_qk(p, i))
                        sc.inc(scalar.activation(e_sb[:, i * 512:(i + 1) * 512],
                                                 pss[:, i * 512:(i + 1) * 512], AF.Exp,
                                                 bias=env.bias30[:, 0:1], scale=ISQ), 1)

        @blk.gpsimd
        def _(g):
            for h in range(NH):
                for qt in range(NQT):
                    p = h * NQT + qt
                    g.wait_ge(ev.h, ev_o(p))
                    sp.inc(g.dma_start(
                        t["osc"].ap()[qt * 128:(qt + 1) * 128, h * 128:(h + 1) * 128],
                        o_st[:, (p % 2) * 128:(p % 2) * 128 + 128]), 16)

    dbg_dump(env, [("osc", t["osc"], None)], wait=("sp",))


def emit_ln(env, buf, stride, off, width, tag="", out_pair=None):
    nc, s = env.nc, env.s
    pe, ev, sc = s["pe"], s["ev"], s["sc"]
    NHW = width // 512
    assert width % 512 == 0 and NHW in (1, 2)
    pe_b, ev_b, sc_b = pe.n, ev.n, sc.n
    NOPS = 6

    with (
        nc.sbuf_tensor(f"ln_sq{tag}", [128, 2 * width], F32) as sq,
        nc.sbuf_tensor(f"ln_srow{tag}", [1, 2 * width], F32) as srow,
        nc.sbuf_tensor(f"ln_inv{tag}", [1, 2 * width], F32) as invr,
        nc.psum_tensor(f"ln_ps{tag}", [128, 2 * width], F32) as pln,
        nc.psum_tensor(f"ln_bc{tag}", [128, 2 * width], F32) as pbc,
        nc.Block() as blk,
    ):
        def sl(d):
            return buf[:, d * stride + off: d * stride + off + width]

        @blk.scalar
        def _(scalar):
            for d in range(DCH):
                if d >= 2:
                    scalar.wait_ge(pe.h, pe_b + d - 1)
                sc.inc(scalar.activation(sq[:, (d % 2) * width:(d % 2) * width + width],
                                         sl(d), AF.Square), 1)
            scalar.wait_ge(ev.h, ev_b + 2)
            sc.inc(scalar.activation(invr[0:1, width:2 * width], srow[0:1, width:2 * width],
                                     AF.Sqrt, bias=0.0, scale=1.0), 1)

        @blk.tensor
        def _(tensor):
            for d in range(DCH):
                tensor.wait_ge(sc.h, sc_b + d + 1)
                for hw in range(NHW):
                    tensor.matmul(pln[0:1, hw * 512:(hw + 1) * 512], env.onescol[:],
                                  sl(d)[:, hw * 512:(hw + 1) * 512],
                                  start=(d == 0), stop=(d == DCH - 1))
                    ins = tensor.matmul(
                        pln[0:1, width + hw * 512: width + (hw + 1) * 512],
                        env.onescol[:],
                        sq[:, (d % 2) * width + hw * 512:(d % 2) * width + (hw + 1) * 512],
                        start=(d == 0), stop=(d == DCH - 1))
                pe.inc(ins, 1)
            tensor.wait_ge(ev.h, ev_b + 4)
            for hw in range(NHW):
                tensor.matmul(pbc[:, hw * 512:(hw + 1) * 512], env.onesrow[:],
                              invr[0:1, hw * 512:(hw + 1) * 512], start=True, stop=True)
                ins = tensor.matmul(pbc[:, width + hw * 512: width + (hw + 1) * 512],
                                    env.onesrow[:],
                                    srow[0:1, width + hw * 512: width + (hw + 1) * 512],
                                    start=True, stop=True)
            pe.inc(ins, 1)

        @blk.vector
        def _(vector):
            vector.wait_ge(pe.h, pe_b + DCH)
            ev.inc(vector.tensor_scalar_mul(srow[0:1, 0:2 * width],
                                            pln[0:1, 0:2 * width], 1.0 / D), 1)
            for _ in range(NOPS):
                vector.engine_nop()
            vector.tensor_mul(invr[0:1, 0:width], srow[0:1, 0:width], srow[0:1, 0:width])
            for _ in range(NOPS):
                vector.engine_nop()
            vector.tensor_sub(invr[0:1, width:2 * width],
                              srow[0:1, width:2 * width], invr[0:1, 0:width])
            for _ in range(NOPS):
                vector.engine_nop()
            ev.inc(vector.tensor_scalar_add(srow[0:1, width:2 * width],
                                            invr[0:1, width:2 * width], LN_EPS), 1)
            vector.wait_ge(sc.h, sc_b + DCH + 1)
            vector.reciprocal(invr[0:1, 0:width], invr[0:1, width:2 * width])
            for _ in range(NOPS):
                vector.engine_nop()
            ev.inc(vector.engine_nop(), 1)
            ev.inc(vector.tensor_mul(srow[0:1, width:2 * width], srow[0:1, 0:width],
                                     invr[0:1, 0:width]), 1)
            vector.wait_ge(pe.h, pe_b + DCH + 1)
            for d in range(DCH):
                vector.tensor_mul(sq[:, 0:width], sl(d), pbc[:, 0:width])
                if out_pair is None:
                    ins = vector.tensor_sub(sl(d), sq[:, 0:width], pbc[:, width:2 * width])
                else:
                    hi, lo, ostride = out_pair
                    osl_h = hi[:, d * ostride + off: d * ostride + off + width]
                    osl_l = lo[:, d * ostride + off: d * ostride + off + width]
                    vector.tensor_sub(sq[:, width:2 * width], sq[:, 0:width],
                                      pbc[:, width:2 * width])
                    vector.tensor_copy(osl_h, sq[:, width:2 * width])
                    ins = vector.tensor_sub(osl_l, sq[:, width:2 * width], osl_h)
            ev.inc(ins, 1)


def emit_wsplit(env, vector, wf, wh, wl, ncols):
    vector.tensor_copy(wh[:, 0:ncols], wf[:, 0:ncols])
    return vector.tensor_sub(wl[:, 0:ncols], wf[:, 0:ncols], wh[:, 0:ncols])


def mm3(tensor, out, lh, ll, rh, rl, start, stop):
    tensor.matmul(out, lh, rh, start=start, stop=False)
    tensor.matmul(out, lh, rl, start=False, stop=False)
    return tensor.matmul(out, ll, rh, start=False, stop=stop)


def emit_p3(env):
    nc, t, s = env.nc, env.t, env.s
    lw, lx, pe, pe2, ev, ev2, sp, sc, a2 = (s[k] for k in
        ("lw", "lx", "pe", "pe2", "ev", "ev2", "sp", "sc", "a2"))
    lw_b, lx_b, pe_b, pe2_b, ev_b, ev2_b, sc_b, a2_b = \
        lw.n, lx.n, pe.n, pe2.n, ev.n, ev2.n, sc.n, a2.n

    pe_pf = lambda tt, f: pe_b + 16 * tt + f + 1
    ev_r1 = lambda tt, f: ev_b + 16 * tt + f + 1
    pe2_tr = lambda tt, c: pe2_b + 64 * tt + c + 1
    ev2_tr = lambda tt, c: ev2_b + 128 * tt + 2 * (c + 1)
    lw_after = lambda gi: lw_b + 16 * (gi + 1)
    lx_after = lambda i: lx_b + 16 * (i + 1)

    r1 = env.r1buf

    with (
        nc.sbuf_tensor("p3_ot", [128, 2 * D], F32) as ot,
        nc.sbuf_tensor("p3_afh", [128, DCH * 512], BF16) as afh,
        nc.sbuf_tensor("p3_afl", [128, DCH * 512], BF16) as afl,
        nc.sbuf_tensor("p3_wf", [128, DCH * 128], F32) as wf,
        nc.sbuf_tensor("p3_wh0", [128, DCH * 128], BF16) as wh0,
        nc.sbuf_tensor("p3_wh1", [128, DCH * 128], BF16) as wh1,
        nc.sbuf_tensor("p3_wl0", [128, DCH * 128], BF16) as wl0,
        nc.sbuf_tensor("p3_wl1", [128, DCH * 128], BF16) as wl1,
        nc.sbuf_tensor("p3_xr", [128, 1024], F32) as xr,
        nc.psum_tensor("p3_ps", [128, 1024], F32) as ps,
        nc.psum_tensor("p3_pt", [128, 2048], F32) as pt,
        nc.Block() as blk,
    ):
        whb, wlb = (wh0, wh1), (wl0, wl1)

        @blk.sync
        def _(sync):
            for tt in range(2):
                sync.wait_ge(sp.h, sp.n)
                if tt == 1:
                    sync.wait_ge(pe2.h, pe2_tr(0, 63))
                for cch in range(4):
                    if cch >= 2:
                        sync.wait_ge(pe2.h, pe2_tr(tt, 16 * (cch - 1) - 1))
                    lx.inc(sync.dma_start(
                        ot[:, (cch % 2) * D:(cch % 2) * D + D],
                        t["osc"].ap()[tt * 512 + cch * 128: tt * 512 + (cch + 1) * 128, :]), 16)
                for f in range(16):
                    gi = 16 * tt + f
                    if gi >= 1:
                        sync.wait_ge(a2.h, a2_b + gi)
                    src = t["wpT"].ap()[:, f * 128:(f + 1) * 128] \
                        .rearrange("(c p) n -> p c n", p=128)
                    lw.inc(sync.dma_start(
                        wf[:].rearrange("p (c n) -> p c n", c=DCH), src), 16)
                    if gi >= 2:
                        pj, pf = divmod(gi - 2, 16)
                        sync.wait_ge(ev.h, ev_r1(pj, pf))
                    src2 = t["xTo"].ap()[f * 128:(f + 1) * 128, tt * 512:(tt + 1) * 512]
                    lx.inc(sync.dma_start(xr[:, (gi % 2) * 512:(gi % 2) * 512 + 512], src2), 16)

        @blk.scalar
        def _(scalar):
            for tt in range(2):
                for f in range(16):
                    gi = 16 * tt + f
                    scalar.wait_ge(lw.h, lw_after(gi))
                    if gi >= 2:
                        pj, pf = divmod(gi - 2, 16)
                        scalar.wait_ge(pe.h, pe_pf(pj, pf))
                    sc.inc(scalar.copy(whb[gi % 2][:], wf[:]), 1)

        @blk.tensor
        def _(tensor):
            for tt in range(2):
                for c in range(64):
                    sblk, fc = divmod(c, 16)
                    tensor.wait_ge(lx.h, lx_after(tt * 20 + sblk))
                    if c >= 4:
                        tensor.wait_ge(ev2.h, ev2_tr(tt, c - 4))
                    ins = tensor.transpose(pt[:, (c % 4) * 512:(c % 4) * 512 + 128],
                                           ot[:, (sblk % 2) * D + fc * 128: (sblk % 2) * D + fc * 128 + 128],
                                           env.ident[:])
                    pe2.inc(ins, 1)
                for f in range(16):
                    gi = 16 * tt + f
                    if f == 0:
                        tensor.wait_ge(ev2.h, ev2_tr(tt, 63))
                    tensor.wait_ge(a2.h, a2_b + gi + 1)
                    if gi >= 2:
                        pj, pf = divmod(gi - 2, 16)
                        tensor.wait_ge(ev.h, ev_r1(pj, pf))
                    par = gi % 2
                    for d in range(DCH):
                        ins = mm3(tensor, ps[:, par * 512: par * 512 + 512],
                                  whb[par][:, d * 128:(d + 1) * 128],
                                  wlb[par][:, d * 128:(d + 1) * 128],
                                  afh[:, d * 512:(d + 1) * 512],
                                  afl[:, d * 512:(d + 1) * 512],
                                  start=(d == 0), stop=(d == DCH - 1))
                    pe.inc(ins, 1)

        @blk.vector
        def _(vector):
            for tt in range(2):
                for c in range(64):
                    sblk, fc = divmod(c, 16)
                    vector.wait_ge(pe2.h, pe2_tr(tt, c))
                    if tt == 1 and c == 0:
                        vector.wait_ge(pe.h, pe_pf(0, 15))
                    dsth = afh[:, fc * 512 + sblk * 128: fc * 512 + sblk * 128 + 128]
                    dstl = afl[:, fc * 512 + sblk * 128: fc * 512 + sblk * 128 + 128]
                    pslot = pt[:, (c % 4) * 512:(c % 4) * 512 + 128]
                    ev2.inc(vector.tensor_copy(dsth, pslot), 1)
                    ev2.inc(vector.tensor_sub(dstl, pslot, dsth), 1)
                for f in range(16):
                    gi = 16 * tt + f
                    vector.wait_ge(sc.h, sc_b + gi + 1)
                    if gi >= 2:
                        pj, pf = divmod(gi - 2, 16)
                        vector.wait_ge(pe.h, pe_pf(pj, pf))
                    a2.inc(vector.tensor_sub(wlb[gi % 2][:], wf[:], whb[gi % 2][:]), 1)
                    vector.wait_ge(pe.h, pe_pf(tt, f))
                    vector.wait_ge(lx.h, lx_after(tt * 20 + 4 + f))
                    ev.inc(vector.tensor_add(
                        r1[:, f * TOK + tt * 512: f * TOK + tt * 512 + 512],
                        ps[:, (gi % 2) * 512:(gi % 2) * 512 + 512],
                        xr[:, (gi % 2) * 512:(gi % 2) * 512 + 512]), 1)

    emit_ln(env, r1, TOK, 0, 512, "a", out_pair=(env.h1h, env.h1l, TOK))
    emit_ln(env, r1, TOK, 512, 512, "b", out_pair=(env.h1h, env.h1l, TOK))
    if "h1" in env.dbg:
        with nc.Block() as blk:
            @blk.sync
            def _(sync):
                for c2 in range(DCH):
                    s["st"].inc(sync.dma_start(
                        env.dbg["h1"].ap()[c2 * 128:(c2 + 1) * 128, 0:TOK],
                        env.h1h[:, c2 * TOK:(c2 + 1) * TOK]), 16)
                    s["st"].inc(sync.dma_start(
                        env.dbg["h1"].ap()[c2 * 128:(c2 + 1) * 128, TOK:2 * TOK],
                        env.h1l[:, c2 * TOK:(c2 + 1) * TOK]), 16)


def emit_p4(env):
    nc, t, s = env.nc, env.t, env.s
    h1h, h1l = env.h1h, env.h1l
    lw, pe, ev, sc, sp = (s[k] for k in ("lw", "pe", "ev", "sc", "sp"))
    NOPS = 6

    with contextlib.ExitStack() as stack:
        en = stack.enter_context
        rwf = en(nc.sbuf_tensor("p4_rwf", [128, DCH * E], F32))
        rwh = en(nc.sbuf_tensor("p4_rwh", [128, DCH * E], BF16))
        rwl = en(nc.sbuf_tensor("p4_rwl", [128, DCH * E], BF16))
        zwf = en(nc.sbuf_tensor("p4_zwf", [128, DCH], F32))
        zwh = en(nc.sbuf_tensor("p4_zwh", [128, DCH], BF16))
        zwl = en(nc.sbuf_tensor("p4_zwl", [128, DCH], BF16))
        lg = en(nc.sbuf_tensor("p4_lg", [E, TOK], F32))
        lt = en(nc.sbuf_tensor("p4_lt", [128, 8], F32))
        er = en(nc.sbuf_tensor("p4_er", [128, 8], F32))
        pr = en(nc.sbuf_tensor("p4_pr", [128, 8], F32))
        eq1 = en(nc.sbuf_tensor("p4_e1", [128, 8], F32))
        t1 = en(nc.sbuf_tensor("p4_t1", [128, 8], F32))
        pr2 = en(nc.sbuf_tensor("p4_p2", [128, 8], F32))
        eq2 = en(nc.sbuf_tensor("p4_e2", [128, 8], F32))
        t2 = en(nc.sbuf_tensor("p4_t2", [128, 8], F32))
        cm = en(nc.sbuf_tensor("p4_cm", [128, 8], F32))
        cmb = en(nc.sbuf_tensor("p4_cmb", [E, TOK], F32))
        s_nm = en(nc.sbuf_tensor("p4_s0", [128, 8], F32))
        s_sum = en(nc.sbuf_tensor("p4_s1", [128, 8], F32))
        s_rs = en(nc.sbuf_tensor("p4_s2", [128, 8], F32))
        s_m1 = en(nc.sbuf_tensor("p4_s3", [128, 8], F32))
        s_m2 = en(nc.sbuf_tensor("p4_s4", [128, 8], F32))
        s_w = en(nc.sbuf_tensor("p4_s5", [128, 8], F32))
        s_wr = en(nc.sbuf_tensor("p4_s6", [128, 8], F32))
        pl = en(nc.psum_tensor("p4_pl", [128, 1024], F32))
        pz = en(nc.psum_tensor("p4_pz", [128, 1024], F32))
        pt = en(nc.psum_tensor("p4_pt", [128, 1024], F32))
        blk = en(nc.Block())

        lw_b, pe_b, ev_b, sc_b, sp_b = lw.n, pe.n, ev.n, sc.n, sp.n
        pe_t = lambda i, k: pe_b + 2 + 2 * i + k
        ev_t = lambda i, k: ev_b + 2 + 3 * i + k
        sc_t = lambda i: sc_b + 1 + i + 1

        @blk.sync
        def _(sync):
            src2 = t["rwT"].ap()[:, :].rearrange("(c p) n -> p c n", p=128)
            lw.inc(sync.dma_start(rwf[:].rearrange("p (c n) -> p c n", c=DCH), src2), 16)
            lw.inc(sync.dma_start(zwf[:], t["sgwT"].ap()), 16)

        @blk.vector
        def _(vector):
            vector.wait_ge(lw.h, lw_b + 32)
            vector.tensor_copy(rwh[:], rwf[:])
            vector.tensor_copy(zwh[:], zwf[:])
            for _ in range(NOPS):
                vector.engine_nop()
            vector.tensor_sub(rwl[:], rwf[:], rwh[:])
            ev.inc(vector.tensor_sub(zwl[:], zwf[:], zwh[:]), 1)
            vector.wait_ge(pe.h, pe_b + 2)
            vector.tensor_copy(lg[0:E, 0:512], pl[0:E, 0:512])
            ev.inc(vector.tensor_copy(lg[0:E, 512:1024], pl[0:E, 512:1024]), 1)
            for i in range(NQT):
                par = i % 2
                vector.wait_ge(pe.h, pe_t(i, 1))
                vector.tensor_copy(lt[:, 0:E], pt[:, par * 512: par * 512 + E])
                for _ in range(NOPS):
                    vector.engine_nop()
                ev.inc(vector.reduce_max(s_nm[:, 0:1], lt[:, 0:E], axis=AX.X, negate=True), 1)
                vector.wait_ge(sc.h, sc_t(i))
                vector.reciprocal(s_rs[:, 0:1], s_sum[:, 0:1])
                for _ in range(NOPS):
                    vector.engine_nop()
                vector.tensor_scalar_mul(pr[:, 0:E], er[:, 0:E], s_rs[:, 0:1])
                for _ in range(NOPS):
                    vector.engine_nop()
                vector.reduce_max(s_m1[:, 0:1], pr[:, 0:E], axis=AX.X)
                for _ in range(NOPS):
                    vector.engine_nop()
                vector.tensor_scalar(eq1[:, 0:E], pr[:, 0:E], s_m1[:, 0:1], None, ALU.is_equal)
                for _ in range(NOPS):
                    vector.engine_nop()
                vector.tensor_mul(t1[:, 0:E], pr[:, 0:E], eq1[:, 0:E])
                for _ in range(NOPS):
                    vector.engine_nop()
                vector.tensor_sub(pr2[:, 0:E], pr[:, 0:E], t1[:, 0:E])
                for _ in range(NOPS):
                    vector.engine_nop()
                vector.reduce_max(s_m2[:, 0:1], pr2[:, 0:E], axis=AX.X)
                for _ in range(NOPS):
                    vector.engine_nop()
                vector.tensor_scalar(eq2[:, 0:E], pr2[:, 0:E], s_m2[:, 0:1], None, ALU.is_equal)
                for _ in range(NOPS):
                    vector.engine_nop()
                vector.tensor_add(t1[:, 0:E], eq1[:, 0:E], eq2[:, 0:E])
                vector.tensor_add(s_w[:, 0:1], s_m1[:, 0:1], s_m2[:, 0:1])
                for _ in range(NOPS):
                    vector.engine_nop()
                vector.reciprocal(s_wr[:, 0:1], s_w[:, 0:1])
                vector.tensor_mul(t2[:, 0:E], pr[:, 0:E], t1[:, 0:E])
                for _ in range(NOPS):
                    vector.engine_nop()
                ev.inc(vector.tensor_scalar_mul(cm[:, 0:E], t2[:, 0:E], s_wr[:, 0:1]), 1)
                vector.wait_ge(pe.h, pe_t(i, 2))
                ev.inc(vector.tensor_copy(cmb[0:E, i * 128:(i + 1) * 128],
                                          pt[0:E, par * 512 + 128: par * 512 + 256]), 1)

        @blk.tensor
        def _(tensor):
            tensor.wait_ge(ev.h, ev_b + 1)
            for hw in range(2):
                for d in range(DCH):
                    rh = h1h[:, d * TOK + hw * 512: d * TOK + hw * 512 + 512]
                    rl = h1l[:, d * TOK + hw * 512: d * TOK + hw * 512 + 512]
                    mm3(tensor, pl[0:E, hw * 512:(hw + 1) * 512],
                        rwh[:, d * E:(d + 1) * E], rwl[:, d * E:(d + 1) * E],
                        rh, rl, start=(d == 0), stop=(d == DCH - 1))
                    ins = mm3(tensor, pz[0:1, hw * 512:(hw + 1) * 512],
                              zwh[:, d:d + 1], zwl[:, d:d + 1],
                              rh, rl, start=(d == 0), stop=(d == DCH - 1))
                pe.inc(ins, 1)
            for i in range(NQT):
                par = i % 2
                tensor.wait_ge(ev.h, ev_b + 2)
                if i >= 2:
                    tensor.wait_ge(ev.h, ev_t(i - 2, 3))
                ins = tensor.transpose(pt[:, par * 512: par * 512 + E],
                                       lg[0:E, i * 128:(i + 1) * 128],
                                       env.ident[0:E, 0:E])
                pe.inc(ins, 1)
                tensor.wait_ge(ev.h, ev_t(i, 2))
                ins = tensor.transpose(pt[0:E, par * 512 + 128: par * 512 + 256],
                                       cm[:, 0:E], env.ident[:])
                pe.inc(ins, 1)

        @blk.scalar
        def _(scalar):
            scalar.wait_ge(pe.h, pe_b + 2)
            scalar.wait_ge(ev.h, ev_b + 2)
            scalar.activation(env.sig[0:1, 0:512], pz[0:1, 0:512], AF.Sigmoid)
            sc.inc(scalar.activation(env.sig[0:1, 512:1024], pz[0:1, 512:1024], AF.Sigmoid), 1)
            for i in range(NQT):
                scalar.wait_ge(ev.h, ev_t(i, 1))
                sc.inc(scalar.activation(er[:, 0:E], lt[:, 0:E], AF.Exp,
                                         bias=s_nm[:, 0:1], scale=1.0,
                                         accum_out=s_sum[:, 0:1]), 1)

        @blk.gpsimd
        def _(g):
            g.wait_ge(ev.h, ev_t(NQT - 1, 3))
            sp.inc(g.dma_start(t["csc"].ap(), cmb[0:E, :]), 16)


def emit_p5(env):
    nc, t, s = env.nc, env.t, env.s
    lx, pe, ev, sp = (s[k] for k in ("lx", "pe", "ev", "sp"))

    moe = nc.alloc_sbuf_tensor("p5_moe", [128, DCH * 512], F32)
    sig_bc = nc.alloc_sbuf_tensor("p5_sigbc", [128, TOK], F32)

    with (
        nc.psum_tensor("p5_pb", [128, 1024], F32) as pb,
        nc.Block() as blk,
    ):
        pe_b, ev_b = pe.n, ev.n

        @blk.sync
        def _(sync):
            sync.wait_ge(sp.h, sp.n)

        @blk.tensor
        def _(tensor):
            for hw in range(2):
                ins = tensor.matmul(pb[:, hw * 512:(hw + 1) * 512], env.onesrow[:],
                                    env.sig[0:1, hw * 512:(hw + 1) * 512],
                                    start=True, stop=True)
            pe.inc(ins, 1)

        @blk.vector
        def _(vector):
            vector.wait_ge(pe.h, pe_b + 1)
            for hw in range(2):
                ev.inc(vector.tensor_copy(sig_bc[:, hw * 512:(hw + 1) * 512],
                                          pb[:, hw * 512:(hw + 1) * 512]), 1)

    for H in range(2):
        _p5_shared(env, moe, sig_bc, H)
        _p5_experts(env, moe, H)
        _p5_tail(env, moe, H)


def _p5_shared(env, moe, sig_bc, H):
    nc, t, s = env.nc, env.t, env.s
    h1h, h1l = env.h1h, env.h1l
    lw, pe, ev, sc = (s[k] for k in ("lw", "pe", "ev", "sc"))
    NSH = ISH // 128

    for q in range(2):
        off = H * 512 + q * 256
        with contextlib.ExitStack() as ostack:
            acth = ostack.enter_context(nc.sbuf_tensor(f"p5s_ah{H}{q}", [128, NSH * 256], BF16))
            actl = ostack.enter_context(nc.sbuf_tensor(f"p5s_al{H}{q}", [128, NSH * 256], BF16))
            stack = ostack.enter_context(contextlib.ExitStack())
            en = stack.enter_context
            gf = en(nc.sbuf_tensor(f"p5s_gf{H}{q}", [128, DCH * 128], F32))
            uf = en(nc.sbuf_tensor(f"p5s_uf{H}{q}", [128, DCH * 128], F32))
            gh0 = en(nc.sbuf_tensor(f"p5s_gh0{H}{q}", [128, DCH * 128], BF16))
            gh1 = en(nc.sbuf_tensor(f"p5s_gh1{H}{q}", [128, DCH * 128], BF16))
            gl0 = en(nc.sbuf_tensor(f"p5s_gl0{H}{q}", [128, DCH * 128], BF16))
            gl1 = en(nc.sbuf_tensor(f"p5s_gl1{H}{q}", [128, DCH * 128], BF16))
            uh0 = en(nc.sbuf_tensor(f"p5s_uh0{H}{q}", [128, DCH * 128], BF16))
            uh1 = en(nc.sbuf_tensor(f"p5s_uh1{H}{q}", [128, DCH * 128], BF16))
            ul0 = en(nc.sbuf_tensor(f"p5s_ul0{H}{q}", [128, DCH * 128], BF16))
            ul1 = en(nc.sbuf_tensor(f"p5s_ul1{H}{q}", [128, DCH * 128], BF16))
            tmp = en(nc.sbuf_tensor(f"p5s_tmp{H}{q}", [128, 512], F32))
            tm2 = en(nc.sbuf_tensor(f"p5s_tm2{H}{q}", [128, 512], F32))
            pg = en(nc.psum_tensor(f"p5s_pg{H}{q}", [128, 1024], F32))
            pu = en(nc.psum_tensor(f"p5s_pu{H}{q}", [128, 1024], F32))
            blk = en(nc.Block())
            ghb, glb, uhb, ulb = (gh0, gh1), (gl0, gl1), (uh0, uh1), (ul0, ul1)
            a3 = s["a3"]
            lw_b, pe_b, ev_b, sc_b, a3_b = lw.n, pe.n, ev.n, sc.n, a3.n
            order = [("sp", 0), ("sp", 1)]
            for i in range(1, NSH):
                if i + 1 < NSH:
                    order.append(("sp", i + 1))
                order.append(("ac", i - 1))
            order.append(("ac", NSH - 1))
            pos = {op: k + 1 for k, op in enumerate(order)}
            ev_sp = lambda i: ev_b + pos[("sp", i)]
            ev_ac = lambda i: ev_b + pos[("ac", i)]

            @blk.sync
            def _(sync):
                for i in range(NSH):
                    if i >= 1:
                        sync.wait_ge(ev.h, ev_sp(i - 1))
                    src = t["sgT"].ap()[:, i * 128:(i + 1) * 128] \
                        .rearrange("(c p) n -> p c n", p=128)
                    lw.inc(sync.dma_start(
                        gf[:].rearrange("p (c n) -> p c n", c=DCH), src), 16)

            @blk.gpsimd
            def _(g):
                for i in range(NSH):
                    if i >= 1:
                        g.wait_ge(ev.h, ev_sp(i - 1))
                    src = t["suT"].ap()[:, i * 128:(i + 1) * 128] \
                        .rearrange("(c p) n -> p c n", p=128)
                    a3.inc(g.dma_start(
                        uf[:].rearrange("p (c n) -> p c n", c=DCH), src), 16)

            @blk.vector
            def _(vector):
                def do_sp(i):
                    par = i % 2
                    vector.wait_ge(lw.h, lw_b + 16 * (i + 1))
                    vector.wait_ge(a3.h, a3_b + 16 * (i + 1))
                    if i >= 2:
                        vector.wait_ge(pe.h, pe_b + i - 1)
                    vector.tensor_copy(ghb[par][:], gf[:])
                    vector.tensor_copy(uhb[par][:], uf[:])
                    vector.tensor_sub(glb[par][:], gf[:], ghb[par][:])
                    ev.inc(vector.tensor_sub(ulb[par][:], uf[:], uhb[par][:]), 1)

                def do_ac(i):
                    par = i % 2
                    vector.wait_ge(sc.h, sc_b + i + 1)
                    vector.tensor_mul(tm2[:, par * 256: par * 256 + 256],
                                      tmp[:, par * 256: par * 256 + 256],
                                      sig_bc[:, off:off + 256])
                    vector.tensor_mul(tmp[:, par * 256: par * 256 + 256],
                                      tm2[:, par * 256: par * 256 + 256],
                                      pu[:, par * 512: par * 512 + 256])
                    vector.tensor_copy(acth[:, i * 256:(i + 1) * 256],
                                       tmp[:, par * 256: par * 256 + 256])
                    ev.inc(vector.tensor_sub(actl[:, i * 256:(i + 1) * 256],
                                             tmp[:, par * 256: par * 256 + 256],
                                             acth[:, i * 256:(i + 1) * 256]), 1)

                for kind, i in order:
                    (do_sp if kind == "sp" else do_ac)(i)

            @blk.tensor
            def _(tensor):
                for i in range(NSH):
                    tensor.wait_ge(ev.h, ev_sp(i))
                    if i >= 2:
                        tensor.wait_ge(sc.h, sc_b + i - 1)
                        tensor.wait_ge(ev.h, ev_ac(i - 1))
                    par = i % 2
                    for d in range(DCH):
                        rh = h1h[:, d * TOK + off: d * TOK + off + 256]
                        rl = h1l[:, d * TOK + off: d * TOK + off + 256]
                        mm3(tensor, pg[:, par * 512: par * 512 + 256],
                            ghb[par][:, d * 128:(d + 1) * 128],
                            glb[par][:, d * 128:(d + 1) * 128],
                            rh, rl, start=(d == 0), stop=(d == DCH - 1))
                        ins = mm3(tensor, pu[:, par * 512: par * 512 + 256],
                                  uhb[par][:, d * 128:(d + 1) * 128],
                                  ulb[par][:, d * 128:(d + 1) * 128],
                                  rh, rl, start=(d == 0), stop=(d == DCH - 1))
                    pe.inc(ins, 1)

            @blk.scalar
            def _(scalar):
                for i in range(NSH):
                    par = i % 2
                    scalar.wait_ge(pe.h, pe_b + i + 1)
                    if i >= 2:
                        scalar.wait_ge(ev.h, ev_ac(i - 2))
                    sc.inc(scalar.activation(tmp[:, par * 256: par * 256 + 256],
                                             pg[:, par * 512: par * 512 + 256],
                                             AF.Silu), 1)

            stack.close()
            _p5_shared_down(env, moe, acth, actl, H, q)


def _p5_shared_down(env, moe, acth, actl, H, q):
    nc, t, s = env.nc, env.t, env.s
    lw, pe, ev = (s[k] for k in ("lw", "pe", "ev"))
    NSH = ISH // 128
    NHH = NSH // 2
    with (
        nc.sbuf_tensor(f"p5d_sf{H}{q}", [128, NHH * 128], F32) as sf,
        nc.sbuf_tensor(f"p5d_h0{H}{q}", [128, NHH * 128], BF16) as sh0,
        nc.sbuf_tensor(f"p5d_h1{H}{q}", [128, NHH * 128], BF16) as sh1,
        nc.sbuf_tensor(f"p5d_l0{H}{q}", [128, NHH * 128], BF16) as sl0,
        nc.sbuf_tensor(f"p5d_l1{H}{q}", [128, NHH * 128], BF16) as sl1,
        nc.psum_tensor(f"p5d_pd{H}{q}", [128, 1024], F32) as pd,
        nc.Block() as blk,
    ):
        shb, slb = (sh0, sh1), (sl0, sl1)
        lw_b, pe_b, ev_b = lw.n, pe.n, ev.n
        NG = 2 * DCH
        order = [("sp", 0), ("sp", 1), ("sp", 2), ("sp", 3)]
        for f in range(DCH):
            if 2 * f + 4 < NG:
                order.append(("sp", 2 * f + 4))
            if 2 * f + 5 < NG:
                order.append(("sp", 2 * f + 5))
            order.append(("mo", f))
        pos = {op: k + 1 for k, op in enumerate(order)}
        ev_split = lambda g: ev_b + pos[("sp", g)]
        ev_moe = lambda f: ev_b + pos[("mo", f)]
        pe_g = lambda g: pe_b + g + 1

        @blk.sync
        def _(sync):
            for g in range(2 * DCH):
                f, hh = divmod(g, 2)
                if g >= 1:
                    sync.wait_ge(ev.h, ev_split(g - 1))
                src = t["sdT"].ap()[hh * (NHH * 128):(hh + 1) * (NHH * 128),
                                    f * 128:(f + 1) * 128] \
                    .rearrange("(c p) n -> p c n", p=128)
                lw.inc(sync.dma_start(
                    sf[:].rearrange("p (c n) -> p c n", c=NHH), src), 16)

        @blk.vector
        def _(vector):
            def do_sp(g):
                par = g % 2
                vector.wait_ge(lw.h, lw_b + 16 * (g + 1))
                if g >= 2:
                    vector.wait_ge(pe.h, pe_g(g - 2))
                vector.tensor_copy(shb[par][:], sf[:])
                ev.inc(vector.tensor_sub(slb[par][:], sf[:], shb[par][:]), 1)

            def do_mo(f):
                vector.wait_ge(pe.h, pe_g(2 * f + 1))
                ev.inc(vector.tensor_copy(
                    moe[:, f * 512 + q * 256: f * 512 + q * 256 + 256],
                    pd[:, (f % 2) * 512:(f % 2) * 512 + 256]), 1)

            for kind, idx in order:
                (do_sp if kind == "sp" else do_mo)(idx)

        @blk.tensor
        def _(tensor):
            for g in range(2 * DCH):
                f, hh = divmod(g, 2)
                par = g % 2
                tensor.wait_ge(ev.h, ev_split(g))
                if hh == 0 and f >= 2:
                    tensor.wait_ge(ev.h, ev_moe(f - 2))
                for i in range(NHH):
                    ins = mm3(tensor, pd[:, (f % 2) * 512:(f % 2) * 512 + 256],
                              shb[par][:, i * 128:(i + 1) * 128],
                              slb[par][:, i * 128:(i + 1) * 128],
                              acth[:, (hh * NHH + i) * 256:(hh * NHH + i + 1) * 256],
                              actl[:, (hh * NHH + i) * 256:(hh * NHH + i + 1) * 256],
                              start=(g % 2 == 0 and i == 0), stop=(g % 2 == 1 and i == NHH - 1))
                pe.inc(ins, 1)


def _p5_experts(env, moe, H):
    nc, t, s = env.nc, env.t, env.s
    h1h, h1l = env.h1h, env.h1l
    lw, lx, pe, ev, sc = (s[k] for k in ("lw", "lx", "pe", "ev", "sc"))
    NIM = IM // 128
    off = H * 512

    with contextlib.ExitStack() as stack:
        en = stack.enter_context
        heh = en(nc.sbuf_tensor(f"p5e_hh{H}", [128, NIM * 512], BF16))
        hel = en(nc.sbuf_tensor(f"p5e_hl{H}", [128, NIM * 512], BF16))
        gf = en(nc.sbuf_tensor(f"p5e_gf{H}", [128, DCH * 128], F32))
        uf = en(nc.sbuf_tensor(f"p5e_uf{H}", [128, DCH * 128], F32))
        gh0 = en(nc.sbuf_tensor(f"p5e_gh0{H}", [128, DCH * 128], BF16))
        gh1 = en(nc.sbuf_tensor(f"p5e_gh1{H}", [128, DCH * 128], BF16))
        gl0 = en(nc.sbuf_tensor(f"p5e_gl0{H}", [128, DCH * 128], BF16))
        gl1 = en(nc.sbuf_tensor(f"p5e_gl1{H}", [128, DCH * 128], BF16))
        uh0 = en(nc.sbuf_tensor(f"p5e_uh0{H}", [128, DCH * 128], BF16))
        uh1 = en(nc.sbuf_tensor(f"p5e_uh1{H}", [128, DCH * 128], BF16))
        ul0 = en(nc.sbuf_tensor(f"p5e_ul0{H}", [128, DCH * 128], BF16))
        ul1 = en(nc.sbuf_tensor(f"p5e_ul1{H}", [128, DCH * 128], BF16))
        df = en(nc.sbuf_tensor(f"p5e_df{H}", [128, NIM * 128], F32))
        dh0 = en(nc.sbuf_tensor(f"p5e_dh0{H}", [128, NIM * 128], BF16))
        dh1 = en(nc.sbuf_tensor(f"p5e_dh1{H}", [128, NIM * 128], BF16))
        dl0 = en(nc.sbuf_tensor(f"p5e_dl0{H}", [128, NIM * 128], BF16))
        dl1 = en(nc.sbuf_tensor(f"p5e_dl1{H}", [128, NIM * 128], BF16))
        bc = en(nc.sbuf_tensor(f"p5e_bc{H}", [128, 512], F32))
        cmbr = en(nc.sbuf_tensor(f"p5e_cr{H}", [1, 2 * 512], F32))
        tmp = en(nc.sbuf_tensor(f"p5e_tmp{H}", [128, 1024], F32))
        tm2 = en(nc.sbuf_tensor(f"p5e_tm2{H}", [128, 1024], F32))
        pg = en(nc.psum_tensor(f"p5e_pg{H}", [128, 1024], F32))
        pu = en(nc.psum_tensor(f"p5e_pu{H}", [128, 1024], F32))
        pd = en(nc.psum_tensor(f"p5e_pd{H}", [128, 1024], F32))
        pb = en(nc.psum_tensor(f"p5e_pb{H}", [128, 512], F32))
        blk = en(nc.Block())

        ghb, glb, uhb, ulb = (gh0, gh1), (gl0, gl1), (uh0, uh1), (ul0, ul1)
        dhb, dlb = (dh0, dh1), (dl0, dl1)
        a3 = s["a3"]
        lw_b, pe_b, ev_b, sc_b, lx_bE, a3_b = lw.n, pe.n, ev.n, sc.n, lx.n, a3.n
        PE_E = 1 + NIM + DCH
        SC_E = NIM
        pe_bc = lambda e: pe_b + PE_E * e + 1
        pe_gu = lambda e, i: pe_b + PE_E * e + 1 + i + 1
        pe_dn = lambda e, f: pe_b + PE_E * e + 1 + NIM + f + 1
        sc_si = lambda e, i: sc_b + SC_E * e + i + 1
        order = []
        for e in range(E):
            order.append(("bc", e, 0))
            order.append(("gs", e, 0))
            for i in range(1, NIM):
                order.append(("gs", e, i))
                order.append(("he", e, i - 1))
            order.append(("he", e, NIM - 1))
            order.append(("ds", e, 0))
            for f in range(1, DCH):
                order.append(("ds", e, f))
                order.append(("mo", e, f - 1))
            order.append(("mo", e, DCH - 1))
        pos = {op: k + 1 for k, op in enumerate(order)}
        ev_bc = lambda e: ev_b + pos[("bc", e, 0)]
        ev_gs = lambda e, i: ev_b + pos[("gs", e, i)]
        ev_he = lambda e, i: ev_b + pos[("he", e, i)]
        ev_ds = lambda e, f: ev_b + pos[("ds", e, f)]
        ev_mo = lambda e, f: ev_b + pos[("mo", e, f)]

        @blk.sync
        def _(sync):
            for e in range(E):
                if e >= 2:
                    sync.wait_ge(pe.h, pe_bc(e - 1))
                lx.inc(sync.dma_start(
                    cmbr[0:1, (e % 2) * 512:(e % 2) * 512 + 512],
                    t["csc"].ap()[e:e + 1, off:off + 512]), 16)
                for i in range(NIM):
                    gi = e * NIM + i
                    if gi >= 1:
                        pj = gi - 1
                        sync.wait_ge(ev.h, ev_gs(pj // NIM, pj % NIM))
                    src = t["guT"].ap()[e, :, i * 128:(i + 1) * 128] \
                        .rearrange("(c p) n -> p c n", p=128)
                    lw.inc(sync.dma_start(
                        gf[:].rearrange("p (c n) -> p c n", c=DCH), src), 16)
                for f in range(DCH):
                    gi = e * DCH + f
                    if gi >= 1:
                        pj = gi - 1
                        sync.wait_ge(ev.h, ev_ds(pj // DCH, pj % DCH))
                    src = t["dpT"].ap()[e, :, f * 128:(f + 1) * 128] \
                        .rearrange("(c p) n -> p c n", p=128)
                    lw.inc(sync.dma_start(
                        df[:].rearrange("p (c n) -> p c n", c=NIM), src), 16)

        @blk.gpsimd
        def _(g):
            for e in range(E):
                for i in range(NIM):
                    gi = e * NIM + i
                    if gi >= 1:
                        pj = gi - 1
                        g.wait_ge(ev.h, ev_gs(pj // NIM, pj % NIM))
                    src = t["guT"].ap()[e, :, IM + i * 128: IM + (i + 1) * 128] \
                        .rearrange("(c p) n -> p c n", p=128)
                    a3.inc(g.dma_start(
                        uf[:].rearrange("p (c n) -> p c n", c=DCH), src), 16)

        @blk.tensor
        def _(tensor):
            for e in range(E):
                if e >= 1:
                    tensor.wait_ge(ev.h, ev_he(e - 1, NIM - 1))
                tensor.wait_ge(lx.h, lx_bE + 16 * (e + 1))
                ins = tensor.matmul(pb[:, 0:512], env.onesrow[:],
                                    cmbr[0:1, (e % 2) * 512:(e % 2) * 512 + 512],
                                    start=True, stop=True)
                pe.inc(ins, 1)
                for i in range(NIM):
                    gi = e * NIM + i
                    tensor.wait_ge(ev.h, ev_gs(e, i))
                    if gi >= 2:
                        pj = gi - 2
                        tensor.wait_ge(sc.h, sc_si(pj // NIM, pj % NIM))
                        tensor.wait_ge(ev.h, ev_he(pj // NIM, pj % NIM))
                    par = i % 2
                    for d in range(DCH):
                        rh = h1h[:, d * TOK + off: d * TOK + off + 512]
                        rl = h1l[:, d * TOK + off: d * TOK + off + 512]
                        mm3(tensor, pg[:, par * 512: par * 512 + 512],
                            ghb[par][:, d * 128:(d + 1) * 128],
                            glb[par][:, d * 128:(d + 1) * 128],
                            rh, rl, start=(d == 0), stop=(d == DCH - 1))
                        ins = mm3(tensor, pu[:, par * 512: par * 512 + 512],
                                  uhb[par][:, d * 128:(d + 1) * 128],
                                  ulb[par][:, d * 128:(d + 1) * 128],
                                  rh, rl, start=(d == 0), stop=(d == DCH - 1))
                    pe.inc(ins, 1)
                for f in range(DCH):
                    gi = e * DCH + f
                    tensor.wait_ge(ev.h, ev_ds(e, f))
                    if gi >= 2:
                        pj = gi - 2
                        tensor.wait_ge(ev.h, ev_mo(pj // DCH, pj % DCH))
                    par = f % 2
                    for i in range(NIM):
                        ins = mm3(tensor, pd[:, par * 512: par * 512 + 512],
                                  dhb[par][:, i * 128:(i + 1) * 128],
                                  dlb[par][:, i * 128:(i + 1) * 128],
                                  heh[:, i * 512:(i + 1) * 512],
                                  hel[:, i * 512:(i + 1) * 512],
                                  start=(i == 0), stop=(i == NIM - 1))
                    pe.inc(ins, 1)

        @blk.scalar
        def _(scalar):
            for e in range(E):
                for i in range(NIM):
                    par = i % 2
                    gi = e * NIM + i
                    scalar.wait_ge(pe.h, pe_gu(e, i))
                    if gi >= 2:
                        pj = gi - 2
                        scalar.wait_ge(ev.h, ev_he(pj // NIM, pj % NIM))
                    sc.inc(scalar.activation(tmp[:, par * 512: par * 512 + 512],
                                             pg[:, par * 512: par * 512 + 512],
                                             AF.Silu), 1)

        @blk.vector
        def _(vector):
            def do_bc(e, _):
                vector.wait_ge(pe.h, pe_bc(e))
                ev.inc(vector.tensor_copy(bc[:], pb[:, 0:512]), 1)

            def do_gs(e, i):
                par = i % 2
                vector.wait_ge(lw.h, lw_b + (16 * NIM + 16 * DCH) * e + 16 * (i + 1))
                vector.wait_ge(a3.h, a3_b + 16 * NIM * e + 16 * (i + 1))
                if e * NIM + i >= 2:
                    pj = e * NIM + i - 2
                    vector.wait_ge(pe.h, pe_gu(pj // NIM, pj % NIM))
                vector.tensor_copy(ghb[par][:], gf[:])
                vector.tensor_copy(uhb[par][:], uf[:])
                vector.tensor_sub(glb[par][:], gf[:], ghb[par][:])
                ev.inc(vector.tensor_sub(ulb[par][:], uf[:], uhb[par][:]), 1)

            def do_he(e, i):
                par = i % 2
                vector.wait_ge(sc.h, sc_si(e, i))
                vector.tensor_mul(tm2[:, par * 512: par * 512 + 512],
                                  tmp[:, par * 512: par * 512 + 512],
                                  bc[:])
                vector.tensor_mul(tmp[:, par * 512: par * 512 + 512],
                                  tm2[:, par * 512: par * 512 + 512],
                                  pu[:, par * 512: par * 512 + 512])
                vector.tensor_copy(heh[:, i * 512:(i + 1) * 512],
                                   tmp[:, par * 512: par * 512 + 512])
                ev.inc(vector.tensor_sub(hel[:, i * 512:(i + 1) * 512],
                                         tmp[:, par * 512: par * 512 + 512],
                                         heh[:, i * 512:(i + 1) * 512]), 1)

            def do_ds(e, f):
                par = f % 2
                vector.wait_ge(lw.h, lw_b + (16 * NIM + 16 * DCH) * e + 16 * NIM + 16 * (f + 1))
                if e * DCH + f >= 2:
                    pj = e * DCH + f - 2
                    vector.wait_ge(pe.h, pe_dn(pj // DCH, pj % DCH))
                vector.tensor_copy(dhb[par][:], df[:])
                ev.inc(vector.tensor_sub(dlb[par][:], df[:], dhb[par][:]), 1)

            def do_mo(e, f):
                par = f % 2
                vector.wait_ge(pe.h, pe_dn(e, f))
                ev.inc(vector.tensor_add(moe[:, f * 512:(f + 1) * 512],
                                         moe[:, f * 512:(f + 1) * 512],
                                         pd[:, par * 512: par * 512 + 512]), 1)

            fns = {"bc": do_bc, "gs": do_gs, "he": do_he, "ds": do_ds, "mo": do_mo}
            for kind, e, idx in order:
                fns[kind](e, idx)


def _p5_tail(env, moe, H):
    nc, t, s = env.nc, env.t, env.s
    h1h, h1l = env.h1h, env.h1l
    pe, ev, st = s["pe"], s["ev"], s["st"]

    with nc.Block() as blk:
        @blk.vector
        def _(vector):
            for f in range(DCH):
                vector.tensor_add(moe[:, f * 512:(f + 1) * 512],
                                  moe[:, f * 512:(f + 1) * 512],
                                  h1h[:, f * TOK + H * 512: f * TOK + H * 512 + 512])
                ins = vector.tensor_add(moe[:, f * 512:(f + 1) * 512],
                                        moe[:, f * 512:(f + 1) * 512],
                                        h1l[:, f * TOK + H * 512: f * TOK + H * 512 + 512])
            ev.inc(ins, 1)

    emit_ln(env, moe, 512, 0, 512, f"t{H}")

    with nc.Block() as blk:
        @blk.sync
        def _(sync):
            dst = t["y"].ap()[:, H * 512:(H + 1) * 512].rearrange("(c p) n -> p c n", p=128)
            st.inc(sync.dma_start(dst, moe[:].rearrange("p (c n) -> p c n", c=DCH)), 16)


def kernel(**inputs):
    x = np.asarray(inputs["L0_attn_x"], dtype=np.float32)
    wq = np.asarray(inputs["L0_attn_Wq"], dtype=np.float32)
    wk = np.asarray(inputs["L0_attn_Wk"], dtype=np.float32)
    wv = np.asarray(inputs["L0_attn_Wv"], dtype=np.float32)
    wp = np.asarray(inputs["L0_attn_Wp"], dtype=np.float32)
    rw = np.asarray(inputs["L0_ffn_router_weight"], dtype=np.float32)
    gu = np.asarray(inputs["L0_ffn_gate_up_proj"], dtype=np.float32)
    dp = np.asarray(inputs["L0_ffn_down_proj"], dtype=np.float32)
    sg = np.asarray(inputs["L0_ffn_sh_gate_w"], dtype=np.float32)
    su = np.asarray(inputs["L0_ffn_sh_up_w"], dtype=np.float32)
    sd = np.asarray(inputs["L0_ffn_sh_down_w"], dtype=np.float32)
    sgw = np.asarray(inputs["L0_ffn_shared_gate_w"], dtype=np.float32)

    xT = [np.ascontiguousarray(x[b].T) for b in range(B)]
    common = {
        "wqT": np.ascontiguousarray(wq.T), "wkT": np.ascontiguousarray(wk.T),
        "wvT": np.ascontiguousarray(wv.T), "wpT": np.ascontiguousarray(wp.T),
        "rwT": np.ascontiguousarray(rw.T),
        "guT": np.ascontiguousarray(gu.transpose(0, 2, 1)),
        "dpT": np.ascontiguousarray(dp.transpose(0, 2, 1)),
        "sgT": np.ascontiguousarray(sg.T), "suT": np.ascontiguousarray(su.T),
        "sdT": np.ascontiguousarray(sd.T),
        "sgwT": np.ascontiguousarray(sgw.reshape(DCH, 128).T),
    }
    in_maps = []
    for c in range(8):
        b, k = c // 4, c % 4
        m = dict(common)
        m["xT"] = xT[b]
        m["xTo"] = np.ascontiguousarray(xT[b][:, k * TOK:(k + 1) * TOK])
        m["qoff"] = np.full((128, 1), k * TOK, dtype=np.float32)
        in_maps.append(m)

    nc = build(dbg=_OVR.get("dbg", ()), phases=_OVR.get("phases", ("p1", "p2", "p3", "p4", "p5")))
    res = run_bass_kernel_spmd(nc, in_maps, core_ids=list(range(8)), **_OVR.get("runkw", {}))
    if "post" in _OVR:
        return _OVR["post"](res)
    out = np.empty((B, S, D), dtype=np.float32)
    for c in range(8):
        b, k = c // 4, c % 4
        out[b, k * TOK:(k + 1) * TOK, :] = res.results[c]["y"].T
    return out


# version 05 (final)
# speedup vs baseline: 1.3310x; 1.1394x over previous
import contextlib
import numpy as np
import concourse.bass as bass
import concourse.mybir as mybir
from concourse.bass_utils import run_bass_kernel_spmd

F32 = mybir.dt.float32
BF16 = mybir.dt.bfloat16
AF = mybir.ActivationFunctionType
ALU = mybir.AluOpType
AX = mybir.AxisListType

B = 2
S = 4096
D = 2048
NH = 16
TOK = 1024
E = 8
IM = 1408
ISH = 5632
DCH = D // 128
NQT = TOK // 128
ISQ = INV_SQRT_HD = 1.0 / float(np.sqrt(128.0))
LN_EPS = 1e-5

_OVR = {}


class Ctr:
    def __init__(self, nc, name):
        self.h = nc.alloc_semaphore(name=name)
        self.n = 0

    def inc(self, ins, k=1):
        ins.then_inc(self.h, k)
        self.n += k
        return self.n


class Env:
    pass


def build(dbg=(), phases=("p1", "p2", "p3", "p4", "p5")):
    nc = bass.Bass("TRN2", target_bir_lowering=False)
    env = Env()
    env.nc = nc
    t = env.t = {}
    for name, shape in (
        ("xT", [D, S]), ("xTo", [D, TOK]), ("qoff", [128, 1]),
        ("wqT", [D, D]), ("wkT", [D, D]), ("wvT", [D, D]), ("wpT", [D, D]),
        ("rwT", [D, E]), ("guT", [E, D, 2 * IM]), ("dpT", [E, IM, D]),
        ("sgT", [D, ISH]), ("suT", [D, ISH]), ("sdT", [ISH, D]), ("sgwT", [128, DCH]),
    ):
        t[name] = nc.dram_tensor(name, shape, F32, kind="ExternalInput")
    t["y"] = nc.dram_tensor("y", [D, TOK], F32, kind="ExternalOutput")
    t["ksc"] = nc.dram_tensor("ksc", [D, S], F32, kind="Internal")
    t["vsc"] = nc.dram_tensor("vsc", [S, D], F32, kind="Internal")
    t["qsc"] = nc.dram_tensor("qsc", [D, TOK], F32, kind="Internal")
    t["osc"] = nc.dram_tensor("osc", [TOK, D], F32, kind="Internal")
    t["csc"] = nc.dram_tensor("csc", [E, TOK], F32, kind="Internal")

    env.dbg = {}
    for entry in dbg:
        name, shape = entry[0], entry[1]
        dt_ = entry[2] if len(entry) > 2 else F32
        env.dbg[name] = nc.dram_tensor("dbg_" + name, shape, dt_, kind="ExternalOutput")

    env.ident = nc.alloc_sbuf_tensor("ident", [128, 128], F32)
    env.onescol = nc.alloc_sbuf_tensor("onescol", [128, 1], F32)
    env.onesrow = nc.alloc_sbuf_tensor("onesrow", [1, 128], F32)
    env.p02 = contextlib.ExitStack()
    env.kpos = env.p02.enter_context(nc.sbuf_tensor("kpos", [128, S], F32))
    env.qpos = env.p02.enter_context(nc.sbuf_tensor("qpos", [128, NQT], F32))
    env.qoff_sb = env.p02.enter_context(nc.sbuf_tensor("qoff_sb", [128, 1], F32))
    env.bias30 = env.p02.enter_context(nc.sbuf_tensor("bias30", [128, 1], F32))

    env.s = {k: Ctr(nc, k) for k in
             ("lw", "lx", "pe", "pe2", "ev", "ev2", "sc", "sp", "st", "a1", "a2", "a3")}

    with nc.named_scope("p0"):
        emit_p0(env)
    if "p1" in phases:
        with nc.named_scope("p1"):
            emit_p1(env)
    if "p2" in phases:
        with nc.named_scope("p2"):
            emit_p2(env)
    env.p02.close()
    if "p3" in phases or "p4" in phases or "p5" in phases:
        env.h1h = nc.alloc_sbuf_tensor("h1h", [128, DCH * TOK], BF16)
        env.h1l = nc.alloc_sbuf_tensor("h1l", [128, DCH * TOK], BF16)
    if "p3" in phases:
        with nc.sbuf_tensor("r1buf", [128, DCH * TOK], F32) as r1buf:
            env.r1buf = r1buf
            with nc.named_scope("p3"):
                emit_p3(env)
    if "p4" in phases:
        env.sig = nc.alloc_sbuf_tensor("sig", [1, TOK], F32)
        with nc.named_scope("p4"):
            emit_p4(env)
    if "p5" in phases:
        with nc.named_scope("p5"):
            emit_p5(env)

    with nc.Block() as blk:
        @blk.sync
        def _(sync):
            sync.wait_ge(env.s["st"].h, env.s["st"].n)
            sync.wait_ge(env.s["sp"].h, env.s["sp"].n)
    return nc


def emit_p0(env):
    nc, t = env.nc, env.t
    lx = env.s["lx"]
    a1 = env.s["a1"]
    with nc.Block() as blk:
        @blk.sync
        def _(sync):
            lx.inc(sync.dma_start(env.qoff_sb[:], t["qoff"].ap()), 16)

        @blk.gpsimd
        def _(g):
            g.memset(env.onescol[:], 1.0)
            g.memset(env.bias30[:], -30.0)
            g.memset(env.onesrow[:], 1.0)
            g.iota(env.kpos[:], [[1, S]], base=0, channel_multiplier=0,
                   allow_small_or_imprecise_dtypes=True)
            a1.inc(g.iota(env.qpos[:], [[128, NQT]], base=0, channel_multiplier=1,
                          allow_small_or_imprecise_dtypes=True), 1)
            g.memset(env.ident[:], 1.0)
            g.affine_select(env.ident[:], env.ident[:], [[1, 128]], ALU.is_equal, 0.0,
                            base=0, channel_multiplier=-1)

        @blk.vector
        def _(v):
            v.wait_ge(lx.h, lx.n)
            v.wait_ge(a1.h, a1.n)
            v.tensor_scalar_add(env.qpos[:], env.qpos[:], env.qoff_sb[:, 0:1])


def nc_ap(handle):
    return handle[:]


def emit_p1(env):
    nc, t, s = env.nc, env.t, env.s
    NTT = S // 512
    jobs = []
    for tt in range(NTT):
        jobs.append(("k", tt, "xT", t["wkT"], t["ksc"]))
        jobs.append(("v", tt, "xT", t["wvT"], t["vsc"]))
    for tq in range(TOK // 512):
        jobs.append(("q", tq, "xTo", t["wqT"], t["qsc"]))

    lw, lx, pe, pe2, ev, ev2, sp = (s[k] for k in ("lw", "lx", "pe", "pe2", "ev", "ev2", "sp"))
    pe_b, pe2_b, ev_b, ev2_b, lw_b, lx_b, sp_b = pe.n, pe2.n, ev.n, ev2.n, lw.n, lx.n, sp.n

    xloads = []
    for j in jobs:
        key = (j[2], j[1])
        if key not in xloads:
            xloads.append(key)

    pe_after = lambda j, f: pe_b + 16 * j + f + 1
    ev_after = lambda j, f: ev_b + 16 * j + f + 1
    lw_after = lambda j, f: lw_b + 16 * (16 * j + f + 1)
    lx_after = lambda i: lx_b + 16 * (i + 1)
    pe2_after = lambda vj, c: pe2_b + 64 * vj + c + 1
    ev2_after = lambda vj, c: ev2_b + 64 * vj + c + 1
    spill_counts = []
    c0 = sp_b
    for job in jobs:
        c0 += 16 * (4 if job[0] == "v" else 1)
        spill_counts.append(c0)

    with (
        nc.sbuf_tensor("p1_x0", [128, DCH * 512], F32) as x0,
        nc.sbuf_tensor("p1_x1", [128, DCH * 512], F32) as x1,
        nc.sbuf_tensor("p1_w0", [128, DCH * 128], F32) as w0,
        nc.sbuf_tensor("p1_w1", [128, DCH * 128], F32) as w1,
        nc.sbuf_tensor("p1_kst", [128, DCH * 512], F32) as kst,
        nc.sbuf_tensor("p1_vst", [128, DCH * 512], F32) as vst,
        nc.sbuf_tensor("p1_vtst", [128, 4 * D], F32) as vtst,
        nc.psum_tensor("p1_ps", [128, 1024], F32) as ps,
        nc.psum_tensor("p1_pt", [128, 1024], F32) as pt,
        nc.Block() as blk,
    ):
        xbufs, wbufs = (x0, x1), (w0, w1)

        @blk.sync
        def _(sync):
            li = 0
            for j, job in enumerate(jobs):
                kind, ts, xsrc, wT, dst = job
                idx = xloads.index((xsrc, ts))
                if idx == li:
                    if idx >= 2:
                        last_j = max(jj for jj, jb in enumerate(jobs)
                                     if (jb[2], jb[1]) == xloads[idx - 2])
                        sync.wait_ge(pe.h, pe_after(last_j, 15))
                    src = t[xsrc].ap()[:, ts * 512:(ts + 1) * 512] \
                        .rearrange("(c p) n -> p c n", p=128)
                    lx.inc(sync.dma_start(
                        xbufs[idx % 2][:].rearrange("p (c n) -> p c n", c=DCH), src), 16)
                    li += 1
                for f in range(16):
                    gi = 16 * j + f
                    if gi >= 2:
                        pj, pf = divmod(gi - 2, 16)
                        sync.wait_ge(pe.h, pe_after(pj, pf))
                    src = wT.ap()[:, f * 128:(f + 1) * 128] \
                        .rearrange("(c p) n -> p c n", p=128)
                    lw.inc(sync.dma_start(
                        wbufs[gi % 2][:].rearrange("p (c n) -> p c n", c=DCH), src), 16)

        @blk.tensor
        def _(tensor):
            vj = 0
            for j, job in enumerate(jobs):
                kind, ts, xsrc, wT, dst = job
                xi = xloads.index((xsrc, ts))
                xb = xbufs[xi % 2]
                for f in range(16):
                    gi = 16 * j + f
                    tensor.wait_ge(lw.h, lw_after(j, f))
                    if f == 0:
                        tensor.wait_ge(lx.h, lx_after(xi))
                    if gi >= 2:
                        pj, pf = divmod(gi - 2, 16)
                        tensor.wait_ge(ev.h, ev_after(pj, pf))
                    wb = wbufs[gi % 2]
                    pss = ps[:, (gi % 2) * 512:(gi % 2) * 512 + 512]
                    for d in range(DCH):
                        ins = tensor.matmul(pss, wb[:, d * 128:(d + 1) * 128],
                                            xb[:, d * 512:(d + 1) * 512],
                                            start=(d == 0), stop=(d == DCH - 1))
                    pe.inc(ins, 1)
                if kind == "v":
                    for c in range(64):
                        sblk, fc = divmod(c, 16)
                        tensor.wait_ge(ev.h, ev_after(j, fc))
                        if c >= 2:
                            tensor.wait_ge(ev2.h, ev2_after(vj, c - 2))
                        ins = tensor.transpose(
                            pt[:, (c % 2) * 512:(c % 2) * 512 + 128],
                            vst[:, fc * 512 + sblk * 128: fc * 512 + sblk * 128 + 128],
                            env.ident[:])
                        pe2.inc(ins, 1)
                    vj += 1

        @blk.vector
        def _(vector):
            vj = 0
            for j, job in enumerate(jobs):
                kind = job[0]
                stg = vst if kind == "v" else kst
                prev = None
                for jj in range(j - 1, -1, -1):
                    if (jobs[jj][0] == "v") == (kind == "v"):
                        prev = jj
                        break
                for f in range(16):
                    gi = 16 * j + f
                    vector.wait_ge(pe.h, pe_after(j, f))
                    if f == 0 and prev is not None:
                        vector.wait_ge(sp.h, spill_counts[prev])
                    pss = ps[:, (gi % 2) * 512:(gi % 2) * 512 + 512]
                    ev.inc(vector.tensor_copy(stg[:, f * 512:(f + 1) * 512], pss), 1)
                if kind == "v":
                    for c in range(64):
                        sblk, fc = divmod(c, 16)
                        vector.wait_ge(pe2.h, pe2_after(vj, c))
                        ev2.inc(vector.tensor_copy(
                            vtst[:, sblk * D + fc * 128: sblk * D + fc * 128 + 128],
                            pt[:, (c % 2) * 512:(c % 2) * 512 + 128]), 1)
                    vj += 1

        @blk.gpsimd
        def _(g):
            vj = 0
            for j, job in enumerate(jobs):
                kind, ts, xsrc, wT, dst = job
                if kind in ("k", "q"):
                    g.wait_ge(ev.h, ev_after(j, 15))
                    dstap = dst.ap()[:, ts * 512:(ts + 1) * 512] \
                        .rearrange("(c p) n -> p c n", p=128)
                    sp.inc(g.dma_start(dstap, kst[:].rearrange("p (c n) -> p c n", c=DCH)), 16)
                else:
                    for sblk in range(4):
                        g.wait_ge(ev2.h, ev2_after(vj, 16 * (sblk + 1) - 1))
                        row = ts * 512 + sblk * 128
                        sp.inc(g.dma_start(dst.ap()[row:row + 128, :],
                                           vtst[:, sblk * D:(sblk + 1) * D]), 16)
                    vj += 1

    dbg_dump(env, [("ksc", t["ksc"], None), ("vsc", t["vsc"], None), ("qsc", t["qsc"], None)],
             wait=("sp",))


def dbg_dump(env, items, wait=()):
    nc, s = env.nc, env.s
    items = [(n, src, ap) for (n, src, ap) in items if n in env.dbg]
    if not items:
        return
    with nc.Block() as blk:
        @blk.sync
        def _(sync):
            for w in wait:
                sync.wait_ge(s[w].h, s[w].n)
            for name, src, ap in items:
                srcap = src.ap() if hasattr(src, "ap") else (ap if ap is not None else src[:])
                s["st"].inc(sync.dma_start(env.dbg[name].ap(), srcap), 16)


def emit_p2(env):
    nc, t, s = env.nc, env.t, env.s
    lx, pe, ev, sc, sp = (s[k] for k in ("lx", "pe", "ev", "sc", "sp"))
    lx_b, pe_b, ev_b, sc_b, sp_b = lx.n, pe.n, ev.n, sc.n, sp.n

    PE_PP = 41
    EV_PP = 42

    def pe_qk(p, i):
        return pe_b + PE_PP * p + i + 1

    def pe_tr(p, c):
        return pe_b + PE_PP * p + 8 + c + 1

    def pe_av(p):
        return pe_b + PE_PP * p + PE_PP

    def ev_stt(p, i):
        return ev_b + EV_PP * p + i + 1

    def ev_rc(p):
        return ev_b + EV_PP * p + 9

    def ev_et(p, c):
        return ev_b + EV_PP * p + 10 + c

    def ev_o(p):
        return ev_b + EV_PP * p + EV_PP

    def sc_exp(p, i):
        return sc_b + 8 * p + i + 1

    with (
        nc.sbuf_tensor("p2_k0", [128, S], F32) as k0,
        nc.sbuf_tensor("p2_k1", [128, S], F32) as k1,
        nc.sbuf_tensor("p2_v0", [128, S], F32) as v0,
        nc.sbuf_tensor("p2_v1", [128, S], F32) as v1,
        nc.sbuf_tensor("p2_q0", [128, TOK], F32) as q0,
        nc.sbuf_tensor("p2_q1", [128, TOK], F32) as q1,
        nc.sbuf_tensor("p2_e", [128, S], F32) as e_sb,
        nc.sbuf_tensor("p2_em", [128, S], F32) as e_m,
        nc.sbuf_tensor("p2_et", [128, 256], F32) as et_sb,
        nc.sbuf_tensor("p2_o", [128, 256], F32) as o_st,
        nc.sbuf_tensor("p2_bm", [128, 64], F32) as bm,
        nc.sbuf_tensor("p2_ac", [128, 64], F32) as acc,
        nc.sbuf_tensor("p2_nm", [128, 8], F32) as nm,
        nc.sbuf_tensor("p2_bi", [128, 8], F32) as bias,
        nc.sbuf_tensor("p2_ll", [128, 8], F32) as ll,
        nc.sbuf_tensor("p2_rr", [128, 8], F32) as rr,
        nc.psum_tensor("p2_ps", [128, 4096], F32) as pss,
        nc.Block() as blk,
    ):
        kb_, vb_, qb_ = (k0, k1), (v0, v1), (q0, q1)

        @blk.sync
        def _(sync):
            sync.wait_ge(sp.h, sp_b)
            for h in range(NH):
                if h >= 2:
                    sync.wait_ge(pe.h, pe_av((h - 1) * NQT - 1))
                lx.inc(sync.dma_start(kb_[h % 2][:],
                                      t["ksc"].ap()[h * 128:(h + 1) * 128, :]), 16)
                src = t["vsc"].ap()[:, h * 128:(h + 1) * 128] \
                    .rearrange("(c p) n -> p c n", p=128)
                lx.inc(sync.dma_start(
                    vb_[h % 2][:].rearrange("p (c n) -> p c n", c=S // 128), src), 16)
                lx.inc(sync.dma_start(qb_[h % 2][:],
                                      t["qsc"].ap()[h * 128:(h + 1) * 128, :]), 16)

        @blk.tensor
        def _(tensor):
            for h in range(NH):
                kb, vb, qb = kb_[h % 2], vb_[h % 2], qb_[h % 2]
                for qt in range(NQT):
                    p = h * NQT + qt
                    if qt == 0:
                        tensor.wait_ge(lx.h, lx_b + 48 * (h + 1))
                    if p > 0:
                        tensor.wait_ge(ev.h, ev_o(p - 1))
                        tensor.wait_ge(sc.h, sc_exp(p - 1, 7))
                    for kbi in range(8):
                        pe.inc(tensor.matmul(pss[:, kbi * 512:(kbi + 1) * 512],
                                             qb[:, qt * 128:(qt + 1) * 128],
                                             kb[:, kbi * 512:(kbi + 1) * 512],
                                             start=True, stop=True), 1)
                    def do_T(c):
                        if c < 4:
                            tensor.wait_ge(sc.h, sc_exp(p, c))
                        tensor.wait_ge(ev.h, ev_stt(p, c // 4))
                        if c >= 4:
                            tensor.wait_ge(ev.h, ev_et(p, c - 4))
                        pe.inc(tensor.transpose(
                            pss[:, (c % 4) * 512:(c % 4) * 512 + 128],
                            e_m[:, c * 128:(c + 1) * 128], env.ident[:]), 1)

                    def do_AV(c):
                        tensor.wait_ge(ev.h, ev_et(p, c))
                        if c == 0:
                            tensor.wait_ge(sc.h, sc_exp(p, 4))
                        ins = tensor.matmul(pss[:, 4 * 512: 4 * 512 + 128],
                                            et_sb[:, (c % 2) * 128:(c % 2) * 128 + 128],
                                            vb[:, c * 128:(c + 1) * 128],
                                            start=(c == 0), stop=(c == 31))
                        if c == 31:
                            pe.inc(ins, 1)

                    do_T(0)
                    do_T(1)
                    for c in range(32):
                        do_AV(c)
                        if c + 2 < 32:
                            do_T(c + 2)

        @blk.vector
        def _(vector):
            NOPS = 6
            for h in range(NH):
                for qt in range(NQT):
                    p = h * NQT + qt
                    for i in range(8):
                        vector.wait_ge(sc.h, sc_exp(p, i))
                        ev.inc(vector.scalar_tensor_tensor(
                            e_m[:, i * 512:(i + 1) * 512],
                            env.kpos[:, i * 512:(i + 1) * 512],
                            env.qpos[:, qt:qt + 1], e_sb[:, i * 512:(i + 1) * 512],
                            ALU.is_le, ALU.mult, accum_out=acc[:, 8 * i:8 * i + 1]), 1)
                    for _ in range(NOPS):
                        vector.engine_nop()
                    vector.reduce_sum(ll[:, 0:1], acc[:, 0:57:8], axis=AX.X)
                    for _ in range(NOPS):
                        vector.engine_nop()
                    ev.inc(vector.reciprocal(rr[:, 0:1], ll[:, 0:1]), 1)
                    for c in range(32):
                        vector.wait_ge(pe.h, pe_tr(p, c))
                        ev.inc(vector.tensor_copy(
                            et_sb[:, (c % 2) * 128:(c % 2) * 128 + 128],
                            pss[:, (c % 4) * 512:(c % 4) * 512 + 128]), 1)
                    vector.wait_ge(pe.h, pe_av(p))
                    if p >= 2:
                        vector.wait_ge(sp.h, sp_b + 16 * (p - 1))
                    ev.inc(vector.tensor_scalar_mul(
                        o_st[:, (p % 2) * 128:(p % 2) * 128 + 128],
                        pss[:, 4 * 512: 4 * 512 + 128], rr[:, 0:1]), 1)

        @blk.scalar
        def _(scalar):
            for h in range(NH):
                for qt in range(NQT):
                    p = h * NQT + qt
                    for i in range(8):
                        scalar.wait_ge(pe.h, pe_qk(p, i))
                        sc.inc(scalar.activation(e_sb[:, i * 512:(i + 1) * 512],
                                                 pss[:, i * 512:(i + 1) * 512], AF.Exp,
                                                 bias=env.bias30[:, 0:1], scale=ISQ), 1)

        @blk.gpsimd
        def _(g):
            for h in range(NH):
                for qt in range(NQT):
                    p = h * NQT + qt
                    g.wait_ge(ev.h, ev_o(p))
                    sp.inc(g.dma_start(
                        t["osc"].ap()[qt * 128:(qt + 1) * 128, h * 128:(h + 1) * 128],
                        o_st[:, (p % 2) * 128:(p % 2) * 128 + 128]), 16)

    dbg_dump(env, [("osc", t["osc"], None)], wait=("sp",))


def emit_ln(env, buf, stride, off, width, tag="", out_pair=None):
    nc, s = env.nc, env.s
    pe, ev, sc = s["pe"], s["ev"], s["sc"]
    NHW = width // 512
    assert width % 512 == 0 and NHW in (1, 2)
    pe_b, ev_b, sc_b = pe.n, ev.n, sc.n
    NOPS = 6

    with (
        nc.sbuf_tensor(f"ln_sq{tag}", [128, 2 * width], F32) as sq,
        nc.sbuf_tensor(f"ln_srow{tag}", [1, 2 * width], F32) as srow,
        nc.sbuf_tensor(f"ln_inv{tag}", [1, 2 * width], F32) as invr,
        nc.psum_tensor(f"ln_ps{tag}", [128, 2 * width], F32) as pln,
        nc.psum_tensor(f"ln_bc{tag}", [128, 2 * width], F32) as pbc,
        nc.Block() as blk,
    ):
        def sl(d):
            return buf[:, d * stride + off: d * stride + off + width]

        @blk.scalar
        def _(scalar):
            for d in range(DCH):
                if d >= 2:
                    scalar.wait_ge(pe.h, pe_b + d - 1)
                sc.inc(scalar.activation(sq[:, (d % 2) * width:(d % 2) * width + width],
                                         sl(d), AF.Square), 1)
            scalar.wait_ge(ev.h, ev_b + 2)
            sc.inc(scalar.activation(invr[0:1, width:2 * width], srow[0:1, width:2 * width],
                                     AF.Sqrt, bias=0.0, scale=1.0), 1)

        @blk.tensor
        def _(tensor):
            for d in range(DCH):
                tensor.wait_ge(sc.h, sc_b + d + 1)
                for hw in range(NHW):
                    tensor.matmul(pln[0:1, hw * 512:(hw + 1) * 512], env.onescol[:],
                                  sl(d)[:, hw * 512:(hw + 1) * 512],
                                  start=(d == 0), stop=(d == DCH - 1))
                    ins = tensor.matmul(
                        pln[0:1, width + hw * 512: width + (hw + 1) * 512],
                        env.onescol[:],
                        sq[:, (d % 2) * width + hw * 512:(d % 2) * width + (hw + 1) * 512],
                        start=(d == 0), stop=(d == DCH - 1))
                pe.inc(ins, 1)
            tensor.wait_ge(ev.h, ev_b + 4)
            for hw in range(NHW):
                tensor.matmul(pbc[:, hw * 512:(hw + 1) * 512], env.onesrow[:],
                              invr[0:1, hw * 512:(hw + 1) * 512], start=True, stop=True)
                ins = tensor.matmul(pbc[:, width + hw * 512: width + (hw + 1) * 512],
                                    env.onesrow[:],
                                    srow[0:1, width + hw * 512: width + (hw + 1) * 512],
                                    start=True, stop=True)
            pe.inc(ins, 1)

        @blk.vector
        def _(vector):
            vector.wait_ge(pe.h, pe_b + DCH)
            ev.inc(vector.tensor_scalar_mul(srow[0:1, 0:2 * width],
                                            pln[0:1, 0:2 * width], 1.0 / D), 1)
            for _ in range(NOPS):
                vector.engine_nop()
            vector.tensor_mul(invr[0:1, 0:width], srow[0:1, 0:width], srow[0:1, 0:width])
            for _ in range(NOPS):
                vector.engine_nop()
            vector.tensor_sub(invr[0:1, width:2 * width],
                              srow[0:1, width:2 * width], invr[0:1, 0:width])
            for _ in range(NOPS):
                vector.engine_nop()
            ev.inc(vector.tensor_scalar_add(srow[0:1, width:2 * width],
                                            invr[0:1, width:2 * width], LN_EPS), 1)
            vector.wait_ge(sc.h, sc_b + DCH + 1)
            vector.reciprocal(invr[0:1, 0:width], invr[0:1, width:2 * width])
            for _ in range(NOPS):
                vector.engine_nop()
            ev.inc(vector.engine_nop(), 1)
            ev.inc(vector.tensor_mul(srow[0:1, width:2 * width], srow[0:1, 0:width],
                                     invr[0:1, 0:width]), 1)
            vector.wait_ge(pe.h, pe_b + DCH + 1)
            for d in range(DCH):
                vector.tensor_mul(sq[:, 0:width], sl(d), pbc[:, 0:width])
                if out_pair is None:
                    ins = vector.tensor_sub(sl(d), sq[:, 0:width], pbc[:, width:2 * width])
                else:
                    hi, lo, ostride = out_pair
                    osl_h = hi[:, d * ostride + off: d * ostride + off + width]
                    osl_l = lo[:, d * ostride + off: d * ostride + off + width]
                    vector.tensor_sub(sq[:, width:2 * width], sq[:, 0:width],
                                      pbc[:, width:2 * width])
                    vector.tensor_copy(osl_h, sq[:, width:2 * width])
                    ins = vector.tensor_sub(osl_l, sq[:, width:2 * width], osl_h)
            ev.inc(ins, 1)


def emit_wsplit(env, vector, wf, wh, wl, ncols):
    vector.tensor_copy(wh[:, 0:ncols], wf[:, 0:ncols])
    return vector.tensor_sub(wl[:, 0:ncols], wf[:, 0:ncols], wh[:, 0:ncols])


def mm3(tensor, out, lh, ll, rh, rl, start, stop):
    tensor.matmul(out, lh, rh, start=start, stop=False)
    tensor.matmul(out, lh, rl, start=False, stop=False)
    return tensor.matmul(out, ll, rh, start=False, stop=stop)


def emit_p3(env):
    nc, t, s = env.nc, env.t, env.s
    lw, lx, pe, pe2, ev, ev2, sp, sc, a2 = (s[k] for k in
        ("lw", "lx", "pe", "pe2", "ev", "ev2", "sp", "sc", "a2"))
    lw_b, lx_b, pe_b, pe2_b, ev_b, ev2_b, sc_b, a2_b = \
        lw.n, lx.n, pe.n, pe2.n, ev.n, ev2.n, sc.n, a2.n

    pe_pf = lambda tt, f: pe_b + 16 * tt + f + 1
    ev_r1 = lambda tt, f: ev_b + 16 * tt + f + 1
    pe2_tr = lambda tt, c: pe2_b + 64 * tt + c + 1
    ev2_tr = lambda tt, c: ev2_b + 128 * tt + 2 * (c + 1)
    lw_after = lambda gi: lw_b + 16 * (gi + 1)
    lx_after = lambda i: lx_b + 16 * (i + 1)

    r1 = env.r1buf

    with (
        nc.sbuf_tensor("p3_ot", [128, 2 * D], F32) as ot,
        nc.sbuf_tensor("p3_afh", [128, DCH * 512], BF16) as afh,
        nc.sbuf_tensor("p3_afl", [128, DCH * 512], BF16) as afl,
        nc.sbuf_tensor("p3_wf", [128, DCH * 128], F32) as wf,
        nc.sbuf_tensor("p3_wh0", [128, DCH * 128], BF16) as wh0,
        nc.sbuf_tensor("p3_wh1", [128, DCH * 128], BF16) as wh1,
        nc.sbuf_tensor("p3_wl0", [128, DCH * 128], BF16) as wl0,
        nc.sbuf_tensor("p3_wl1", [128, DCH * 128], BF16) as wl1,
        nc.sbuf_tensor("p3_xr", [128, 1024], F32) as xr,
        nc.psum_tensor("p3_ps", [128, 1024], F32) as ps,
        nc.psum_tensor("p3_pt", [128, 2048], F32) as pt,
        nc.Block() as blk,
    ):
        whb, wlb = (wh0, wh1), (wl0, wl1)

        @blk.sync
        def _(sync):
            for tt in range(2):
                sync.wait_ge(sp.h, sp.n)
                if tt == 1:
                    sync.wait_ge(pe2.h, pe2_tr(0, 63))
                for cch in range(4):
                    if cch >= 2:
                        sync.wait_ge(pe2.h, pe2_tr(tt, 16 * (cch - 1) - 1))
                    lx.inc(sync.dma_start(
                        ot[:, (cch % 2) * D:(cch % 2) * D + D],
                        t["osc"].ap()[tt * 512 + cch * 128: tt * 512 + (cch + 1) * 128, :]), 16)
                for f in range(16):
                    gi = 16 * tt + f
                    if gi >= 1:
                        sync.wait_ge(a2.h, a2_b + gi)
                    src = t["wpT"].ap()[:, f * 128:(f + 1) * 128] \
                        .rearrange("(c p) n -> p c n", p=128)
                    lw.inc(sync.dma_start(
                        wf[:].rearrange("p (c n) -> p c n", c=DCH), src), 16)
                    if gi >= 2:
                        pj, pf = divmod(gi - 2, 16)
                        sync.wait_ge(ev.h, ev_r1(pj, pf))
                    src2 = t["xTo"].ap()[f * 128:(f + 1) * 128, tt * 512:(tt + 1) * 512]
                    lx.inc(sync.dma_start(xr[:, (gi % 2) * 512:(gi % 2) * 512 + 512], src2), 16)

        @blk.scalar
        def _(scalar):
            for tt in range(2):
                for f in range(16):
                    gi = 16 * tt + f
                    scalar.wait_ge(lw.h, lw_after(gi))
                    if gi >= 2:
                        pj, pf = divmod(gi - 2, 16)
                        scalar.wait_ge(pe.h, pe_pf(pj, pf))
                    sc.inc(scalar.copy(whb[gi % 2][:], wf[:]), 1)

        @blk.tensor
        def _(tensor):
            for tt in range(2):
                for c in range(64):
                    sblk, fc = divmod(c, 16)
                    tensor.wait_ge(lx.h, lx_after(tt * 20 + sblk))
                    if c >= 4:
                        tensor.wait_ge(ev2.h, ev2_tr(tt, c - 4))
                    ins = tensor.transpose(pt[:, (c % 4) * 512:(c % 4) * 512 + 128],
                                           ot[:, (sblk % 2) * D + fc * 128: (sblk % 2) * D + fc * 128 + 128],
                                           env.ident[:])
                    pe2.inc(ins, 1)
                for f in range(16):
                    gi = 16 * tt + f
                    if f == 0:
                        tensor.wait_ge(ev2.h, ev2_tr(tt, 63))
                    tensor.wait_ge(a2.h, a2_b + gi + 1)
                    if gi >= 2:
                        pj, pf = divmod(gi - 2, 16)
                        tensor.wait_ge(ev.h, ev_r1(pj, pf))
                    par = gi % 2
                    for d in range(DCH):
                        ins = mm3(tensor, ps[:, par * 512: par * 512 + 512],
                                  whb[par][:, d * 128:(d + 1) * 128],
                                  wlb[par][:, d * 128:(d + 1) * 128],
                                  afh[:, d * 512:(d + 1) * 512],
                                  afl[:, d * 512:(d + 1) * 512],
                                  start=(d == 0), stop=(d == DCH - 1))
                    pe.inc(ins, 1)

        @blk.vector
        def _(vector):
            for tt in range(2):
                for c in range(64):
                    sblk, fc = divmod(c, 16)
                    vector.wait_ge(pe2.h, pe2_tr(tt, c))
                    if tt == 1 and c == 0:
                        vector.wait_ge(pe.h, pe_pf(0, 15))
                    dsth = afh[:, fc * 512 + sblk * 128: fc * 512 + sblk * 128 + 128]
                    dstl = afl[:, fc * 512 + sblk * 128: fc * 512 + sblk * 128 + 128]
                    pslot = pt[:, (c % 4) * 512:(c % 4) * 512 + 128]
                    ev2.inc(vector.tensor_copy(dsth, pslot), 1)
                    ev2.inc(vector.tensor_sub(dstl, pslot, dsth), 1)
                for f in range(16):
                    gi = 16 * tt + f
                    vector.wait_ge(sc.h, sc_b + gi + 1)
                    if gi >= 2:
                        pj, pf = divmod(gi - 2, 16)
                        vector.wait_ge(pe.h, pe_pf(pj, pf))
                    a2.inc(vector.tensor_sub(wlb[gi % 2][:], wf[:], whb[gi % 2][:]), 1)
                    vector.wait_ge(pe.h, pe_pf(tt, f))
                    vector.wait_ge(lx.h, lx_after(tt * 20 + 4 + f))
                    ev.inc(vector.tensor_add(
                        r1[:, f * TOK + tt * 512: f * TOK + tt * 512 + 512],
                        ps[:, (gi % 2) * 512:(gi % 2) * 512 + 512],
                        xr[:, (gi % 2) * 512:(gi % 2) * 512 + 512]), 1)

    emit_ln(env, r1, TOK, 0, 512, "a", out_pair=(env.h1h, env.h1l, TOK))
    emit_ln(env, r1, TOK, 512, 512, "b", out_pair=(env.h1h, env.h1l, TOK))
    if "h1" in env.dbg:
        with nc.Block() as blk:
            @blk.sync
            def _(sync):
                for c2 in range(DCH):
                    s["st"].inc(sync.dma_start(
                        env.dbg["h1"].ap()[c2 * 128:(c2 + 1) * 128, 0:TOK],
                        env.h1h[:, c2 * TOK:(c2 + 1) * TOK]), 16)
                    s["st"].inc(sync.dma_start(
                        env.dbg["h1"].ap()[c2 * 128:(c2 + 1) * 128, TOK:2 * TOK],
                        env.h1l[:, c2 * TOK:(c2 + 1) * TOK]), 16)


def emit_p4(env):
    nc, t, s = env.nc, env.t, env.s
    h1h, h1l = env.h1h, env.h1l
    lw, pe, ev, sc, sp = (s[k] for k in ("lw", "pe", "ev", "sc", "sp"))
    NOPS = 6

    with contextlib.ExitStack() as stack:
        en = stack.enter_context
        rwf = en(nc.sbuf_tensor("p4_rwf", [128, DCH * E], F32))
        rwh = en(nc.sbuf_tensor("p4_rwh", [128, DCH * E], BF16))
        rwl = en(nc.sbuf_tensor("p4_rwl", [128, DCH * E], BF16))
        zwf = en(nc.sbuf_tensor("p4_zwf", [128, DCH], F32))
        zwh = en(nc.sbuf_tensor("p4_zwh", [128, DCH], BF16))
        zwl = en(nc.sbuf_tensor("p4_zwl", [128, DCH], BF16))
        lg = en(nc.sbuf_tensor("p4_lg", [E, TOK], F32))
        lt = en(nc.sbuf_tensor("p4_lt", [128, 8], F32))
        er = en(nc.sbuf_tensor("p4_er", [128, 8], F32))
        pr = en(nc.sbuf_tensor("p4_pr", [128, 8], F32))
        eq1 = en(nc.sbuf_tensor("p4_e1", [128, 8], F32))
        t1 = en(nc.sbuf_tensor("p4_t1", [128, 8], F32))
        pr2 = en(nc.sbuf_tensor("p4_p2", [128, 8], F32))
        eq2 = en(nc.sbuf_tensor("p4_e2", [128, 8], F32))
        t2 = en(nc.sbuf_tensor("p4_t2", [128, 8], F32))
        cm = en(nc.sbuf_tensor("p4_cm", [128, 8], F32))
        cmb = en(nc.sbuf_tensor("p4_cmb", [E, TOK], F32))
        s_nm = en(nc.sbuf_tensor("p4_s0", [128, 8], F32))
        s_sum = en(nc.sbuf_tensor("p4_s1", [128, 8], F32))
        s_rs = en(nc.sbuf_tensor("p4_s2", [128, 8], F32))
        s_m1 = en(nc.sbuf_tensor("p4_s3", [128, 8], F32))
        s_m2 = en(nc.sbuf_tensor("p4_s4", [128, 8], F32))
        s_w = en(nc.sbuf_tensor("p4_s5", [128, 8], F32))
        s_wr = en(nc.sbuf_tensor("p4_s6", [128, 8], F32))
        pl = en(nc.psum_tensor("p4_pl", [128, 1024], F32))
        pz = en(nc.psum_tensor("p4_pz", [128, 1024], F32))
        pt = en(nc.psum_tensor("p4_pt", [128, 1024], F32))
        blk = en(nc.Block())

        lw_b, pe_b, ev_b, sc_b, sp_b = lw.n, pe.n, ev.n, sc.n, sp.n
        pe_t = lambda i, k: pe_b + 2 + 2 * i + k
        ev_t = lambda i, k: ev_b + 2 + 3 * i + k
        sc_t = lambda i: sc_b + 1 + i + 1

        @blk.sync
        def _(sync):
            src2 = t["rwT"].ap()[:, :].rearrange("(c p) n -> p c n", p=128)
            lw.inc(sync.dma_start(rwf[:].rearrange("p (c n) -> p c n", c=DCH), src2), 16)
            lw.inc(sync.dma_start(zwf[:], t["sgwT"].ap()), 16)

        @blk.vector
        def _(vector):
            vector.wait_ge(lw.h, lw_b + 32)
            vector.tensor_copy(rwh[:], rwf[:])
            vector.tensor_copy(zwh[:], zwf[:])
            for _ in range(NOPS):
                vector.engine_nop()
            vector.tensor_sub(rwl[:], rwf[:], rwh[:])
            ev.inc(vector.tensor_sub(zwl[:], zwf[:], zwh[:]), 1)
            vector.wait_ge(pe.h, pe_b + 2)
            vector.tensor_copy(lg[0:E, 0:512], pl[0:E, 0:512])
            ev.inc(vector.tensor_copy(lg[0:E, 512:1024], pl[0:E, 512:1024]), 1)
            for i in range(NQT):
                par = i % 2
                vector.wait_ge(pe.h, pe_t(i, 1))
                vector.tensor_copy(lt[:, 0:E], pt[:, par * 512: par * 512 + E])
                for _ in range(NOPS):
                    vector.engine_nop()
                ev.inc(vector.reduce_max(s_nm[:, 0:1], lt[:, 0:E], axis=AX.X, negate=True), 1)
                vector.wait_ge(sc.h, sc_t(i))
                vector.reciprocal(s_rs[:, 0:1], s_sum[:, 0:1])
                for _ in range(NOPS):
                    vector.engine_nop()
                vector.tensor_scalar_mul(pr[:, 0:E], er[:, 0:E], s_rs[:, 0:1])
                for _ in range(NOPS):
                    vector.engine_nop()
                vector.reduce_max(s_m1[:, 0:1], pr[:, 0:E], axis=AX.X)
                for _ in range(NOPS):
                    vector.engine_nop()
                vector.tensor_scalar(eq1[:, 0:E], pr[:, 0:E], s_m1[:, 0:1], None, ALU.is_equal)
                for _ in range(NOPS):
                    vector.engine_nop()
                vector.tensor_mul(t1[:, 0:E], pr[:, 0:E], eq1[:, 0:E])
                for _ in range(NOPS):
                    vector.engine_nop()
                vector.tensor_sub(pr2[:, 0:E], pr[:, 0:E], t1[:, 0:E])
                for _ in range(NOPS):
                    vector.engine_nop()
                vector.reduce_max(s_m2[:, 0:1], pr2[:, 0:E], axis=AX.X)
                for _ in range(NOPS):
                    vector.engine_nop()
                vector.tensor_scalar(eq2[:, 0:E], pr2[:, 0:E], s_m2[:, 0:1], None, ALU.is_equal)
                for _ in range(NOPS):
                    vector.engine_nop()
                vector.tensor_add(t1[:, 0:E], eq1[:, 0:E], eq2[:, 0:E])
                vector.tensor_add(s_w[:, 0:1], s_m1[:, 0:1], s_m2[:, 0:1])
                for _ in range(NOPS):
                    vector.engine_nop()
                vector.reciprocal(s_wr[:, 0:1], s_w[:, 0:1])
                vector.tensor_mul(t2[:, 0:E], pr[:, 0:E], t1[:, 0:E])
                for _ in range(NOPS):
                    vector.engine_nop()
                ev.inc(vector.tensor_scalar_mul(cm[:, 0:E], t2[:, 0:E], s_wr[:, 0:1]), 1)
                vector.wait_ge(pe.h, pe_t(i, 2))
                ev.inc(vector.tensor_copy(cmb[0:E, i * 128:(i + 1) * 128],
                                          pt[0:E, par * 512 + 128: par * 512 + 256]), 1)

        @blk.tensor
        def _(tensor):
            tensor.wait_ge(ev.h, ev_b + 1)
            for hw in range(2):
                for d in range(DCH):
                    rh = h1h[:, d * TOK + hw * 512: d * TOK + hw * 512 + 512]
                    rl = h1l[:, d * TOK + hw * 512: d * TOK + hw * 512 + 512]
                    mm3(tensor, pl[0:E, hw * 512:(hw + 1) * 512],
                        rwh[:, d * E:(d + 1) * E], rwl[:, d * E:(d + 1) * E],
                        rh, rl, start=(d == 0), stop=(d == DCH - 1))
                    ins = mm3(tensor, pz[0:1, hw * 512:(hw + 1) * 512],
                              zwh[:, d:d + 1], zwl[:, d:d + 1],
                              rh, rl, start=(d == 0), stop=(d == DCH - 1))
                pe.inc(ins, 1)
            for i in range(NQT):
                par = i % 2
                tensor.wait_ge(ev.h, ev_b + 2)
                if i >= 2:
                    tensor.wait_ge(ev.h, ev_t(i - 2, 3))
                ins = tensor.transpose(pt[:, par * 512: par * 512 + E],
                                       lg[0:E, i * 128:(i + 1) * 128],
                                       env.ident[0:E, 0:E])
                pe.inc(ins, 1)
                tensor.wait_ge(ev.h, ev_t(i, 2))
                ins = tensor.transpose(pt[0:E, par * 512 + 128: par * 512 + 256],
                                       cm[:, 0:E], env.ident[:])
                pe.inc(ins, 1)

        @blk.scalar
        def _(scalar):
            scalar.wait_ge(pe.h, pe_b + 2)
            scalar.wait_ge(ev.h, ev_b + 2)
            scalar.activation(env.sig[0:1, 0:512], pz[0:1, 0:512], AF.Sigmoid)
            sc.inc(scalar.activation(env.sig[0:1, 512:1024], pz[0:1, 512:1024], AF.Sigmoid), 1)
            for i in range(NQT):
                scalar.wait_ge(ev.h, ev_t(i, 1))
                sc.inc(scalar.activation(er[:, 0:E], lt[:, 0:E], AF.Exp,
                                         bias=s_nm[:, 0:1], scale=1.0,
                                         accum_out=s_sum[:, 0:1]), 1)

        @blk.gpsimd
        def _(g):
            g.wait_ge(ev.h, ev_t(NQT - 1, 3))
            sp.inc(g.dma_start(t["csc"].ap(), cmb[0:E, :]), 16)


def emit_p5(env):
    nc, t, s = env.nc, env.t, env.s
    lx, pe, ev, sp = (s[k] for k in ("lx", "pe", "ev", "sp"))

    moe = nc.alloc_sbuf_tensor("p5_moe", [128, DCH * 512], F32)
    sig_bc = nc.alloc_sbuf_tensor("p5_sigbc", [128, TOK], F32)

    with (
        nc.psum_tensor("p5_pb", [128, 1024], F32) as pb,
        nc.Block() as blk,
    ):
        pe_b, ev_b = pe.n, ev.n

        @blk.sync
        def _(sync):
            sync.wait_ge(sp.h, sp.n)

        @blk.tensor
        def _(tensor):
            for hw in range(2):
                ins = tensor.matmul(pb[:, hw * 512:(hw + 1) * 512], env.onesrow[:],
                                    env.sig[0:1, hw * 512:(hw + 1) * 512],
                                    start=True, stop=True)
            pe.inc(ins, 1)

        @blk.vector
        def _(vector):
            vector.wait_ge(pe.h, pe_b + 1)
            for hw in range(2):
                ev.inc(vector.tensor_copy(sig_bc[:, hw * 512:(hw + 1) * 512],
                                          pb[:, hw * 512:(hw + 1) * 512]), 1)

    for H in range(2):
        _p5_shared(env, moe, sig_bc, H)
        _p5_experts(env, moe, H)
        _p5_tail(env, moe, H)


def _p5_shared(env, moe, sig_bc, H):
    nc, t, s = env.nc, env.t, env.s
    h1h, h1l = env.h1h, env.h1l
    lw, pe, ev, sc = (s[k] for k in ("lw", "pe", "ev", "sc"))
    NSH = ISH // 128

    for q in range(2):
        off = H * 512 + q * 256
        with contextlib.ExitStack() as ostack:
            acth = ostack.enter_context(nc.sbuf_tensor(f"p5s_ah{H}{q}", [128, NSH * 256], BF16))
            actl = ostack.enter_context(nc.sbuf_tensor(f"p5s_al{H}{q}", [128, NSH * 256], BF16))
            stack = ostack.enter_context(contextlib.ExitStack())
            en = stack.enter_context
            gf = en(nc.sbuf_tensor(f"p5s_gf{H}{q}", [128, DCH * 128], F32))
            uf = en(nc.sbuf_tensor(f"p5s_uf{H}{q}", [128, DCH * 128], F32))
            gh0 = en(nc.sbuf_tensor(f"p5s_gh0{H}{q}", [128, DCH * 128], BF16))
            gh1 = en(nc.sbuf_tensor(f"p5s_gh1{H}{q}", [128, DCH * 128], BF16))
            gl0 = en(nc.sbuf_tensor(f"p5s_gl0{H}{q}", [128, DCH * 128], BF16))
            gl1 = en(nc.sbuf_tensor(f"p5s_gl1{H}{q}", [128, DCH * 128], BF16))
            uh0 = en(nc.sbuf_tensor(f"p5s_uh0{H}{q}", [128, DCH * 128], BF16))
            uh1 = en(nc.sbuf_tensor(f"p5s_uh1{H}{q}", [128, DCH * 128], BF16))
            ul0 = en(nc.sbuf_tensor(f"p5s_ul0{H}{q}", [128, DCH * 128], BF16))
            ul1 = en(nc.sbuf_tensor(f"p5s_ul1{H}{q}", [128, DCH * 128], BF16))
            tmp = en(nc.sbuf_tensor(f"p5s_tmp{H}{q}", [128, 512], F32))
            tm2 = en(nc.sbuf_tensor(f"p5s_tm2{H}{q}", [128, 512], F32))
            pg = en(nc.psum_tensor(f"p5s_pg{H}{q}", [128, 1024], F32))
            pu = en(nc.psum_tensor(f"p5s_pu{H}{q}", [128, 1024], F32))
            blk = en(nc.Block())
            ghb, glb, uhb, ulb = (gh0, gh1), (gl0, gl1), (uh0, uh1), (ul0, ul1)
            a3 = s["a3"]
            lw_b, pe_b, ev_b, sc_b, a3_b = lw.n, pe.n, ev.n, sc.n, a3.n
            order = [("sp", 0), ("sp", 1)]
            for i in range(1, NSH):
                if i + 1 < NSH:
                    order.append(("sp", i + 1))
                order.append(("ac", i - 1))
            order.append(("ac", NSH - 1))
            pos = {op: k + 1 for k, op in enumerate(order)}
            ev_sp = lambda i: ev_b + pos[("sp", i)]
            ev_ac = lambda i: ev_b + pos[("ac", i)]

            @blk.sync
            def _(sync):
                for i in range(NSH):
                    if i >= 1:
                        sync.wait_ge(ev.h, ev_sp(i - 1))
                    src = t["sgT"].ap()[:, i * 128:(i + 1) * 128] \
                        .rearrange("(c p) n -> p c n", p=128)
                    lw.inc(sync.dma_start(
                        gf[:].rearrange("p (c n) -> p c n", c=DCH), src), 16)

            @blk.gpsimd
            def _(g):
                for i in range(NSH):
                    if i >= 1:
                        g.wait_ge(ev.h, ev_sp(i - 1))
                    src = t["suT"].ap()[:, i * 128:(i + 1) * 128] \
                        .rearrange("(c p) n -> p c n", p=128)
                    a3.inc(g.dma_start(
                        uf[:].rearrange("p (c n) -> p c n", c=DCH), src), 16)

            @blk.vector
            def _(vector):
                def do_sp(i):
                    par = i % 2
                    vector.wait_ge(lw.h, lw_b + 16 * (i + 1))
                    vector.wait_ge(a3.h, a3_b + 16 * (i + 1))
                    if i >= 2:
                        vector.wait_ge(pe.h, pe_b + i - 1)
                    vector.tensor_copy(ghb[par][:], gf[:])
                    vector.tensor_copy(uhb[par][:], uf[:])
                    vector.tensor_sub(glb[par][:], gf[:], ghb[par][:])
                    ev.inc(vector.tensor_sub(ulb[par][:], uf[:], uhb[par][:]), 1)

                def do_ac(i):
                    par = i % 2
                    vector.wait_ge(sc.h, sc_b + i + 1)
                    vector.tensor_mul(tm2[:, par * 256: par * 256 + 256],
                                      tmp[:, par * 256: par * 256 + 256],
                                      sig_bc[:, off:off + 256])
                    vector.tensor_mul(tmp[:, par * 256: par * 256 + 256],
                                      tm2[:, par * 256: par * 256 + 256],
                                      pu[:, par * 512: par * 512 + 256])
                    vector.tensor_copy(acth[:, i * 256:(i + 1) * 256],
                                       tmp[:, par * 256: par * 256 + 256])
                    ev.inc(vector.tensor_sub(actl[:, i * 256:(i + 1) * 256],
                                             tmp[:, par * 256: par * 256 + 256],
                                             acth[:, i * 256:(i + 1) * 256]), 1)

                for kind, i in order:
                    (do_sp if kind == "sp" else do_ac)(i)

            @blk.tensor
            def _(tensor):
                for i in range(NSH):
                    tensor.wait_ge(ev.h, ev_sp(i))
                    if i >= 2:
                        tensor.wait_ge(sc.h, sc_b + i - 1)
                        tensor.wait_ge(ev.h, ev_ac(i - 1))
                    par = i % 2
                    for d in range(DCH):
                        rh = h1h[:, d * TOK + off: d * TOK + off + 256]
                        rl = h1l[:, d * TOK + off: d * TOK + off + 256]
                        mm3(tensor, pg[:, par * 512: par * 512 + 256],
                            ghb[par][:, d * 128:(d + 1) * 128],
                            glb[par][:, d * 128:(d + 1) * 128],
                            rh, rl, start=(d == 0), stop=(d == DCH - 1))
                        ins = mm3(tensor, pu[:, par * 512: par * 512 + 256],
                                  uhb[par][:, d * 128:(d + 1) * 128],
                                  ulb[par][:, d * 128:(d + 1) * 128],
                                  rh, rl, start=(d == 0), stop=(d == DCH - 1))
                    pe.inc(ins, 1)

            @blk.scalar
            def _(scalar):
                for i in range(NSH):
                    par = i % 2
                    scalar.wait_ge(pe.h, pe_b + i + 1)
                    if i >= 2:
                        scalar.wait_ge(ev.h, ev_ac(i - 2))
                    sc.inc(scalar.activation(tmp[:, par * 256: par * 256 + 256],
                                             pg[:, par * 512: par * 512 + 256],
                                             AF.Silu), 1)

            stack.close()
            _p5_shared_down(env, moe, acth, actl, H, q)


def _p5_shared_down(env, moe, acth, actl, H, q):
    nc, t, s = env.nc, env.t, env.s
    lw, pe, ev = (s[k] for k in ("lw", "pe", "ev"))
    NSH = ISH // 128
    NHH = NSH // 2
    with (
        nc.sbuf_tensor(f"p5d_sf{H}{q}", [128, NHH * 128], F32) as sf,
        nc.sbuf_tensor(f"p5d_h0{H}{q}", [128, NHH * 128], BF16) as sh0,
        nc.sbuf_tensor(f"p5d_h1{H}{q}", [128, NHH * 128], BF16) as sh1,
        nc.sbuf_tensor(f"p5d_l0{H}{q}", [128, NHH * 128], BF16) as sl0,
        nc.sbuf_tensor(f"p5d_l1{H}{q}", [128, NHH * 128], BF16) as sl1,
        nc.psum_tensor(f"p5d_pd{H}{q}", [128, 1024], F32) as pd,
        nc.Block() as blk,
    ):
        shb, slb = (sh0, sh1), (sl0, sl1)
        lw_b, pe_b, ev_b = lw.n, pe.n, ev.n
        NG = 2 * DCH
        order = [("sp", 0), ("sp", 1), ("sp", 2), ("sp", 3)]
        for f in range(DCH):
            if 2 * f + 4 < NG:
                order.append(("sp", 2 * f + 4))
            if 2 * f + 5 < NG:
                order.append(("sp", 2 * f + 5))
            order.append(("mo", f))
        pos = {op: k + 1 for k, op in enumerate(order)}
        ev_split = lambda g: ev_b + pos[("sp", g)]
        ev_moe = lambda f: ev_b + pos[("mo", f)]
        pe_g = lambda g: pe_b + g + 1

        @blk.sync
        def _(sync):
            for g in range(2 * DCH):
                f, hh = divmod(g, 2)
                if g >= 1:
                    sync.wait_ge(ev.h, ev_split(g - 1))
                src = t["sdT"].ap()[hh * (NHH * 128):(hh + 1) * (NHH * 128),
                                    f * 128:(f + 1) * 128] \
                    .rearrange("(c p) n -> p c n", p=128)
                lw.inc(sync.dma_start(
                    sf[:].rearrange("p (c n) -> p c n", c=NHH), src), 16)

        @blk.vector
        def _(vector):
            def do_sp(g):
                par = g % 2
                vector.wait_ge(lw.h, lw_b + 16 * (g + 1))
                if g >= 2:
                    vector.wait_ge(pe.h, pe_g(g - 2))
                vector.tensor_copy(shb[par][:], sf[:])
                ev.inc(vector.tensor_sub(slb[par][:], sf[:], shb[par][:]), 1)

            def do_mo(f):
                vector.wait_ge(pe.h, pe_g(2 * f + 1))
                ev.inc(vector.tensor_copy(
                    moe[:, f * 512 + q * 256: f * 512 + q * 256 + 256],
                    pd[:, (f % 2) * 512:(f % 2) * 512 + 256]), 1)

            for kind, idx in order:
                (do_sp if kind == "sp" else do_mo)(idx)

        @blk.tensor
        def _(tensor):
            for g in range(2 * DCH):
                f, hh = divmod(g, 2)
                par = g % 2
                tensor.wait_ge(ev.h, ev_split(g))
                if hh == 0 and f >= 2:
                    tensor.wait_ge(ev.h, ev_moe(f - 2))
                for i in range(NHH):
                    ins = mm3(tensor, pd[:, (f % 2) * 512:(f % 2) * 512 + 256],
                              shb[par][:, i * 128:(i + 1) * 128],
                              slb[par][:, i * 128:(i + 1) * 128],
                              acth[:, (hh * NHH + i) * 256:(hh * NHH + i + 1) * 256],
                              actl[:, (hh * NHH + i) * 256:(hh * NHH + i + 1) * 256],
                              start=(g % 2 == 0 and i == 0), stop=(g % 2 == 1 and i == NHH - 1))
                pe.inc(ins, 1)


def _p5_experts(env, moe, H):
    nc, t, s = env.nc, env.t, env.s
    h1h, h1l = env.h1h, env.h1l
    lw, lx, pe, ev, sc = (s[k] for k in ("lw", "lx", "pe", "ev", "sc"))
    NIM = IM // 128
    off = H * 512

    with contextlib.ExitStack() as stack:
        en = stack.enter_context
        heh = en(nc.sbuf_tensor(f"p5e_hh{H}", [128, NIM * 512], BF16))
        hel = en(nc.sbuf_tensor(f"p5e_hl{H}", [128, NIM * 512], BF16))
        gf = en(nc.sbuf_tensor(f"p5e_gf{H}", [128, DCH * 128], F32))
        uf = en(nc.sbuf_tensor(f"p5e_uf{H}", [128, DCH * 128], F32))
        gh0 = en(nc.sbuf_tensor(f"p5e_gh0{H}", [128, DCH * 128], BF16))
        gh1 = en(nc.sbuf_tensor(f"p5e_gh1{H}", [128, DCH * 128], BF16))
        gl0 = en(nc.sbuf_tensor(f"p5e_gl0{H}", [128, DCH * 128], BF16))
        gl1 = en(nc.sbuf_tensor(f"p5e_gl1{H}", [128, DCH * 128], BF16))
        uh0 = en(nc.sbuf_tensor(f"p5e_uh0{H}", [128, DCH * 128], BF16))
        uh1 = en(nc.sbuf_tensor(f"p5e_uh1{H}", [128, DCH * 128], BF16))
        ul0 = en(nc.sbuf_tensor(f"p5e_ul0{H}", [128, DCH * 128], BF16))
        ul1 = en(nc.sbuf_tensor(f"p5e_ul1{H}", [128, DCH * 128], BF16))
        df = en(nc.sbuf_tensor(f"p5e_df{H}", [128, NIM * 128], F32))
        dh0 = en(nc.sbuf_tensor(f"p5e_dh0{H}", [128, NIM * 128], BF16))
        dh1 = en(nc.sbuf_tensor(f"p5e_dh1{H}", [128, NIM * 128], BF16))
        dl0 = en(nc.sbuf_tensor(f"p5e_dl0{H}", [128, NIM * 128], BF16))
        dl1 = en(nc.sbuf_tensor(f"p5e_dl1{H}", [128, NIM * 128], BF16))
        bc = en(nc.sbuf_tensor(f"p5e_bc{H}", [128, 512], F32))
        cmbr = en(nc.sbuf_tensor(f"p5e_cr{H}", [1, 2 * 512], F32))
        tmp = en(nc.sbuf_tensor(f"p5e_tmp{H}", [128, 1024], F32))
        tm2 = en(nc.sbuf_tensor(f"p5e_tm2{H}", [128, 1024], F32))
        pg = en(nc.psum_tensor(f"p5e_pg{H}", [128, 1024], F32))
        pu = en(nc.psum_tensor(f"p5e_pu{H}", [128, 1024], F32))
        pd = en(nc.psum_tensor(f"p5e_pd{H}", [128, 1024], F32))
        pb = en(nc.psum_tensor(f"p5e_pb{H}", [128, 512], F32))
        blk = en(nc.Block())

        ghb, glb, uhb, ulb = (gh0, gh1), (gl0, gl1), (uh0, uh1), (ul0, ul1)
        dhb, dlb = (dh0, dh1), (dl0, dl1)
        a3 = s["a3"]
        lw_b, pe_b, ev_b, sc_b, lx_bE, a3_b = lw.n, pe.n, ev.n, sc.n, lx.n, a3.n
        PE_E = 1 + NIM + DCH
        SC_E = NIM
        pe_bc = lambda e: pe_b + PE_E * e + 1
        pe_gu = lambda e, i: pe_b + PE_E * e + 1 + i + 1
        pe_dn = lambda e, f: pe_b + PE_E * e + 1 + NIM + f + 1
        sc_si = lambda e, i: sc_b + SC_E * e + i + 1
        order = []
        for e in range(E):
            order.append(("bc", e, 0))
            order.append(("gs", e, 0))
            for i in range(1, NIM):
                order.append(("gs", e, i))
                order.append(("he", e, i - 1))
            order.append(("he", e, NIM - 1))
            order.append(("ds", e, 0))
            for f in range(1, DCH):
                order.append(("ds", e, f))
                order.append(("mo", e, f - 1))
            order.append(("mo", e, DCH - 1))
        pos = {op: k + 1 for k, op in enumerate(order)}
        ev_bc = lambda e: ev_b + pos[("bc", e, 0)]
        ev_gs = lambda e, i: ev_b + pos[("gs", e, i)]
        ev_he = lambda e, i: ev_b + pos[("he", e, i)]
        ev_ds = lambda e, f: ev_b + pos[("ds", e, f)]
        ev_mo = lambda e, f: ev_b + pos[("mo", e, f)]

        @blk.sync
        def _(sync):
            for e in range(E):
                if e >= 2:
                    sync.wait_ge(pe.h, pe_bc(e - 1))
                lx.inc(sync.dma_start(
                    cmbr[0:1, (e % 2) * 512:(e % 2) * 512 + 512],
                    t["csc"].ap()[e:e + 1, off:off + 512]), 16)
                for i in range(NIM):
                    gi = e * NIM + i
                    if gi >= 1:
                        pj = gi - 1
                        sync.wait_ge(ev.h, ev_gs(pj // NIM, pj % NIM))
                    src = t["guT"].ap()[e, :, i * 128:(i + 1) * 128] \
                        .rearrange("(c p) n -> p c n", p=128)
                    lw.inc(sync.dma_start(
                        gf[:].rearrange("p (c n) -> p c n", c=DCH), src), 16)
                for f in range(DCH):
                    gi = e * DCH + f
                    if gi >= 1:
                        pj = gi - 1
                        sync.wait_ge(ev.h, ev_ds(pj // DCH, pj % DCH))
                    src = t["dpT"].ap()[e, :, f * 128:(f + 1) * 128] \
                        .rearrange("(c p) n -> p c n", p=128)
                    lw.inc(sync.dma_start(
                        df[:].rearrange("p (c n) -> p c n", c=NIM), src), 16)

        @blk.gpsimd
        def _(g):
            for e in range(E):
                for i in range(NIM):
                    gi = e * NIM + i
                    if gi >= 1:
                        pj = gi - 1
                        g.wait_ge(ev.h, ev_gs(pj // NIM, pj % NIM))
                    src = t["guT"].ap()[e, :, IM + i * 128: IM + (i + 1) * 128] \
                        .rearrange("(c p) n -> p c n", p=128)
                    a3.inc(g.dma_start(
                        uf[:].rearrange("p (c n) -> p c n", c=DCH), src), 16)

        @blk.tensor
        def _(tensor):
            for e in range(E):
                if e >= 1:
                    tensor.wait_ge(ev.h, ev_he(e - 1, NIM - 1))
                tensor.wait_ge(lx.h, lx_bE + 16 * (e + 1))
                ins = tensor.matmul(pb[:, 0:512], env.onesrow[:],
                                    cmbr[0:1, (e % 2) * 512:(e % 2) * 512 + 512],
                                    start=True, stop=True)
                pe.inc(ins, 1)
                for i in range(NIM):
                    gi = e * NIM + i
                    tensor.wait_ge(ev.h, ev_gs(e, i))
                    if gi >= 2:
                        pj = gi - 2
                        tensor.wait_ge(sc.h, sc_si(pj // NIM, pj % NIM))
                        tensor.wait_ge(ev.h, ev_he(pj // NIM, pj % NIM))
                    par = i % 2
                    for d in range(DCH):
                        rh = h1h[:, d * TOK + off: d * TOK + off + 512]
                        rl = h1l[:, d * TOK + off: d * TOK + off + 512]
                        mm3(tensor, pg[:, par * 512: par * 512 + 512],
                            ghb[par][:, d * 128:(d + 1) * 128],
                            glb[par][:, d * 128:(d + 1) * 128],
                            rh, rl, start=(d == 0), stop=(d == DCH - 1))
                        ins = mm3(tensor, pu[:, par * 512: par * 512 + 512],
                                  uhb[par][:, d * 128:(d + 1) * 128],
                                  ulb[par][:, d * 128:(d + 1) * 128],
                                  rh, rl, start=(d == 0), stop=(d == DCH - 1))
                    pe.inc(ins, 1)
                for f in range(DCH):
                    gi = e * DCH + f
                    tensor.wait_ge(ev.h, ev_ds(e, f))
                    if gi >= 2:
                        pj = gi - 2
                        tensor.wait_ge(ev.h, ev_mo(pj // DCH, pj % DCH))
                    par = f % 2
                    for i in range(NIM):
                        ins = mm3(tensor, pd[:, par * 512: par * 512 + 512],
                                  dhb[par][:, i * 128:(i + 1) * 128],
                                  dlb[par][:, i * 128:(i + 1) * 128],
                                  heh[:, i * 512:(i + 1) * 512],
                                  hel[:, i * 512:(i + 1) * 512],
                                  start=(i == 0), stop=(i == NIM - 1))
                    pe.inc(ins, 1)

        @blk.scalar
        def _(scalar):
            for e in range(E):
                for i in range(NIM):
                    par = i % 2
                    gi = e * NIM + i
                    scalar.wait_ge(pe.h, pe_gu(e, i))
                    if gi >= 2:
                        pj = gi - 2
                        scalar.wait_ge(ev.h, ev_he(pj // NIM, pj % NIM))
                    sc.inc(scalar.activation(tmp[:, par * 512: par * 512 + 512],
                                             pg[:, par * 512: par * 512 + 512],
                                             AF.Silu), 1)

        @blk.vector
        def _(vector):
            def do_bc(e, _):
                vector.wait_ge(pe.h, pe_bc(e))
                ev.inc(vector.tensor_copy(bc[:], pb[:, 0:512]), 1)

            def do_gs(e, i):
                par = i % 2
                vector.wait_ge(lw.h, lw_b + (16 * NIM + 16 * DCH) * e + 16 * (i + 1))
                vector.wait_ge(a3.h, a3_b + 16 * NIM * e + 16 * (i + 1))
                if e * NIM + i >= 2:
                    pj = e * NIM + i - 2
                    vector.wait_ge(pe.h, pe_gu(pj // NIM, pj % NIM))
                vector.tensor_copy(ghb[par][:], gf[:])
                vector.tensor_copy(uhb[par][:], uf[:])
                vector.tensor_sub(glb[par][:], gf[:], ghb[par][:])
                ev.inc(vector.tensor_sub(ulb[par][:], uf[:], uhb[par][:]), 1)

            def do_he(e, i):
                par = i % 2
                vector.wait_ge(sc.h, sc_si(e, i))
                vector.tensor_mul(tm2[:, par * 512: par * 512 + 512],
                                  tmp[:, par * 512: par * 512 + 512],
                                  bc[:])
                vector.tensor_mul(tmp[:, par * 512: par * 512 + 512],
                                  tm2[:, par * 512: par * 512 + 512],
                                  pu[:, par * 512: par * 512 + 512])
                vector.tensor_copy(heh[:, i * 512:(i + 1) * 512],
                                   tmp[:, par * 512: par * 512 + 512])
                ev.inc(vector.tensor_sub(hel[:, i * 512:(i + 1) * 512],
                                         tmp[:, par * 512: par * 512 + 512],
                                         heh[:, i * 512:(i + 1) * 512]), 1)

            def do_ds(e, f):
                par = f % 2
                vector.wait_ge(lw.h, lw_b + (16 * NIM + 16 * DCH) * e + 16 * NIM + 16 * (f + 1))
                if e * DCH + f >= 2:
                    pj = e * DCH + f - 2
                    vector.wait_ge(pe.h, pe_dn(pj // DCH, pj % DCH))
                vector.tensor_copy(dhb[par][:], df[:])
                ev.inc(vector.tensor_sub(dlb[par][:], df[:], dhb[par][:]), 1)

            def do_mo(e, f):
                par = f % 2
                vector.wait_ge(pe.h, pe_dn(e, f))
                ev.inc(vector.tensor_add(moe[:, f * 512:(f + 1) * 512],
                                         moe[:, f * 512:(f + 1) * 512],
                                         pd[:, par * 512: par * 512 + 512]), 1)

            fns = {"bc": do_bc, "gs": do_gs, "he": do_he, "ds": do_ds, "mo": do_mo}
            for kind, e, idx in order:
                fns[kind](e, idx)


def _p5_tail(env, moe, H):
    nc, t, s = env.nc, env.t, env.s
    h1h, h1l = env.h1h, env.h1l
    pe, ev, st = s["pe"], s["ev"], s["st"]

    with nc.Block() as blk:
        @blk.vector
        def _(vector):
            for f in range(DCH):
                vector.tensor_add(moe[:, f * 512:(f + 1) * 512],
                                  moe[:, f * 512:(f + 1) * 512],
                                  h1h[:, f * TOK + H * 512: f * TOK + H * 512 + 512])
                ins = vector.tensor_add(moe[:, f * 512:(f + 1) * 512],
                                        moe[:, f * 512:(f + 1) * 512],
                                        h1l[:, f * TOK + H * 512: f * TOK + H * 512 + 512])
            ev.inc(ins, 1)

    emit_ln(env, moe, 512, 0, 512, f"t{H}")

    with nc.Block() as blk:
        @blk.sync
        def _(sync):
            dst = t["y"].ap()[:, H * 512:(H + 1) * 512].rearrange("(c p) n -> p c n", p=128)
            st.inc(sync.dma_start(dst, moe[:].rearrange("p (c n) -> p c n", c=DCH)), 16)


def kernel(**inputs):
    x = np.asarray(inputs["L0_attn_x"], dtype=np.float32)
    wq = np.asarray(inputs["L0_attn_Wq"], dtype=np.float32)
    wk = np.asarray(inputs["L0_attn_Wk"], dtype=np.float32)
    wv = np.asarray(inputs["L0_attn_Wv"], dtype=np.float32)
    wp = np.asarray(inputs["L0_attn_Wp"], dtype=np.float32)
    rw = np.asarray(inputs["L0_ffn_router_weight"], dtype=np.float32)
    gu = np.asarray(inputs["L0_ffn_gate_up_proj"], dtype=np.float32)
    dp = np.asarray(inputs["L0_ffn_down_proj"], dtype=np.float32)
    sg = np.asarray(inputs["L0_ffn_sh_gate_w"], dtype=np.float32)
    su = np.asarray(inputs["L0_ffn_sh_up_w"], dtype=np.float32)
    sd = np.asarray(inputs["L0_ffn_sh_down_w"], dtype=np.float32)
    sgw = np.asarray(inputs["L0_ffn_shared_gate_w"], dtype=np.float32)

    xT = [np.ascontiguousarray(x[b].T) for b in range(B)]
    common = {
        "wqT": np.ascontiguousarray(wq.T), "wkT": np.ascontiguousarray(wk.T),
        "wvT": np.ascontiguousarray(wv.T), "wpT": np.ascontiguousarray(wp.T),
        "rwT": np.ascontiguousarray(rw.T),
        "guT": np.ascontiguousarray(gu.transpose(0, 2, 1)),
        "dpT": np.ascontiguousarray(dp.transpose(0, 2, 1)),
        "sgT": np.ascontiguousarray(sg.T), "suT": np.ascontiguousarray(su.T),
        "sdT": np.ascontiguousarray(sd.T),
        "sgwT": np.ascontiguousarray(sgw.reshape(DCH, 128).T),
    }
    in_maps = []
    for c in range(8):
        b, k = c // 4, c % 4
        m = dict(common)
        m["xT"] = xT[b]
        m["xTo"] = np.ascontiguousarray(xT[b][:, k * TOK:(k + 1) * TOK])
        m["qoff"] = np.full((128, 1), k * TOK, dtype=np.float32)
        in_maps.append(m)

    nc = build(dbg=_OVR.get("dbg", ()), phases=_OVR.get("phases", ("p1", "p2", "p3", "p4", "p5")))
    res = run_bass_kernel_spmd(nc, in_maps, core_ids=list(range(8)), **_OVR.get("runkw", {}))
    if "post" in _OVR:
        return _OVR["post"](res)
    out = np.empty((B, S, D), dtype=np.float32)
    for c in range(8):
        b, k = c // 4, c % 4
        out[b, k * TOK:(k + 1) * TOK, :] = res.results[c]["y"].T
    return out
